# version 05
# baseline (speedup 1.0000x reference)
import numpy as np
import ml_dtypes

import concourse.bass as bass
import concourse.mybir as mybir
from concourse.bass_utils import run_bass_kernel_spmd

F32 = mybir.dt.float32
BF16 = mybir.dt.bfloat16
AF = mybir.ActivationFunctionType
AL = mybir.AluOpType
AX = mybir.AxisListType

B, S, D = 2, 4096, 4096
HKV, DH, KV, G = 8, 128, 1024, 4
TOK = 1024
NT = TOK // 128
NDC = D // 128
NKC = S // 128
NOB = 8
RK, E = 32, 8
INV_SQRT_DH = 1.0 / float(np.sqrt(DH))
LN_EPS = 1e-5
SC_EPS = 1e-6
NGROUPS = [[0, 1, 2, 3], [4, 5, 6, 7]]
ALLG = [[0, 1, 2, 3, 4, 5, 6, 7]]


class Led:

    def __init__(self, nc, name, dma=False):
        self.h = nc.alloc_semaphore(name)
        self.n = 0
        self.step = 16 if dma else 1

    def inc(self, instr, k=None):
        k = self.step if k is None else k
        instr.then_inc(self.h, k)
        self.n += k
        return self.n

    def wait(self, eng, at=None):
        eng.wait_ge(self.h, self.n if at is None else at)



class VGuard:

    def __init__(self, vP):
        self.vP = vP
        self.w = {}
        self.guarded = 0

    def bind(self, v):
        self.v = v
        return self

    def _name(self, ap):
        try:
            return ap.tensor.name
        except AttributeError:
            return None

    def _pre(self, ins_aps, out_aps):
        need = 0
        for ap in list(ins_aps) + list(out_aps):
            n = self._name(ap)
            if n is not None and n in self.w:
                need = max(need, self.w[n])
        if need > self.guarded:
            self.v.wait_ge(self.vP.h, need)
            self.guarded = need

    def _post(self, ins, out_aps):
        self.vP.inc(ins)
        for ap in out_aps:
            n = self._name(ap)
            if n is not None:
                self.w[n] = self.vP.n
        return ins

    def wait_ge(self, sem, val):
        return self.v.wait_ge(sem, val)

    def memset(self, out, c):
        self._pre([], [out])
        return self._post(self.v.memset(out, c), [out])

    def tensor_copy(self, out, in_):
        self._pre([in_], [out])
        return self._post(self.v.tensor_copy(out, in_), [out])

    def tensor_add(self, out, a, b):
        self._pre([a, b], [out])
        return self._post(self.v.tensor_add(out, a, b), [out])

    def tensor_sub(self, out, a, b):
        self._pre([a, b], [out])
        return self._post(self.v.tensor_sub(out, a, b), [out])

    def tensor_mul(self, out, a, b):
        self._pre([a, b], [out])
        return self._post(self.v.tensor_mul(out, a, b), [out])

    def tensor_max(self, out, a, b):
        self._pre([a, b], [out])
        return self._post(self.v.tensor_max(out, a, b), [out])

    def tensor_scalar(self, out, in_, s1, s2, op0, op1=None, **kw):
        aps = [in_] + [s for s in (s1, s2) if hasattr(s, "tensor")]
        self._pre(aps, [out])
        if op1 is None:
            ins = self.v.tensor_scalar(out, in_, s1, s2, op0, **kw)
        else:
            ins = self.v.tensor_scalar(out, in_, s1, s2, op0, op1, **kw)
        return self._post(ins, [out])

    def tensor_scalar_max(self, out, in_, s1):
        aps = [in_] + ([s1] if hasattr(s1, "tensor") else [])
        self._pre(aps, [out])
        return self._post(self.v.tensor_scalar_max(out, in_, s1), [out])

    def tensor_reduce(self, out, in_, ax, op, **kw):
        self._pre([in_], [out])
        return self._post(self.v.tensor_reduce(out, in_, ax, op, **kw), [out])

    def reciprocal(self, out, in_):
        self._pre([in_], [out])
        return self._post(self.v.reciprocal(out, in_), [out])

    def max(self, out, in_):
        self._pre([in_], [out])
        return self._post(self.v.max(out=out, in_=in_), [out])

    def scalar_tensor_tensor(self, out, in0, sc, in1, op0, op1, **kw):
        aps = [in0, in1] + ([sc] if hasattr(sc, "tensor") else [])
        self._pre(aps, [out])
        return self._post(
            self.v.scalar_tensor_tensor(out, in0, sc, in1, op0, op1, **kw),
            [out])


_LAST_T = [None]


def _tm(t, *a, **k):
    ins = t.matmul(*a, **k)
    _LAST_T[0] = ins
    return ins


def _tt(t, *a, **k):
    ins = t.transpose(*a, **k)
    _LAST_T[0] = ins
    return ins


def tmark(t, tP):
    return tP.inc(_LAST_T[0])

def build_program(taps=False):
    nc = bass.Bass("TRN2", target_bir_lowering=False)

    xfm_d = nc.dram_tensor("xfm", [D, TOK], F32, kind="ExternalInput").ap()
    xtm_d = nc.dram_tensor("xtm", [TOK, D], F32, kind="ExternalInput").ap()
    wqg_d = nc.dram_tensor("wqg", [HKV, G, D, DH], F32, kind="ExternalInput").ap()
    wkf_d = nc.dram_tensor("wkf", [HKV, D, DH], F32, kind="ExternalInput").ap()
    wvf_d = nc.dram_tensor("wvf", [HKV, D, DH], F32, kind="ExternalInput").ap()
    wo_d = nc.dram_tensor("wo", [KV, D], F32, kind="ExternalInput").ap()
    wt_d = nc.dram_tensor("wt", [NOB, D, 512], F32, kind="ExternalInput").ap()
    w8_d = nc.dram_tensor("w8", [D, E], F32, kind="ExternalInput").ap()
    b_d = nc.dram_tensor("bvec", [1, D], F32, kind="ExternalInput").ap()
    a_d = nc.dram_tensor("amat", [D, RK], F32, kind="ExternalInput").ap()
    bm_d = nc.dram_tensor("bmat", [RK, D], F32, kind="ExternalInput").ap()
    limt_d = nc.dram_tensor("limt", [RK, E], F32, kind="ExternalInput").ap()
    lim_d = nc.dram_tensor("lime", [E, RK], F32, kind="ExternalInput").ap()
    id_d = nc.dram_tensor("identbf", [128, 128], BF16, kind="ExternalInput").ap()
    y_d = nc.dram_tensor("y", [TOK, D], F32, kind="ExternalOutput").ap()

    kown_d = nc.dram_tensor("kown", [HKV, DH, TOK], BF16)
    kall_d = nc.dram_tensor("kall", [4 * HKV, DH, TOK], BF16)
    vown_d = nc.dram_tensor("vown", [HKV, TOK, DH], BF16)
    vall_d = nc.dram_tensor("vall", [4 * HKV, TOK, DH], BF16)
    scpr_d = nc.dram_tensor("scpr", [128, 2], F32)
    scsrc_d = nc.dram_tensor("scsrc", [8], F32)
    scdst_d = nc.dram_tensor("scdst", [8], F32)
    attn_d = nc.dram_tensor("attnout", [TOK, 7 * 512], F32)
    x1sp_d = nc.dram_tensor("x1sp", [TOK, D], F32)
    x1bf_d = nc.dram_tensor("x1bf", [NDC, TOK, 128], BF16)
    hdb_d = nc.dram_tensor("hdb", [TOK, 7 * 512], F32)

    SB_END = 229376
    base = SB_END - nc.sbuf_bytes_remaining
    cur = [base]

    def alloc_at(name, shape, dtype, offset):
        o = (offset + 63) // 64 * 64
        h = nc.alloc_sbuf_tensor_at(name, shape, dtype, offset=o)
        per = int(np.prod(shape[1:])) * mybir.dt.size(dtype)
        return h, o + per

    def perm(name, shape, dtype):
        h, end = alloc_at(name, shape, dtype, cur[0])
        cur[0] = end
        return h

    class Region:
        def __init__(self, size):
            self.start = (cur[0] + 63) // 64 * 64
            self.size = size
            cur[0] = self.start + size

        def seq(self, *specs):
            out = []
            pos = self.start
            for name, shape, dtype in specs:
                h, pos = alloc_at(name, shape, dtype, pos)
                out.append(h)
            assert pos <= self.start + self.size, \
                f"region overflow {pos - self.start} > {self.size}"
            return out

    ident = perm("ident", [128, 128], BF16)
    ones128 = perm("ones128", [128, 128], BF16)
    csrc = perm("csrc", [128, 2], BF16)
    scsrc = perm("scsrc", [128, 2], BF16)
    cbias = perm("cbias", [128, 1], F32)
    nsc = perm("nsc", [128, 16], F32)
    nsc2 = perm("nsc2", [128, 16], F32)
    lnm = perm("lnm", [128, 4], F32)
    lnsum = perm("lnsum", [128, 2], F32)
    recips = perm("recips", [128, 2], F32)
    mx2 = perm("mx2", [2, 128], F32)
    sc2 = perm("sc2", [1, 2], F32)

    r1 = Region(16384)
    (q_bf,) = r1.seq(("q_bf", [128, HKV, TOK], BF16))
    z_s, z_bf, pmix_tok, pmix_s, h8_s, ds_s, gt1, gt2, gt3, wgt, wem = r1.seq(
        ("z_s", [RK, TOK], F32), ("z_bf", [RK, TOK], BF16),
        ("pmix_tok", [RK, TOK], F32), ("pmix_s", [RK, 512], F32),
        ("h8_s", [128, 4, E], F32), ("ds_s", [128, 4, E], F32),
        ("gt1", [128, E], F32), ("gt2", [128, E], F32), ("gt3", [128, E], F32),
        ("wgt", [128, 4, RK], BF16), ("wem", [RK, 512], BF16))
    (f_x1,) = r1.seq(("f_x1", [128, D], F32))

    r2 = Region(16384)
    (val_bf,) = r2.seq(("val_bf", [128, HKV, TOK], BF16))
    (f_hdb,) = r2.seq(("f_hdb", [128, D], F32))

    r3 = Region(16384)
    (k_bf,) = r3.seq(("k_bf", [128, HKV, TOK], BF16))
    ksq, qsq, recipb = r3.seq(
        ("ksq", [128, S], BF16), ("qsq", [128, TOK], BF16),
        ("recipb", [128, TOK], F32))
    (df_x1bf,) = r3.seq(("df_x1bf", [128, D], BF16))
    (hd7,) = r3.seq(("hd7", [128, NT, 512], F32))

    r4 = Region(16384)
    (v_fm,) = r4.seq(("v_fm", [128, HKV, TOK], BF16))
    (df_acc,) = r4.seq(("df_acc", [128, D], F32))
    r5 = Region(16384)
    wkv0, wkv1 = r5.seq(("wkv0", [128, NDC, DH], BF16),
                        ("wkv1", [128, NDC, DH], BF16))
    (df_xtm,) = r5.seq(("df_xtm", [128, D], F32))
    pt0, pt1 = r5.seq(("pt0", [128, TOK], BF16), ("pt1", [128, TOK], BF16))
    wpong, _ = alloc_at("wpong", [128, NDC, 512], BF16, r4.start)

    r6 = Region(65536)
    (xfm_bf,) = r6.seq(("xfm_bf", [128, NDC, TOK], BF16))
    kh0, kh1, vh0, vh1, att7, qn1 = r6.seq(
        ("kh0", [128, S], BF16), ("kh1", [128, S], BF16),
        ("vh0", [128, NKC, DH], BF16), ("vh1", [128, NKC, DH], BF16),
        ("att7", [128, NT, 512], F32), ("qn1", [1, TOK], F32))
    (x1_fm,) = r6.seq(("x1_fm", [128, NDC, TOK], BF16))

    r7 = Region(60416)
    wqs0, wqs1, wqs2, eff_bf, vtt = r7.seq(
        ("wqs0", [128, D], F32), ("wqs1", [128, D], F32),
        ("wqs2", [128, D], F32), ("eff_bf", [128, NDC, DH], BF16),
        ("vtt", [128, HKV, DH], BF16))
    wo0, wo1, strip0, strip1, strip2, strip3, wping = r7.seq(
        ("wo0", [128, HKV, 512], BF16), ("wo1", [128, HKV, 512], BF16),
        ("strip0", [128, 512], F32), ("strip1", [128, 512], F32),
        ("strip2", [128, 512], F32), ("strip3", [128, 512], F32),
        ("wping", [128, NDC, 512], BF16))
    a_bf, zp_bf, w8_bf, limt_bf, lim_bf, b8_bf, bmob0, bmob1, bob0, bob1 = \
        r7.seq(("a_bf", [128, NDC, RK], BF16), ("zp_bf", [RK, TOK], BF16),
               ("w8_bf", [128, NDC, E], BF16), ("limt_bf", [RK, E], BF16),
               ("lim_bf", [RK, RK], BF16), ("b8_bf", [128, E], BF16),
               ("bmob0", [RK, 512], BF16), ("bmob1", [RK, 512], BF16),
               ("bob0", [128, 512], BF16), ("bob1", [128, 512], BF16))

    assert cur[0] <= SB_END, f"SBUF over budget: {cur[0] - base} used"

    pa = nc.alloc_psum_tensor("pa", [128, 1024], F32)
    pb = nc.alloc_psum_tensor("pb", [128, 1024], F32)
    pv = nc.alloc_psum_tensor("pv", [128, 1024], F32)
    pc = nc.alloc_psum_tensor("pc", [128, 512], F32)
    pd = nc.alloc_psum_tensor("pd", [128, 512], F32)
    ptb = pc.ap().bitcast(BF16)

    slots = [pa.ap()[:, 0:512], pa.ap()[:, 512:1024],
             pb.ap()[:, 0:512], pb.ap()[:, 512:1024]]
    eslots = slots + [pv.ap()[:, 0:512], pv.ap()[:, 512:1024]]

    dG = Led(nc, "dG", dma=True)
    dS = Led(nc, "dS", dma=True)
    dA = Led(nc, "dA", dma=True)
    dQ = Led(nc, "dQ", dma=True)
    dW = [Led(nc, "dW0", dma=True), Led(nc, "dW1", dma=True)]
    dKK = [Led(nc, "dKK0", dma=True), Led(nc, "dKK1", dma=True)]
    dKV = [Led(nc, "dKV0", dma=True), Led(nc, "dKV1", dma=True)]
    dWO = [Led(nc, "dWO0", dma=True), Led(nc, "dWO1", dma=True)]
    dWP = [Led(nc, "dWP0", dma=True), Led(nc, "dWP1", dma=True)]
    cc = Led(nc, "cc")
    tP = Led(nc, "tP")
    vP = Led(nc, "vP")
    sP = Led(nc, "sP")
    vg = VGuard(vP)

    with nc.Block() as blk:
        @blk.vector
        def _(v):
            v = vg.bind(v)
            v.memset(ones128.ap(), 1.0)
            v.memset(csrc.ap(), 0.0)
            v.memset(scsrc.ap(), 0.0)
            v.memset(nsc2.ap()[0:8, 6:7], 0.0)

        @blk.sync
        def _(sy):
            dS.inc(sy.dma_start(out=ident.ap(), in_=id_d))
        ident_dma = dS.n

        @blk.scalar
        def _(s):
            s.preload_act_table(AF.Exp)

        @blk.gpsimd
        def _(g):
            src = xfm_d.rearrange("(c p) t -> p c t", p=128)
            for i in range(4):
                dG.inc(g.dma_start(out=xfm_bf.ap()[:, 8 * i:8 * (i + 1), :],
                                   in_=src[:, 8 * i:8 * (i + 1), :]))
        xfm_done = dG.n

        slot_free = [0, 0, 0, 0]
        slot_seq = [0]

        def next_slot():
            i = slot_seq[0] % 4
            slot_seq[0] += 1
            return i

        stage_consumed = [0, 0]
        kevac = []
        vevac = []

        def proj_pass(which, f, src_d, out_sb):
            par = f % 2
            stg = wkv0 if par == 0 else wkv1

            @blk.gpsimd
            def _(g):
                if stage_consumed[par]:
                    tP.wait(g, at=stage_consumed[par])
                dW[par].inc(g.dma_start(
                    out=stg.ap()[:, :, :],
                    in_=src_d[f].rearrange("(dc p) c -> p dc c", p=128)))

            @blk.tensor
            def _(t):
                dW[par].wait(t)
                dG.wait(t, at=xfm_done)
                for tb in range(2):
                    si = next_slot()
                    if slot_free[si]:
                        vP.wait(t, at=slot_free[si])
                    for dc in range(NDC):
                        _tm(t, slots[si], stg.ap()[:, dc, :],
                                 xfm_bf.ap()[:, dc, 512 * tb:512 * (tb + 1)],
                                 start=(dc == 0), stop=(dc == NDC - 1))
                    tmark(t, tP)
            stage_consumed[par] = tP.n

            @blk.vector
            def _(v):
                v = vg.bind(v)
                for tb in range(2):
                    si = (slot_seq[0] - 2 + tb) % 4
                    tP.wait(v, at=tP.n - (1 - tb))
                    ins = v.tensor_copy(
                        out_sb.ap()[:, f, 512 * tb:512 * (tb + 1)], slots[si])
                    slot_free[si] = vP.n

        for f in range(HKV):
            proj_pass("k", f, wkf_d, k_bf)
            kevac.append(vP.n)

            @blk.sync
            def _(sy, f=f):
                vP.wait(sy, at=kevac[f])
                dS.inc(sy.dma_start(out=kown_d[f], in_=k_bf.ap()[:, f, :]))
        kown_done = dS.n

        @blk.gpsimd
        def _(g):
            dS.wait(g, at=kown_done)
            cc.inc(g.collective_compute(
                "AllGather", AL.bypass, replica_groups=NGROUPS,
                ins=[kown_d.ap().opt()], outs=[kall_d.ap().opt()]))
        cc_k = cc.n

        for f in range(HKV):
            proj_pass("v", f, wvf_d, v_fm)
            vevac.append(vP.n)

        vtt_ev = []
        vtt_store = []
        for tt in range(NT):
            @blk.tensor
            def _(t, tt=tt):
                if tt == 0:
                    vP.wait(t, at=vevac[-1])
                    dS.wait(t, at=kown_done)
                else:
                    vP.wait(t, at=vtt_ev[tt - 1])
                for h in range(HKV):
                    _tt(t, ptb[:, 128 * h:128 * (h + 1)],
                                v_fm.ap()[:, h, 128 * tt:128 * (tt + 1)],
                                ident.ap())
                tmark(t, tP)

            @blk.vector
            def _(v, tt=tt):
                v = vg.bind(v)
                tP.wait(v)
                if tt >= 1:
                    dS.wait(v, at=vtt_store[tt - 1])
                v.tensor_copy(
                    vtt.ap().rearrange("p h d -> p (h d)"), ptb[:, :])
                vtt_ev.append(vP.n)

            @blk.sync
            def _(sy, tt=tt):
                vP.wait(sy, at=vtt_ev[tt])
                vtt_store.append(dS.inc(sy.dma_start(
                    out=vown_d.ap()[:, 128 * tt:128 * (tt + 1), :]
                    .transpose([1, 0, 2]),
                    in_=vtt.ap()[:, :, :])))

        @blk.gpsimd
        def _(g):
            dS.wait(g, at=vtt_store[-1])
            cc.inc(g.collective_compute(
                "AllGather", AL.bypass, replica_groups=NGROUPS,
                ins=[vown_d.ap().opt()], outs=[vall_d.ap().opt()]))
        cc_v = cc.n

        qevac = []
        cast_done = [0]
        for kvh in range(HKV):
            @blk.gpsimd
            def _(g, kvh=kvh):
                if kvh > 0:
                    sP.wait(g, at=cast_done[0])
                for gi in range(3):
                    dQ.inc(g.dma_start(
                        out=[wqs0, wqs1, wqs2][gi].ap().rearrange(
                            "p (dc c) -> p dc c", c=128),
                        in_=wqg_d[kvh, gi].rearrange(
                            "(dc p) c -> p dc c", p=128)))
            w012 = dQ.n

            @blk.vector
            def _(v, kvh=kvh):
                v = vg.bind(v)
                dQ.wait(v, at=w012)
                v.tensor_add(wqs0.ap(), wqs0.ap(), wqs1.ap())
                v.tensor_add(wqs0.ap(), wqs0.ap(), wqs2.ap())
            add2 = vP.n

            @blk.gpsimd
            def _(g, kvh=kvh):
                vP.wait(g, at=add2)
                dQ.inc(g.dma_start(
                    out=wqs1.ap().rearrange("p (dc c) -> p dc c", c=128),
                    in_=wqg_d[kvh, 3].rearrange("(dc p) c -> p dc c", p=128)))
            w3 = dQ.n

            @blk.vector
            def _(v, kvh=kvh):
                v = vg.bind(v)
                dQ.wait(v, at=w3)
                v.tensor_add(wqs0.ap(), wqs0.ap(), wqs1.ap())
            add3 = vP.n

            @blk.scalar
            def _(s, kvh=kvh):
                vP.wait(s, at=add3)
                if kvh > 0:
                    tP.wait(s)
                cast_done[0] = sP.inc(s.copy(
                    eff_bf.ap().rearrange("p dc c -> p (dc c)"), wqs0.ap()))

            @blk.tensor
            def _(t, kvh=kvh):
                sP.wait(t, at=cast_done[0])
                for tb in range(2):
                    si = next_slot()
                    if slot_free[si]:
                        vP.wait(t, at=slot_free[si])
                    for dc in range(NDC):
                        _tm(t, slots[si], eff_bf.ap()[:, dc, :],
                                 xfm_bf.ap()[:, dc, 512 * tb:512 * (tb + 1)],
                                 start=(dc == 0), stop=(dc == NDC - 1))
                    tmark(t, tP)

            @blk.vector
            def _(v, kvh=kvh):
                v = vg.bind(v)
                for tb in range(2):
                    si = (slot_seq[0] - 2 + tb) % 4
                    tP.wait(v, at=tP.n - (1 - tb))
                    ins = v.tensor_copy(
                        q_bf.ap()[:, kvh, 512 * tb:512 * (tb + 1)], slots[si])
                    slot_free[si] = vP.n
            qevac.append(vP.n)
        q_mm_last = tP.n
        wq_cast_last = cast_done[0]

        @blk.gpsimd
        def _(g):
            sP.wait(g, at=wq_cast_last)
            tP.wait(g, at=q_mm_last)
            dWO[0].inc(g.dma_start(
                out=wo0.ap()[:, :, :],
                in_=wo_d[:, 0:512].rearrange("(kc p) o -> p kc o", p=128)))

        headlast = {}
        khdma = []
        vhdma = []
        for h in range(HKV):
            par = h % 2
            khb = kh0 if par == 0 else kh1
            vhb = vh0 if par == 0 else vh1

            @blk.sync
            def _(sy, h=h, par=par, khb=khb, vhb=vhb):
                cc.wait(sy, at=cc_v)
                if h < 2:
                    tP.wait(sy, at=q_mm_last)
                else:
                    tP.wait(sy, at=headlast[h - 2])
                for sh in range(4):
                    dKK[par].inc(sy.dma_start(
                        out=khb.ap()[:, TOK * sh:TOK * (sh + 1)],
                        in_=kall_d[sh * HKV + h]))
                khdma.append(dKK[par].n)
                for sh in range(4):
                    dKV[par].inc(sy.dma_start(
                        out=vhb.ap()[:, 8 * sh:8 * (sh + 1), :],
                        in_=vall_d[sh * HKV + h].rearrange(
                            "(kc p) dh -> p kc dh", p=128)))
                vhdma.append(dKV[par].n)

            @blk.scalar
            def _(s, h=h):
                vP.wait(s, at=qevac[h])
                sP.inc(s.activation(qsq.ap(), q_bf.ap()[:, h, :], AF.Square))
            qsq_done = sP.n

            @blk.tensor
            def _(t, h=h):
                sP.wait(t, at=qsq_done)
                if h > 0:
                    vP.wait(t, at=pcpd_free[h - 1])
                _tm(t, pc.ap()[:, :], ones128.ap(), qsq.ap()[:, 0:512],
                    start=True, stop=True)
                _tm(t, pd.ap()[:, :], ones128.ap(), qsq.ap()[:, 512:1024],
                    start=True, stop=True)
                tmark(t, tP)

            if h == 0:
                pcpd_free = {}

            @blk.vector
            def _(v, h=h):
                v = vg.bind(v)
                tP.wait(v)
                v.tensor_reduce(nsc.ap()[0:1, 0:1], pc.ap()[0:1, :], AX.X,
                                AL.max)
                v.tensor_reduce(nsc.ap()[0:1, 1:2], pd.ap()[0:1, :], AX.X,
                                AL.max)
                v.tensor_max(nsc.ap()[0:1, 0:1], nsc.ap()[0:1, 0:1],
                                    nsc.ap()[0:1, 1:2])
            qn_red = vP.n

            @blk.scalar
            def _(s, h=h, par=par, khb=khb):
                dKK[par].wait(s, at=khdma[h])
                sP.inc(s.activation(ksq.ap(), khb.ap()[:, :], AF.Square))
            ksq_done = sP.n

            kn_red = []
            for piece in range(8):
                @blk.tensor
                def _(t, h=h, piece=piece):
                    if piece == 0:
                        sP.wait(t, at=ksq_done)
                        vP.wait(t, at=qn_red)
                    if piece >= 2:
                        vP.wait(t, at=kn_red[piece - 2])
                    tgt = pc if piece % 2 == 0 else pd
                    _tm(t, tgt.ap()[:, :], ones128.ap(),
                        ksq.ap()[:, 512 * piece:512 * (piece + 1)],
                        start=True, stop=True)
                    tmark(t, tP)

                @blk.vector
                def _(v, h=h, piece=piece):
                    v = vg.bind(v)
                    tP.wait(v)
                    tgt = pc if piece % 2 == 0 else pd
                    v.tensor_reduce(nsc.ap()[0:1, 2:3], tgt.ap()[0:1, :],
                                    AX.X, AL.max)
                    if piece == 0:
                        ins = v.tensor_copy(nsc.ap()[0:1, 3:4],
                                            nsc.ap()[0:1, 2:3])
                    else:
                        ins = v.tensor_max(nsc.ap()[0:1, 3:4],
                                           nsc.ap()[0:1, 3:4],
                                           nsc.ap()[0:1, 2:3])
                    kn_red.append(vP.n)

            @blk.vector
            def _(v, h=h):
                v = vg.bind(v)
                v.tensor_add(nsc.ap()[0:1, 4:5], nsc.ap()[0:1, 0:1],
                             nsc.ap()[0:1, 3:4])
                v.tensor_scalar(csrc.ap()[0:1, 0:1],
                                       nsc.ap()[0:1, 4:5],
                                       1.02 * 0.5 * INV_SQRT_DH, None,
                                       AL.mult)

            @blk.tensor
            def _(t, h=h):
                vP.wait(t)
                _tm(t, pc.ap()[:, 0:2], ones128.ap(), csrc.ap(),
                    start=True, stop=True)
                tmark(t, tP)

            @blk.vector
            def _(v, h=h):
                v = vg.bind(v)
                tP.wait(v)
                v.tensor_scalar(cbias.ap(), pc.ap()[:, 0:1], -1.0,
                                       None, AL.mult)
            cb_ready = vP.n

            exp_done = {}
            pt_used = {}
            for kc in range(NKC):
                sb = pa if kc % 2 == 0 else pb
                ptbuf = pt0 if kc % 2 == 0 else pt1

                @blk.tensor
                def _(t, h=h, kc=kc, sb=sb, khb=khb, par=par):
                    if kc < 2:
                        vP.wait(t, at=cb_ready)
                        dKK[par].wait(t, at=khdma[h])
                    else:
                        sP.wait(t, at=exp_done[kc - 2])
                    for half in range(2):
                        _tm(t, sb.ap()[:, 512 * half:512 * (half + 1)],
                                 khb.ap()[:, 128 * kc:128 * (kc + 1)],
                                 q_bf.ap()[:, h, 512 * half:512 * (half + 1)],
                                 start=True, stop=True,
                                 skip_group_check=True)
                    tmark(t, tP)

                @blk.scalar
                def _(s, h=h, kc=kc, sb=sb, ptbuf=ptbuf):
                    tP.wait(s)
                    if kc >= 2:
                        tP.wait(s, at=pt_used[kc - 2])
                    exp_done[kc] = sP.inc(s.activation(
                        ptbuf.ap(), sb.ap()[:, :], AF.Exp,
                        bias=cbias.ap(), scale=INV_SQRT_DH))

                @blk.tensor
                def _(t, h=h, kc=kc, ptbuf=ptbuf, vhb=vhb, par=par):
                    sP.wait(t, at=exp_done[kc])
                    if kc == 0:
                        dKV[par].wait(t, at=vhdma[h])
                    _tm(t, pc.ap()[:, :], ones128.ap(),
                        ptbuf.ap()[:, 0:512],
                        start=(kc == 0), stop=(kc == NKC - 1),
                        skip_group_check=True)
                    _tm(t, pd.ap()[:, :], ones128.ap(),
                        ptbuf.ap()[:, 512:1024],
                        start=(kc == 0), stop=(kc == NKC - 1),
                        skip_group_check=True)
                    for half in range(2):
                        _tm(t, pv.ap()[:, 512 * half:512 * (half + 1)],
                                 vhb.ap()[:, kc, :],
                                 ptbuf.ap()[:, 512 * half:512 * (half + 1)],
                                 start=(kc == 0), stop=(kc == NKC - 1),
                                 skip_group_check=True)
                    pt_used[kc] = tmark(t, tP)

            @blk.vector
            def _(v, h=h):
                v = vg.bind(v)
                tP.wait(v)
                v.reciprocal(recipb.ap()[:, 0:512], pc.ap()[:, :])
                v.reciprocal(recipb.ap()[:, 512:1024], pd.ap()[:, :])
                v.tensor_mul(val_bf.ap()[:, h, :], pv.ap()[:, :],
                                    recipb.ap())
            pcpd_free[h] = vP.n
            headlast[h] = tP.n
        attn_done = vP.n

        for _i in range(4):
            slot_free[_i] = attn_done
        d_spill = []
        strips = [strip0, strip1, strip2, strip3]
        strip_rd = [0, 0, 0, 0]
        d_mm_done = {}
        d7_evac = []
        for ob in range(NOB):
            par = ob % 2
            wob = wo0 if par == 0 else wo1
            if ob + 1 < NOB:
                @blk.gpsimd
                def _(g, ob=ob):
                    if ob >= 1:
                        tP.wait(g, at=d_mm_done[ob - 1])
                    dWO[(ob + 1) % 2].inc(g.dma_start(
                        out=(wo1 if ob % 2 == 0 else wo0).ap()[:, :, :],
                        in_=wo_d[:, 512 * (ob + 1):512 * (ob + 2)].rearrange(
                            "(kc p) o -> p kc o", p=128)))

            for tt in range(NT):
                @blk.tensor
                def _(t, ob=ob, tt=tt, wob=wob, par=par):
                    if tt == 0:
                        dWO[par].wait(t)
                        if ob == 0:
                            vP.wait(t, at=attn_done)
                    si = next_slot()
                    if slot_free[si]:
                        vP.wait(t, at=slot_free[si])
                    for kc in range(HKV):
                        _tm(t, slots[si],
                                 val_bf.ap()[:, kc, 128 * tt:128 * (tt + 1)],
                                 wob.ap()[:, kc, :],
                                 start=(kc == 0), stop=(kc == HKV - 1))
                    tmark(t, tP)

                if ob < 7:
                    @blk.vector
                    def _(v, ob=ob, tt=tt):
                        v = vg.bind(v)
                        si = (slot_seq[0] - 1) % 4
                        sbi = (ob * NT + tt) % 4
                        tP.wait(v)
                        dA.wait(v)
                        v.tensor_copy(strips[sbi].ap(), slots[si])
                        slot_free[si] = vP.n

                    @blk.scalar
                    def _(s, ob=ob, tt=tt):
                        sbi = (ob * NT + tt) % 4
                        vP.wait(s)
                        ins = s.dma_start(
                            out=attn_d[128 * tt:128 * (tt + 1),
                                       512 * ob:512 * (ob + 1)],
                            in_=strips[sbi].ap())
                        strip_rd[sbi] = dA.inc(ins)
                        d_spill.append(dA.n)
                else:
                    @blk.vector
                    def _(v, ob=ob, tt=tt):
                        v = vg.bind(v)
                        si = (slot_seq[0] - 1) % 4
                        tP.wait(v)
                        v.tensor_copy(att7.ap()[:, tt, :], slots[si])
                        slot_free[si] = vP.n
                        d7_evac.append(vP.n)
            d_mm_done[ob] = tP.n
        d_spill_all = dA.n
        d_v_last = vP.n

        x1_spill = []
        x1bf_spill = []
        df_consumed = {}
        for tt in range(NT):
            @blk.sync
            def _(sy, tt=tt):
                dA.wait(sy, at=d_spill_all)
                vP.wait(sy, at=d7_evac[tt])
                if tt >= 1:
                    sP.wait(sy, at=df_consumed[tt - 1])
                    dS.wait(sy, at=x1bf_spill[tt - 1])
                dS.inc(sy.dma_start(
                    out=df_acc.ap()[:, 0:3584],
                    in_=attn_d[128 * tt:128 * (tt + 1), :]))
                dS.inc(sy.dma_start(out=df_xtm.ap(),
                                    in_=xtm_d[128 * tt:128 * (tt + 1), :]))
            df_ld = dS.n

            @blk.vector
            def _(v, tt=tt):
                v = vg.bind(v)
                dS.wait(v, at=df_ld)
                v.tensor_add(df_acc.ap()[:, 0:3584], df_acc.ap()[:, 0:3584],
                             df_xtm.ap()[:, 0:3584])
                v.tensor_add(df_acc.ap()[:, 3584:4096],
                                    att7.ap()[:, tt, :],
                                    df_xtm.ap()[:, 3584:4096])

            @blk.scalar
            def _(s, tt=tt):
                vP.wait(s)
                if tt == 0:
                    s.preload_act_table(AF.Sqrt)
                s.activation(df_xtm.ap(), df_acc.ap(), AF.Copy,
                             accum_out=lnsum.ap()[:, 0:1])
                sP.inc(s.activation(df_xtm.ap(), df_acc.ap(), AF.Square,
                                    accum_out=lnsum.ap()[:, 1:2]))

            @blk.vector
            def _(v, tt=tt):
                v = vg.bind(v)
                sP.wait(v)
                v.tensor_scalar(lnm.ap()[:, 0:1], lnsum.ap()[:, 0:1], 1.0 / D,
                                None, AL.mult)
                v.tensor_scalar(lnm.ap()[:, 1:2], lnsum.ap()[:, 1:2], 1.0 / D,
                                None, AL.mult)
                v.tensor_mul(nsc2.ap()[:, 0:1], lnm.ap()[:, 0:1],
                             lnm.ap()[:, 0:1])
                v.tensor_sub(lnm.ap()[:, 1:2], lnm.ap()[:, 1:2],
                             nsc2.ap()[:, 0:1])
                v.tensor_scalar(lnm.ap()[:, 1:2], lnm.ap()[:, 1:2],
                                       1.0, LN_EPS, AL.mult, AL.add)

            @blk.scalar
            def _(s, tt=tt):
                vP.wait(s)
                sP.inc(s.activation(nsc2.ap()[:, 1:2], lnm.ap()[:, 1:2],
                                    AF.Sqrt))

            @blk.vector
            def _(v, tt=tt):
                v = vg.bind(v)
                sP.wait(v)
                v.reciprocal(lnm.ap()[:, 2:3], nsc2.ap()[:, 1:2])
                v.tensor_mul(nsc2.ap()[:, 2:3], lnm.ap()[:, 1:2],
                             lnm.ap()[:, 2:3])
                v.tensor_mul(nsc2.ap()[:, 2:3], nsc2.ap()[:, 2:3],
                             lnm.ap()[:, 2:3])
                v.tensor_scalar(nsc2.ap()[:, 2:3], nsc2.ap()[:, 2:3], -0.5,
                                1.5, AL.mult, AL.add)
                v.tensor_mul(lnm.ap()[:, 2:3], lnm.ap()[:, 2:3],
                             nsc2.ap()[:, 2:3])
                v.tensor_mul(lnm.ap()[:, 3:4], lnm.ap()[:, 0:1],
                             lnm.ap()[:, 2:3])
                v.tensor_scalar(lnm.ap()[:, 3:4], lnm.ap()[:, 3:4],
                                       -1.0, None, AL.mult)

            @blk.vector
            def _(v, tt=tt):
                v = vg.bind(v)
                if tt >= 1:
                    dS.wait(v, at=x1_spill[tt - 1])
                v.tensor_scalar(df_xtm.ap(), df_acc.ap(),
                                       lnm.ap()[:, 2:3], lnm.ap()[:, 3:4],
                                       AL.mult, AL.add)

            @blk.scalar
            def _(s, tt=tt):
                vP.wait(s)
                df_consumed[tt] = sP.inc(s.copy(df_x1bf.ap(), df_xtm.ap()))

            @blk.sync
            def _(sy, tt=tt):
                sP.wait(sy, at=df_consumed[tt])
                x1_spill.append(dS.inc(sy.dma_start(
                    out=x1sp_d[128 * tt:128 * (tt + 1), :], in_=df_xtm.ap())))
                x1bf_spill.append(dS.inc(sy.dma_start(
                    out=x1bf_d.ap()[:, 128 * tt:128 * (tt + 1), :]
                    .transpose([1, 0, 2]),
                    in_=df_x1bf.ap().rearrange("p (dc c) -> p dc c", c=128))))

        @blk.sync
        def _(sy):
            dS.wait(sy, at=x1bf_spill[-1])
            for dc in range(NDC):
                dS.inc(sy.dma_start(out=x1_fm.ap()[:, dc, :], in_=x1bf_d[dc],
                                    transpose=True))
        x1fm_done = dS.n

        @blk.vector
        def _(v):
            v = vg.bind(v)
            tP.wait(v, at=d_mm_done[NOB - 1])
            v.memset(b8_bf.ap(), 0.0)
            v.memset(bob0.ap(), 0.0)
            v.memset(bob1.ap(), 0.0)
            v.memset(lim_bf.ap(), 0.0)
            v.memset(wgt.ap().rearrange("p a r -> p (a r)"), 0.0)
        e_zero = vP.n

        @blk.gpsimd
        def _(g):
            tP.wait(g, at=d_mm_done[NOB - 1])
            vP.wait(g, at=e_zero)
            dG.inc(g.dma_start(out=a_bf.ap()[:, :, :],
                               in_=a_d.rearrange("(dc p) r -> p dc r", p=128)))
            dG.inc(g.dma_start(out=w8_bf.ap()[:, :, :],
                               in_=w8_d.rearrange("(dc p) e -> p dc e", p=128)))
            dG.inc(g.dma_start(out=limt_bf.ap(), in_=limt_d))
            dG.inc(g.dma_start(out=lim_bf.ap()[0:E, :], in_=lim_d))
            dG.inc(g.dma_start(out=b8_bf.ap()[0:1, :], in_=b_d[:, 0:E]))
            dWP[0].inc(g.dma_start(
                out=wping.ap()[:, :, :],
                in_=wt_d[0].rearrange("(dc p) o -> p dc o", p=128)))
            dWP[0].inc(g.dma_start(out=bmob0.ap(), in_=bm_d[:, 0:512]))
            dWP[0].inc(g.dma_start(out=bob0.ap()[0:1, :], in_=b_d[:, 0:512]))
        e_small = dG.n

        @blk.tensor
        def _(t):
            dS.wait(t, at=x1fm_done)
            dG.wait(t, at=e_small)
            for half in range(2):
                for dc in range(NDC):
                    _tm(t, pv.ap()[0:RK, 512 * half:512 * (half + 1)],
                             a_bf.ap()[:, dc, :],
                             x1_fm.ap()[:, dc, 512 * half:512 * (half + 1)],
                             start=(dc == 0), stop=(dc == NDC - 1))
            tmark(t, tP)

        @blk.vector
        def _(v):
            v = vg.bind(v)
            tP.wait(v)
            v.tensor_copy(z_s.ap(), pv.ap()[0:RK, :])
            v.tensor_copy(z_bf.ap(), z_s.ap())
        z_ready = vP.n

        @blk.tensor
        def _(t):
            vP.wait(t, at=z_ready)
            x1r = x1_fm.ap().rearrange("p dc (at w) -> p dc at w", w=256)
            zr = z_bf.ap().rearrange("r (at w) -> r at w", w=256)
            for at_ in range(4):
                for dc in range(NDC):
                    _tm(t, pc.ap()[:, 8 * at_:8 * (at_ + 1)],
                        x1r[:, dc, at_, 1::2], w8_bf.ap()[:, dc, :],
                        start=(dc == 0), stop=False, skip_group_check=True)
                _tm(t, pc.ap()[:, 8 * at_:8 * (at_ + 1)],
                    ones128.ap(), b8_bf.ap(), start=False, stop=True,
                    skip_group_check=True)
                _tm(t, pd.ap()[:, 8 * at_:8 * (at_ + 1)],
                         zr[:, at_, 1::2], limt_bf.ap(),
                         start=True, stop=True)
            tmark(t, tP)

        @blk.vector
        def _(v):
            v = vg.bind(v)
            tP.wait(v)
            v.tensor_copy(h8_s.ap().rearrange("p a e -> p (a e)"),
                          pc.ap()[:, 0:32])
            v.tensor_copy(ds_s.ap().rearrange("p a e -> p (a e)"),
                          pd.ap()[:, 0:32])
            v.tensor_reduce(nsc2.ap()[:, 4:5],
                            h8_s.ap().rearrange("p a e -> p (a e)"),
                            AX.X, AL.max, apply_absolute_value=True)
            v.tensor_reduce(nsc2.ap()[:, 5:6],
                                   ds_s.ap().rearrange("p a e -> p (a e)"),
                                   AX.X, AL.max, apply_absolute_value=True)

        @blk.sync
        def _(sy):
            vP.wait(sy)
            dS.inc(sy.dma_start(out=scpr_d[:, :], in_=nsc2.ap()[:, 4:6]))
            dS.wait(sy)
            dS.inc(sy.dma_start(out=mx2.ap(),
                                in_=scpr_d.ap().transpose([1, 0]),
                                allow_slow_non_contiguous=True))

        @blk.vector
        def _(v):
            v = vg.bind(v)
            dS.wait(v)
            v.tensor_reduce(nsc2.ap()[0:2, 6:7], mx2.ap(), AX.X,
                                   AL.max)

        @blk.sync
        def _(sy):
            vP.wait(sy)
            dS.inc(sy.dma_start(out=scsrc_d.ap(), in_=nsc2.ap()[0:8, 6:7]))

        @blk.gpsimd
        def _(g):
            dS.wait(g)
            cc.inc(g.collective_compute(
                "AllReduce", AL.max, replica_groups=ALLG,
                ins=[scsrc_d.ap().opt()], outs=[scdst_d.ap().opt()]))

        @blk.gpsimd
        def _(g):
            cc.wait(g)
            dG.inc(g.dma_start(
                out=scsrc.ap()[0:1, :],
                in_=scdst_d.ap()[0:2].rearrange("(a b) -> a b", a=1)))

        @blk.tensor
        def _(t):
            dG.wait(t)
            _tm(t, pc.ap()[:, 0:2], ones128.ap(), scsrc.ap(),
                start=True, stop=True)
            tmark(t, tP)

        @blk.vector
        def _(v):
            v = vg.bind(v)
            tP.wait(v)
            v.tensor_scalar_max(recips.ap(), pc.ap()[:, 0:2], SC_EPS)
            v.reciprocal(recips.ap(), recips.ap())

        wt_done = []
        for at_ in range(4):
            @blk.vector
            def _(v, at_=at_):
                v = vg.bind(v)
                v.tensor_scalar(gt1.ap(), h8_s.ap()[:, at_, :],
                                recips.ap()[:, 0:1], 0.5, AL.mult, AL.mult)
                v.tensor_scalar(gt2.ap(), ds_s.ap()[:, at_, :],
                                recips.ap()[:, 1:2], 0.5, AL.mult, AL.mult)
                v.tensor_add(gt1.ap(), gt1.ap(), gt2.ap())
                v.tensor_reduce(nsc2.ap()[:, 7:8], gt1.ap(), AX.X, AL.max)
                v.tensor_scalar(nsc2.ap()[:, 8:9], nsc2.ap()[:, 7:8],
                                       -1.0, None, AL.mult)

            @blk.scalar
            def _(s, at_=at_):
                vP.wait(s)
                if at_ == 0:
                    s.preload_act_table(AF.Exp)
                sP.inc(s.activation(gt2.ap(), gt1.ap(), AF.Exp,
                                    bias=nsc2.ap()[:, 8:9]))

            @blk.vector
            def _(v, at_=at_):
                v = vg.bind(v)
                sP.wait(v)
                v.tensor_reduce(nsc2.ap()[:, 9:10], gt2.ap(), AX.X, AL.add)
                v.reciprocal(nsc2.ap()[:, 10:11], nsc2.ap()[:, 9:10])
                v.tensor_scalar(gt1.ap(), gt2.ap(), nsc2.ap()[:, 10:11], None,
                                AL.mult)
                v.max(out=gt3.ap(), in_=gt1.ap())
                v.tensor_scalar(nsc2.ap()[:, 11:12], gt3.ap()[:, 1:2],
                                       -1.0, None, AL.mult)

            @blk.scalar
            def _(s, at_=at_):
                vP.wait(s)
                sP.inc(s.activation(gt2.ap(), gt1.ap(), AF.Tanh,
                                    bias=nsc2.ap()[:, 11:12], scale=1.0))

            @blk.vector
            def _(v, at_=at_):
                v = vg.bind(v)
                sP.wait(v)
                v.tensor_scalar(gt2.ap(), gt2.ap(), 0.5, 0.5, AL.mult, AL.add)
                v.tensor_mul(gt1.ap(), gt1.ap(), gt2.ap())
                v.tensor_reduce(nsc2.ap()[:, 12:13], gt1.ap(), AX.X, AL.add)
                v.tensor_scalar(nsc2.ap()[:, 12:13], nsc2.ap()[:, 12:13], 1.0,
                                1e-9, AL.mult, AL.add)
                v.reciprocal(nsc2.ap()[:, 13:14], nsc2.ap()[:, 12:13])
                v.tensor_scalar(
                    wgt.ap()[:, at_, 0:E], gt1.ap(), nsc2.ap()[:, 13:14], None,
                    AL.mult)
                wt_done.append(vP.n)

        @blk.tensor
        def _(t):
            vP.wait(t, at=wt_done[-1])
            for at_ in range(4):
                _tt(t, ptb[0:RK, 128 * at_:128 * (at_ + 1)],
                    wgt.ap()[:, at_, :], ident.ap())
            tmark(t, tP)

        @blk.vector
        def _(v):
            v = vg.bind(v)
            tP.wait(v)
            v.tensor_copy(wem.ap(), ptb[0:RK, 0:512])

        @blk.tensor
        def _(t):
            vP.wait(t)
            _tm(t, pd.ap()[0:RK, :], lim_bf.ap(), wem.ap(),
                     start=True, stop=True)
            tmark(t, tP)

        @blk.vector
        def _(v):
            v = vg.bind(v)
            tP.wait(v)
            v.tensor_copy(pmix_s.ap(), pd.ap()[0:RK, :])
            v.tensor_copy(pmix_tok.ap()[:, 0::2], pmix_s.ap())
            v.tensor_copy(pmix_tok.ap()[:, 1::2], pmix_s.ap())
            v.scalar_tensor_tensor(zp_bf.ap(), z_s.ap(), 0.5,
                                          pmix_tok.ap(), AL.mult, AL.mult)
        zp_ready = vP.n

        e_spill = []
        e_slot_free = [max(d_v_last, z_ready)] * 6
        e_seq = [0]
        strip_rd2 = [d_spill_all] * 4
        e_mm_done = {}
        e7_evac = []
        for ob in range(NOB):
            par = ob % 2
            wbuf = wping if par == 0 else wpong
            bmob = bmob0 if par == 0 else bmob1
            bob = bob0 if par == 0 else bob1

            if ob + 1 < NOB:
                @blk.gpsimd
                def _(g, ob=ob):
                    if ob == 0:
                        sP.wait(g, at=df_consumed[NT - 1])
                        dS.wait(g, at=x1bf_spill[-1])
                    if ob >= 1:
                        tP.wait(g, at=e_mm_done[ob - 1])
                    np_ = (ob + 1) % 2
                    dWP[np_].inc(g.dma_start(
                        out=(wpong if ob % 2 == 0 else wping).ap()[:, :, :],
                        in_=wt_d[ob + 1].rearrange("(dc p) o -> p dc o",
                                                   p=128)))
                    dWP[np_].inc(g.dma_start(
                        out=(bmob1 if ob % 2 == 0 else bmob0).ap(),
                        in_=bm_d[:, 512 * (ob + 1):512 * (ob + 2)]))
                    dWP[np_].inc(g.dma_start(
                        out=(bob1 if ob % 2 == 0 else bob0).ap()[0:1, :],
                        in_=b_d[:, 512 * (ob + 1):512 * (ob + 2)]))

            for tt in range(NT):
                @blk.tensor
                def _(t, ob=ob, tt=tt, wbuf=wbuf, bmob=bmob, bob=bob,
                      par=par):
                    if tt == 0:
                        dWP[par].wait(t)
                        if ob == 0:
                            dS.wait(t, at=x1fm_done)
                    si = e_seq[0] % 6
                    e_seq[0] += 1
                    if e_slot_free[si]:
                        vP.wait(t, at=e_slot_free[si])
                    for dc in range(NDC):
                        _tm(t, eslots[si],
                                 x1_fm.ap()[:, dc, 128 * tt:128 * (tt + 1)],
                                 wbuf.ap()[:, dc, :],
                                 start=(dc == 0), stop=False)
                    _tm(t, eslots[si], ones128.ap(), bob.ap(),
                        start=False, stop=False, skip_group_check=True)
                    if ob == 0 and tt == 0:
                        vP.wait(t, at=zp_ready)
                    _tm(t, eslots[si],
                             zp_bf.ap()[:, 128 * tt:128 * (tt + 1)],
                             bmob.ap(), start=False, stop=True)
                    tmark(t, tP)

                if ob < 7:
                    @blk.vector
                    def _(v, ob=ob, tt=tt):
                        v = vg.bind(v)
                        si = (e_seq[0] - 1) % 6
                        sbi = (ob * NT + tt) % 4
                        tP.wait(v)
                        dA.wait(v)
                        v.tensor_copy(strips[sbi].ap(), eslots[si])
                        e_slot_free[si] = vP.n

                    @blk.scalar
                    def _(s, ob=ob, tt=tt):
                        sbi = (ob * NT + tt) % 4
                        vP.wait(s)
                        ins = s.dma_start(
                            out=hdb_d[128 * tt:128 * (tt + 1),
                                      512 * ob:512 * (ob + 1)],
                            in_=strips[sbi].ap())
                        strip_rd2[sbi] = dA.inc(ins)
                        e_spill.append(dA.n)
                else:
                    @blk.vector
                    def _(v, ob=ob, tt=tt):
                        v = vg.bind(v)
                        si = (e_seq[0] - 1) % 6
                        tP.wait(v)
                        v.tensor_copy(hd7.ap()[:, tt, :], eslots[si])
                        e_slot_free[si] = vP.n
                        e7_evac.append(vP.n)
            e_mm_done[ob] = tP.n
        e_spill_all = dA.n

        y_out = []
        f_consumed = {}
        for tt in range(NT):
            @blk.sync
            def _(sy, tt=tt):
                dA.wait(sy, at=e_spill_all)
                vP.wait(sy, at=e7_evac[tt])
                if tt >= 1:
                    vP.wait(sy, at=f_consumed[tt - 1])
                    dA.wait(sy, at=y_out[tt - 1])
                dS.inc(sy.dma_start(out=f_x1.ap(),
                                    in_=x1sp_d[128 * tt:128 * (tt + 1), :]))
                dS.inc(sy.dma_start(out=f_hdb.ap()[:, 0:3584],
                                    in_=hdb_d[128 * tt:128 * (tt + 1), :]))
            f_ld = dS.n

            @blk.vector
            def _(v, tt=tt):
                v = vg.bind(v)
                dS.wait(v, at=f_ld)
                v.tensor_add(f_hdb.ap()[:, 0:3584], f_hdb.ap()[:, 0:3584],
                             f_x1.ap()[:, 0:3584])
                v.tensor_add(f_hdb.ap()[:, 3584:4096],
                                    hd7.ap()[:, tt, :],
                                    f_x1.ap()[:, 3584:4096])

            @blk.scalar
            def _(s, tt=tt):
                vP.wait(s)
                if tt == 0:
                    s.preload_act_table(AF.Sqrt)
                s.activation(f_x1.ap(), f_hdb.ap(), AF.Copy,
                             accum_out=lnsum.ap()[:, 0:1])
                sP.inc(s.activation(f_x1.ap(), f_hdb.ap(), AF.Square,
                                    accum_out=lnsum.ap()[:, 1:2]))

            @blk.vector
            def _(v, tt=tt):
                v = vg.bind(v)
                sP.wait(v)
                v.tensor_scalar(lnm.ap()[:, 0:1], lnsum.ap()[:, 0:1], 1.0 / D,
                                None, AL.mult)
                v.tensor_scalar(lnm.ap()[:, 1:2], lnsum.ap()[:, 1:2], 1.0 / D,
                                None, AL.mult)
                v.tensor_mul(nsc2.ap()[:, 0:1], lnm.ap()[:, 0:1],
                             lnm.ap()[:, 0:1])
                v.tensor_sub(lnm.ap()[:, 1:2], lnm.ap()[:, 1:2],
                             nsc2.ap()[:, 0:1])
                v.tensor_scalar(lnm.ap()[:, 1:2], lnm.ap()[:, 1:2],
                                       1.0, LN_EPS, AL.mult, AL.add)

            @blk.scalar
            def _(s, tt=tt):
                vP.wait(s)
                sP.inc(s.activation(nsc2.ap()[:, 1:2], lnm.ap()[:, 1:2],
                                    AF.Sqrt))

            @blk.vector
            def _(v, tt=tt):
                v = vg.bind(v)
                sP.wait(v)
                v.reciprocal(lnm.ap()[:, 2:3], nsc2.ap()[:, 1:2])
                v.tensor_mul(nsc2.ap()[:, 2:3], lnm.ap()[:, 1:2],
                             lnm.ap()[:, 2:3])
                v.tensor_mul(nsc2.ap()[:, 2:3], nsc2.ap()[:, 2:3],
                             lnm.ap()[:, 2:3])
                v.tensor_scalar(nsc2.ap()[:, 2:3], nsc2.ap()[:, 2:3], -0.5,
                                1.5, AL.mult, AL.add)
                v.tensor_mul(lnm.ap()[:, 2:3], lnm.ap()[:, 2:3],
                             nsc2.ap()[:, 2:3])
                v.tensor_mul(lnm.ap()[:, 3:4], lnm.ap()[:, 0:1],
                             lnm.ap()[:, 2:3])
                v.tensor_scalar(lnm.ap()[:, 3:4], lnm.ap()[:, 3:4],
                                       -1.0, None, AL.mult)

            @blk.vector
            def _(v, tt=tt):
                v = vg.bind(v)
                v.tensor_scalar(
                    f_x1.ap(), f_hdb.ap(), lnm.ap()[:, 2:3], lnm.ap()[:, 3:4],
                    AL.mult, AL.add)
                f_consumed[tt] = vP.n

            @blk.scalar
            def _(s, tt=tt):
                vP.wait(s, at=f_consumed[tt])
                y_out.append(dA.inc(s.dma_start(
                    out=y_d[128 * tt:128 * (tt + 1), :], in_=f_x1.ap())))

        @blk.sync
        def _(sy):
            dA.wait(sy, at=y_out[-1])

        if taps:
            tp_kall = nc.dram_tensor("tp_kall", [4 * HKV, DH, TOK], BF16,
                                     kind="ExternalOutput").ap()
            tp_vall = nc.dram_tensor("tp_vall", [4 * HKV, TOK, DH], BF16,
                                     kind="ExternalOutput").ap()
            tp_att = nc.dram_tensor("tp_att", [TOK, 7 * 512], F32,
                                    kind="ExternalOutput").ap()
            tp_x1 = nc.dram_tensor("tp_x1", [TOK, D], F32,
                                   kind="ExternalOutput").ap()
            tp_hdb = nc.dram_tensor("tp_hdb", [TOK, 7 * 512], F32,
                                    kind="ExternalOutput").ap()
            tp_sc = nc.dram_tensor("tp_sc", [128, 2], F32,
                                   kind="ExternalOutput").ap()
            tp_zp = nc.dram_tensor("tp_zp", [RK, TOK], BF16,
                                   kind="ExternalOutput").ap()
            tp_x1bf = nc.dram_tensor("tp_x1bf", [NDC, TOK, 128], BF16,
                                     kind="ExternalOutput").ap()

            @blk.sync
            def _(sy):
                dA.wait(sy, at=y_out[-1])
                dS.inc(sy.dma_start(out=tp_kall, in_=kall_d.ap()))
                dS.inc(sy.dma_start(out=tp_vall, in_=vall_d.ap()))
                dS.inc(sy.dma_start(out=tp_att, in_=attn_d.ap()))
                dS.inc(sy.dma_start(out=tp_x1, in_=x1sp_d.ap()))
                dS.inc(sy.dma_start(out=tp_hdb, in_=hdb_d.ap()))
                dS.inc(sy.dma_start(out=tp_x1bf, in_=x1bf_d.ap()))
                dS.inc(sy.dma_start(out=tp_sc, in_=recips.ap()))
                dS.inc(sy.dma_start(out=tp_zp, in_=zp_bf.ap()))
                dS.wait(sy)

    return nc


_CACHE = {}


def kernel(L0_attn_x, L0_attn_wq, L0_attn_wk, L0_attn_wv, L0_attn_wo,
           L0_ffn_W, L0_ffn_b, L0_ffn_A, L0_ffn_Bmat, L0_ffn_LiMEs):
    x = np.ascontiguousarray(np.asarray(L0_attn_x, dtype=np.float32))
    wq = np.ascontiguousarray(np.asarray(L0_attn_wq, dtype=np.float32))
    wk = np.ascontiguousarray(np.asarray(L0_attn_wk, dtype=np.float32))
    wv = np.ascontiguousarray(np.asarray(L0_attn_wv, dtype=np.float32))
    wo = np.ascontiguousarray(np.asarray(L0_attn_wo, dtype=np.float32))
    W = np.ascontiguousarray(np.asarray(L0_ffn_W, dtype=np.float32))
    bvec = np.ascontiguousarray(np.asarray(L0_ffn_b, dtype=np.float32))
    A = np.ascontiguousarray(np.asarray(L0_ffn_A, dtype=np.float32))
    Bm = np.ascontiguousarray(np.asarray(L0_ffn_Bmat, dtype=np.float32))
    Lm = np.ascontiguousarray(np.asarray(L0_ffn_LiMEs, dtype=np.float32))

    wqg = np.ascontiguousarray(wq.reshape(D, HKV, G, DH).transpose(1, 2, 0, 3))
    wkf = np.ascontiguousarray(wk.reshape(D, HKV, DH).transpose(1, 0, 2))
    wvf = np.ascontiguousarray(wv.reshape(D, HKV, DH).transpose(1, 0, 2))
    wt = np.ascontiguousarray(W.T.reshape(D, NOB, 512).transpose(1, 0, 2))
    w8 = np.ascontiguousarray(W[0:E, :].T)
    bvec2 = np.ascontiguousarray(bvec.reshape(1, D))
    identbf = np.eye(128, dtype=ml_dtypes.bfloat16)
    limT = np.ascontiguousarray(Lm.T)

    in_maps = []
    for c in range(8):
        b = c // 4
        r0 = (c % 4) * TOK
        xs = x[b, r0:r0 + TOK, :]
        in_maps.append({
            "xfm": np.ascontiguousarray(xs.T),
            "xtm": np.ascontiguousarray(xs),
            "wqg": wqg, "wkf": wkf, "wvf": wvf, "wo": wo,
            "wt": wt, "w8": w8, "bvec": bvec2, "amat": A, "bmat": Bm,
            "limt": limT, "lime": Lm, "identbf": identbf,
        })

    if "nc" not in _CACHE:
        _CACHE["nc"] = build_program()
    nc = _CACHE["nc"]
    res = run_bass_kernel_spmd(nc, in_maps, core_ids=list(range(8)))

    out = np.empty((B, S, D), dtype=np.float32)
    for c in range(8):
        b = c // 4
        r0 = (c % 4) * TOK
        out[b, r0:r0 + TOK, :] = res.results[c]["y"]
    return out


# version 06
# speedup vs baseline: 1.0252x; 1.0252x over previous
import numpy as np
import ml_dtypes

import concourse.bass as bass
import concourse.mybir as mybir
from concourse.bass_utils import run_bass_kernel_spmd

F32 = mybir.dt.float32
BF16 = mybir.dt.bfloat16
AF = mybir.ActivationFunctionType
AL = mybir.AluOpType
AX = mybir.AxisListType

B, S, D = 2, 4096, 4096
HKV, DH, KV, G = 8, 128, 1024, 4
TOK = 1024
NT = TOK // 128
NDC = D // 128
NKC = S // 128
NOB = 8
RK, E = 32, 8
INV_SQRT_DH = 1.0 / float(np.sqrt(DH))
LN_EPS = 1e-5
SC_EPS = 1e-6
NGROUPS = [[0, 1, 2, 3], [4, 5, 6, 7]]
ALLG = [[0, 1, 2, 3, 4, 5, 6, 7]]


class Led:

    def __init__(self, nc, name, dma=False):
        self.h = nc.alloc_semaphore(name)
        self.n = 0
        self.step = 16 if dma else 1

    def inc(self, instr, k=None):
        k = self.step if k is None else k
        instr.then_inc(self.h, k)
        self.n += k
        return self.n

    def wait(self, eng, at=None):
        eng.wait_ge(self.h, self.n if at is None else at)



class VGuard:

    def __init__(self, vP):
        self.vP = vP
        self.w = {}
        self.guarded = 0

    def bind(self, v):
        self.v = v
        return self

    def _name(self, ap):
        try:
            return ap.tensor.name
        except AttributeError:
            return None

    def _pre(self, ins_aps, out_aps):
        need = 0
        for ap in list(ins_aps) + list(out_aps):
            n = self._name(ap)
            if n is not None and n in self.w:
                need = max(need, self.w[n])
        if need > self.guarded:
            self.v.wait_ge(self.vP.h, need)
            self.guarded = need

    def _post(self, ins, out_aps):
        self.vP.inc(ins)
        for ap in out_aps:
            n = self._name(ap)
            if n is not None:
                self.w[n] = self.vP.n
        return ins

    def wait_ge(self, sem, val):
        return self.v.wait_ge(sem, val)

    def memset(self, out, c):
        self._pre([], [out])
        return self._post(self.v.memset(out, c), [out])

    def tensor_copy(self, out, in_):
        self._pre([in_], [out])
        return self._post(self.v.tensor_copy(out, in_), [out])

    def tensor_add(self, out, a, b):
        self._pre([a, b], [out])
        return self._post(self.v.tensor_add(out, a, b), [out])

    def tensor_sub(self, out, a, b):
        self._pre([a, b], [out])
        return self._post(self.v.tensor_sub(out, a, b), [out])

    def tensor_mul(self, out, a, b):
        self._pre([a, b], [out])
        return self._post(self.v.tensor_mul(out, a, b), [out])

    def tensor_max(self, out, a, b):
        self._pre([a, b], [out])
        return self._post(self.v.tensor_max(out, a, b), [out])

    def tensor_scalar(self, out, in_, s1, s2, op0, op1=None, **kw):
        aps = [in_] + [s for s in (s1, s2) if hasattr(s, "tensor")]
        self._pre(aps, [out])
        if op1 is None:
            ins = self.v.tensor_scalar(out, in_, s1, s2, op0, **kw)
        else:
            ins = self.v.tensor_scalar(out, in_, s1, s2, op0, op1, **kw)
        return self._post(ins, [out])

    def tensor_scalar_max(self, out, in_, s1):
        aps = [in_] + ([s1] if hasattr(s1, "tensor") else [])
        self._pre(aps, [out])
        return self._post(self.v.tensor_scalar_max(out, in_, s1), [out])

    def tensor_reduce(self, out, in_, ax, op, **kw):
        self._pre([in_], [out])
        return self._post(self.v.tensor_reduce(out, in_, ax, op, **kw), [out])

    def reciprocal(self, out, in_):
        self._pre([in_], [out])
        return self._post(self.v.reciprocal(out, in_), [out])

    def max(self, out, in_):
        self._pre([in_], [out])
        return self._post(self.v.max(out=out, in_=in_), [out])

    def scalar_tensor_tensor(self, out, in0, sc, in1, op0, op1, **kw):
        aps = [in0, in1] + ([sc] if hasattr(sc, "tensor") else [])
        self._pre(aps, [out])
        return self._post(
            self.v.scalar_tensor_tensor(out, in0, sc, in1, op0, op1, **kw),
            [out])


_LAST_T = [None]


def _tm(t, *a, **k):
    ins = t.matmul(*a, **k)
    _LAST_T[0] = ins
    return ins


def _tt(t, *a, **k):
    ins = t.transpose(*a, **k)
    _LAST_T[0] = ins
    return ins


def tmark(t, tP):
    return tP.inc(_LAST_T[0])

def build_program(taps=False):
    nc = bass.Bass("TRN2", target_bir_lowering=False)

    xfm_d = nc.dram_tensor("xfm", [D, TOK], F32, kind="ExternalInput").ap()
    xtm_d = nc.dram_tensor("xtm", [TOK, D], F32, kind="ExternalInput").ap()
    wqg_d = nc.dram_tensor("wqg", [HKV, G, D, DH], F32, kind="ExternalInput").ap()
    wkf_d = nc.dram_tensor("wkf", [HKV, D, DH], F32, kind="ExternalInput").ap()
    wvf_d = nc.dram_tensor("wvf", [HKV, D, DH], F32, kind="ExternalInput").ap()
    wo_d = nc.dram_tensor("wo", [KV, D], F32, kind="ExternalInput").ap()
    wt_d = nc.dram_tensor("wt", [NOB, D, 512], F32, kind="ExternalInput").ap()
    w8_d = nc.dram_tensor("w8", [D, E], F32, kind="ExternalInput").ap()
    b_d = nc.dram_tensor("bvec", [1, D], F32, kind="ExternalInput").ap()
    a_d = nc.dram_tensor("amat", [D, RK], F32, kind="ExternalInput").ap()
    bm_d = nc.dram_tensor("bmat", [RK, D], F32, kind="ExternalInput").ap()
    limt_d = nc.dram_tensor("limt", [RK, E], F32, kind="ExternalInput").ap()
    lim_d = nc.dram_tensor("lime", [E, RK], F32, kind="ExternalInput").ap()
    id_d = nc.dram_tensor("identbf", [128, 128], BF16, kind="ExternalInput").ap()
    y_d = nc.dram_tensor("y", [TOK, D], F32, kind="ExternalOutput").ap()

    kown_d = nc.dram_tensor("kown", [HKV, DH, TOK], BF16)
    kall_d = nc.dram_tensor("kall", [4 * HKV, DH, TOK], BF16)
    vown_d = nc.dram_tensor("vown", [HKV, TOK, DH], BF16)
    vall_d = nc.dram_tensor("vall", [4 * HKV, TOK, DH], BF16)
    scpr_d = nc.dram_tensor("scpr", [128, 2], F32)
    scsrc_d = nc.dram_tensor("scsrc", [8], F32)
    scdst_d = nc.dram_tensor("scdst", [8], F32)
    attn_d = nc.dram_tensor("attnout", [TOK, 7 * 512], F32)
    x1sp_d = nc.dram_tensor("x1sp", [TOK, D], F32)
    hdb_d = nc.dram_tensor("hdb", [TOK, 7 * 512], F32)

    SB_END = 229376
    base = SB_END - nc.sbuf_bytes_remaining
    cur = [base]

    def alloc_at(name, shape, dtype, offset):
        o = (offset + 63) // 64 * 64
        h = nc.alloc_sbuf_tensor_at(name, shape, dtype, offset=o)
        per = int(np.prod(shape[1:])) * mybir.dt.size(dtype)
        return h, o + per

    def perm(name, shape, dtype):
        h, end = alloc_at(name, shape, dtype, cur[0])
        cur[0] = end
        return h

    class Region:
        def __init__(self, size):
            self.start = (cur[0] + 63) // 64 * 64
            self.size = size
            cur[0] = self.start + size

        def seq(self, *specs):
            out = []
            pos = self.start
            for name, shape, dtype in specs:
                h, pos = alloc_at(name, shape, dtype, pos)
                out.append(h)
            assert pos <= self.start + self.size, \
                f"region overflow {pos - self.start} > {self.size}"
            return out

    ident = perm("ident", [128, 128], BF16)
    ones128 = perm("ones128", [128, 128], BF16)
    csrc = perm("csrc", [128, 2], BF16)
    scsrc = perm("scsrc", [128, 2], BF16)
    cbias = perm("cbias", [128, 1], F32)
    nsc = perm("nsc", [128, 16], F32)
    nsc2 = perm("nsc2", [128, 16], F32)
    lnm = perm("lnm", [128, 32], F32)
    lnsum = perm("lnsum", [128, 16], F32)
    recips = perm("recips", [128, 2], F32)
    mx2 = perm("mx2", [2, 128], F32)
    sc2 = perm("sc2", [1, 2], F32)

    r1 = Region(16384)
    (q_bf,) = r1.seq(("q_bf", [128, HKV, TOK], BF16))
    (att7,) = r1.seq(("att7", [128, NT, 512], F32))
    z_s, z_bf, pmix_tok, pmix_s, h8_s, ds_s, gt1, gt2, gt3, wgt, wem = r1.seq(
        ("z_s", [RK, TOK], F32), ("z_bf", [RK, TOK], BF16),
        ("pmix_tok", [RK, TOK], F32), ("pmix_s", [RK, 512], F32),
        ("h8_s", [128, 4, E], F32), ("ds_s", [128, 4, E], F32),
        ("gt1", [128, E], F32), ("gt2", [128, E], F32), ("gt3", [128, E], F32),
        ("wgt", [128, 4, RK], BF16), ("wem", [RK, 512], BF16))
    (f_x1,) = r1.seq(("f_x1", [128, D], F32))

    r2 = Region(16384)
    (val_bf,) = r2.seq(("val_bf", [128, HKV, TOK], BF16))
    (f_hdb,) = r2.seq(("f_hdb", [128, D], F32))

    r3 = Region(16384)
    (k_bf,) = r3.seq(("k_bf", [128, HKV, TOK], BF16))
    ksq, qsq, recipb = r3.seq(
        ("ksq", [128, S], BF16), ("qsq", [128, TOK], BF16),
        ("recipb", [128, TOK], F32))
    df_x1bfA, df_x1bfB = r3.seq(("df_x1bfA", [128, D], BF16),
                                ("df_x1bfB", [128, D], BF16))
    (hd7,) = r3.seq(("hd7", [128, NT, 512], F32))

    r4 = Region(16384)
    (v_fm,) = r4.seq(("v_fm", [128, HKV, TOK], BF16))
    (df_acc,) = r4.seq(("df_acc", [128, D], F32))
    r5 = Region(16384)
    wkv0, wkv1 = r5.seq(("wkv0", [128, NDC, DH], BF16),
                        ("wkv1", [128, NDC, DH], BF16))
    (df_xtm,) = r5.seq(("df_xtm", [128, D], F32))
    pt0, pt1 = r5.seq(("pt0", [128, TOK], BF16), ("pt1", [128, TOK], BF16))
    wpong, _ = alloc_at("wpong", [128, NDC, 512], BF16, r4.start)

    r6 = Region(65536)
    (xfm_bf,) = r6.seq(("xfm_bf", [128, NDC, TOK], BF16))
    kh0, kh1, vh0, vh1, qn1 = r6.seq(
        ("kh0", [128, S], BF16), ("kh1", [128, S], BF16),
        ("vh0", [128, NKC, DH], BF16), ("vh1", [128, NKC, DH], BF16),
        ("qn1", [1, TOK], F32))
    (x1_fm,) = r6.seq(("x1_fm", [128, NDC, TOK], BF16))

    r7 = Region(60416)
    wqs0, wqs1, wqs2, eff_bf, vtt = r7.seq(
        ("wqs0", [128, D], F32), ("wqs1", [128, D], F32),
        ("wqs2", [128, D], F32), ("eff_bf", [128, NDC, DH], BF16),
        ("vtt", [128, HKV, DH], BF16))
    wo0, wo1, strip0, strip1, strip2, strip3, wping = r7.seq(
        ("wo0", [128, HKV, 512], BF16), ("wo1", [128, HKV, 512], BF16),
        ("strip0", [128, 512], F32), ("strip1", [128, 512], F32),
        ("strip2", [128, 512], F32), ("strip3", [128, 512], F32),
        ("wping", [128, NDC, 512], BF16))
    a_bf, zp_bf, w8_bf, limt_bf, lim_bf, b8_bf, bmob0, bmob1, bob0, bob1 = \
        r7.seq(("a_bf", [128, NDC, RK], BF16), ("zp_bf", [RK, TOK], BF16),
               ("w8_bf", [128, NDC, E], BF16), ("limt_bf", [RK, E], BF16),
               ("lim_bf", [RK, RK], BF16), ("b8_bf", [128, E], BF16),
               ("bmob0", [RK, 512], BF16), ("bmob1", [RK, 512], BF16),
               ("bob0", [128, 512], BF16), ("bob1", [128, 512], BF16))

    assert cur[0] <= SB_END, f"SBUF over budget: {cur[0] - base} used"

    pa = nc.alloc_psum_tensor("pa", [128, 1024], F32)
    pb = nc.alloc_psum_tensor("pb", [128, 1024], F32)
    pv = nc.alloc_psum_tensor("pv", [128, 1024], F32)
    pc = nc.alloc_psum_tensor("pc", [128, 512], F32)
    pd = nc.alloc_psum_tensor("pd", [128, 512], F32)
    ptb = pc.ap().bitcast(BF16)

    slots = [pa.ap()[:, 0:512], pa.ap()[:, 512:1024],
             pb.ap()[:, 0:512], pb.ap()[:, 512:1024]]
    eslots = slots + [pv.ap()[:, 0:512], pv.ap()[:, 512:1024]]

    dG = Led(nc, "dG", dma=True)
    dS = Led(nc, "dS", dma=True)
    dA = Led(nc, "dA", dma=True)
    dQ = Led(nc, "dQ", dma=True)
    dW = [Led(nc, "dW0", dma=True), Led(nc, "dW1", dma=True)]
    dKK = [Led(nc, "dKK0", dma=True), Led(nc, "dKK1", dma=True)]
    dKV = [Led(nc, "dKV0", dma=True), Led(nc, "dKV1", dma=True)]
    dWO = [Led(nc, "dWO0", dma=True), Led(nc, "dWO1", dma=True)]
    dWP = [Led(nc, "dWP0", dma=True), Led(nc, "dWP1", dma=True)]
    dA2 = Led(nc, "dA2", dma=True)
    cc = Led(nc, "cc")
    tP = Led(nc, "tP")
    vP = Led(nc, "vP")
    sP = Led(nc, "sP")
    vg = VGuard(vP)

    with nc.Block() as blk:
        @blk.vector
        def _(v):
            v = vg.bind(v)
            v.memset(ones128.ap(), 1.0)
            v.memset(csrc.ap(), 0.0)
            v.memset(scsrc.ap(), 0.0)
            v.memset(nsc2.ap()[0:8, 6:7], 0.0)

        @blk.sync
        def _(sy):
            dS.inc(sy.dma_start(out=ident.ap(), in_=id_d))
        ident_dma = dS.n

        @blk.scalar
        def _(s):
            s.preload_act_table(AF.Exp)

        @blk.gpsimd
        def _(g):
            src = xfm_d.rearrange("(c p) t -> p c t", p=128)
            for i in range(4):
                dG.inc(g.dma_start(out=xfm_bf.ap()[:, 8 * i:8 * (i + 1), :],
                                   in_=src[:, 8 * i:8 * (i + 1), :]))
        xfm_done = dG.n

        slot_free = [0, 0, 0, 0]
        slot_seq = [0]

        def next_slot():
            i = slot_seq[0] % 4
            slot_seq[0] += 1
            return i

        stage_consumed = [0, 0]
        kevac = []
        vevac = []

        def proj_pass(which, f, src_d, out_sb):
            par = f % 2
            stg = wkv0 if par == 0 else wkv1

            @blk.gpsimd
            def _(g):
                if stage_consumed[par]:
                    tP.wait(g, at=stage_consumed[par])
                dW[par].inc(g.dma_start(
                    out=stg.ap()[:, :, :],
                    in_=src_d[f].rearrange("(dc p) c -> p dc c", p=128)))

            @blk.tensor
            def _(t):
                dW[par].wait(t)
                dG.wait(t, at=xfm_done)
                for tb in range(2):
                    si = next_slot()
                    if slot_free[si]:
                        vP.wait(t, at=slot_free[si])
                    for dc in range(NDC):
                        _tm(t, slots[si], stg.ap()[:, dc, :],
                                 xfm_bf.ap()[:, dc, 512 * tb:512 * (tb + 1)],
                                 start=(dc == 0), stop=(dc == NDC - 1))
                    tmark(t, tP)
            stage_consumed[par] = tP.n

            @blk.vector
            def _(v):
                v = vg.bind(v)
                for tb in range(2):
                    si = (slot_seq[0] - 2 + tb) % 4
                    tP.wait(v, at=tP.n - (1 - tb))
                    ins = v.tensor_copy(
                        out_sb.ap()[:, f, 512 * tb:512 * (tb + 1)], slots[si])
                    slot_free[si] = vP.n

        for f in range(HKV):
            proj_pass("k", f, wkf_d, k_bf)
            kevac.append(vP.n)

            @blk.sync
            def _(sy, f=f):
                vP.wait(sy, at=kevac[f])
                dS.inc(sy.dma_start(out=kown_d[f], in_=k_bf.ap()[:, f, :]))
        kown_done = dS.n

        @blk.gpsimd
        def _(g):
            dS.wait(g, at=kown_done)
            cc.inc(g.collective_compute(
                "AllGather", AL.bypass, replica_groups=NGROUPS,
                ins=[kown_d.ap().opt()], outs=[kall_d.ap().opt()]))
        cc_k = cc.n

        for f in range(HKV):
            proj_pass("v", f, wvf_d, v_fm)
            vevac.append(vP.n)

        vtt_ev = []
        vtt_store = []
        for tt in range(NT):
            @blk.tensor
            def _(t, tt=tt):
                if tt == 0:
                    vP.wait(t, at=vevac[-1])
                    dS.wait(t, at=kown_done)
                else:
                    vP.wait(t, at=vtt_ev[tt - 1])
                for h in range(HKV):
                    _tt(t, ptb[:, 128 * h:128 * (h + 1)],
                                v_fm.ap()[:, h, 128 * tt:128 * (tt + 1)],
                                ident.ap())
                tmark(t, tP)

            @blk.vector
            def _(v, tt=tt):
                v = vg.bind(v)
                tP.wait(v)
                if tt >= 1:
                    dS.wait(v, at=vtt_store[tt - 1])
                v.tensor_copy(
                    vtt.ap().rearrange("p h d -> p (h d)"), ptb[:, :])
                vtt_ev.append(vP.n)

            @blk.sync
            def _(sy, tt=tt):
                vP.wait(sy, at=vtt_ev[tt])
                vtt_store.append(dS.inc(sy.dma_start(
                    out=vown_d.ap()[:, 128 * tt:128 * (tt + 1), :]
                    .transpose([1, 0, 2]),
                    in_=vtt.ap()[:, :, :])))

        @blk.gpsimd
        def _(g):
            dS.wait(g, at=vtt_store[-1])
            cc.inc(g.collective_compute(
                "AllGather", AL.bypass, replica_groups=NGROUPS,
                ins=[vown_d.ap().opt()], outs=[vall_d.ap().opt()]))
        cc_v = cc.n

        qevac = []
        cast_done = [0]
        for kvh in range(HKV):
            @blk.scalar
            def _(s, kvh=kvh):
                if kvh > 0:
                    sP.wait(s, at=cast_done[0])
                for gi in range(3):
                    dQ.inc(s.dma_start(
                        out=[wqs0, wqs1, wqs2][gi].ap().rearrange(
                            "p (dc c) -> p dc c", c=128),
                        in_=wqg_d[kvh, gi].rearrange(
                            "(dc p) c -> p dc c", p=128)))
            w012 = dQ.n

            @blk.vector
            def _(v, kvh=kvh):
                v = vg.bind(v)
                dQ.wait(v, at=w012)
                v.tensor_add(wqs0.ap(), wqs0.ap(), wqs1.ap())
                v.tensor_add(wqs0.ap(), wqs0.ap(), wqs2.ap())
            add2 = vP.n

            @blk.scalar
            def _(s, kvh=kvh):
                vP.wait(s, at=add2)
                dQ.inc(s.dma_start(
                    out=wqs1.ap().rearrange("p (dc c) -> p dc c", c=128),
                    in_=wqg_d[kvh, 3].rearrange("(dc p) c -> p dc c", p=128)))
            w3 = dQ.n

            @blk.vector
            def _(v, kvh=kvh):
                v = vg.bind(v)
                dQ.wait(v, at=w3)
                v.tensor_add(wqs0.ap(), wqs0.ap(), wqs1.ap())
            add3 = vP.n

            @blk.scalar
            def _(s, kvh=kvh):
                vP.wait(s, at=add3)
                if kvh > 0:
                    tP.wait(s)
                cast_done[0] = sP.inc(s.copy(
                    eff_bf.ap().rearrange("p dc c -> p (dc c)"), wqs0.ap()))

            @blk.tensor
            def _(t, kvh=kvh):
                sP.wait(t, at=cast_done[0])
                for tb in range(2):
                    si = next_slot()
                    if slot_free[si]:
                        vP.wait(t, at=slot_free[si])
                    for dc in range(NDC):
                        _tm(t, slots[si], eff_bf.ap()[:, dc, :],
                                 xfm_bf.ap()[:, dc, 512 * tb:512 * (tb + 1)],
                                 start=(dc == 0), stop=(dc == NDC - 1))
                    tmark(t, tP)

            @blk.vector
            def _(v, kvh=kvh):
                v = vg.bind(v)
                for tb in range(2):
                    si = (slot_seq[0] - 2 + tb) % 4
                    tP.wait(v, at=tP.n - (1 - tb))
                    ins = v.tensor_copy(
                        q_bf.ap()[:, kvh, 512 * tb:512 * (tb + 1)], slots[si])
                    slot_free[si] = vP.n
            qevac.append(vP.n)
        q_mm_last = tP.n
        wq_cast_last = cast_done[0]

        @blk.gpsimd
        def _(g):
            sP.wait(g, at=wq_cast_last)
            tP.wait(g, at=q_mm_last)
            dWO[0].inc(g.dma_start(
                out=wo0.ap()[:, :, :],
                in_=wo_d[:, 0:512].rearrange("(kc p) o -> p kc o", p=128)))

        headlast = {}
        khdma = []
        vhdma = []
        for h in range(HKV):
            par = h % 2
            khb = kh0 if par == 0 else kh1
            vhb = vh0 if par == 0 else vh1

            @blk.sync
            def _(sy, h=h, par=par, khb=khb, vhb=vhb):
                cc.wait(sy, at=cc_v)
                if h < 2:
                    tP.wait(sy, at=q_mm_last)
                else:
                    tP.wait(sy, at=headlast[h - 2])
                for sh in range(4):
                    dKK[par].inc(sy.dma_start(
                        out=khb.ap()[:, TOK * sh:TOK * (sh + 1)],
                        in_=kall_d[sh * HKV + h]))
                khdma.append(dKK[par].n)
                for sh in range(4):
                    dKV[par].inc(sy.dma_start(
                        out=vhb.ap()[:, 8 * sh:8 * (sh + 1), :],
                        in_=vall_d[sh * HKV + h].rearrange(
                            "(kc p) dh -> p kc dh", p=128)))
                vhdma.append(dKV[par].n)

            @blk.scalar
            def _(s, h=h):
                vP.wait(s, at=qevac[h])
                sP.inc(s.activation(qsq.ap(), q_bf.ap()[:, h, :], AF.Square))
            qsq_done = sP.n

            @blk.tensor
            def _(t, h=h):
                sP.wait(t, at=qsq_done)
                if h > 0:
                    vP.wait(t, at=pcpd_free[h - 1])
                _tm(t, pc.ap()[:, :], ones128.ap(), qsq.ap()[:, 0:512],
                    start=True, stop=True)
                _tm(t, pd.ap()[:, :], ones128.ap(), qsq.ap()[:, 512:1024],
                    start=True, stop=True)
                tmark(t, tP)

            if h == 0:
                pcpd_free = {}

            @blk.vector
            def _(v, h=h):
                v = vg.bind(v)
                tP.wait(v)
                v.tensor_reduce(nsc.ap()[0:1, 0:1], pc.ap()[0:1, :], AX.X,
                                AL.max)
                v.tensor_reduce(nsc.ap()[0:1, 1:2], pd.ap()[0:1, :], AX.X,
                                AL.max)
                v.tensor_max(nsc.ap()[0:1, 0:1], nsc.ap()[0:1, 0:1],
                                    nsc.ap()[0:1, 1:2])
            qn_red = vP.n

            @blk.scalar
            def _(s, h=h, par=par, khb=khb):
                dKK[par].wait(s, at=khdma[h])
                sP.inc(s.activation(ksq.ap(), khb.ap()[:, :], AF.Square))
            ksq_done = sP.n

            kn_red = []
            for piece in range(8):
                @blk.tensor
                def _(t, h=h, piece=piece):
                    if piece == 0:
                        sP.wait(t, at=ksq_done)
                        vP.wait(t, at=qn_red)
                    if piece >= 2:
                        vP.wait(t, at=kn_red[piece - 2])
                    tgt = pc if piece % 2 == 0 else pd
                    _tm(t, tgt.ap()[:, :], ones128.ap(),
                        ksq.ap()[:, 512 * piece:512 * (piece + 1)],
                        start=True, stop=True)
                    tmark(t, tP)

                @blk.vector
                def _(v, h=h, piece=piece):
                    v = vg.bind(v)
                    tP.wait(v)
                    tgt = pc if piece % 2 == 0 else pd
                    v.tensor_reduce(nsc.ap()[0:1, 2:3], tgt.ap()[0:1, :],
                                    AX.X, AL.max)
                    if piece == 0:
                        ins = v.tensor_copy(nsc.ap()[0:1, 3:4],
                                            nsc.ap()[0:1, 2:3])
                    else:
                        ins = v.tensor_max(nsc.ap()[0:1, 3:4],
                                           nsc.ap()[0:1, 3:4],
                                           nsc.ap()[0:1, 2:3])
                    kn_red.append(vP.n)

            @blk.vector
            def _(v, h=h):
                v = vg.bind(v)
                v.tensor_add(nsc.ap()[0:1, 4:5], nsc.ap()[0:1, 0:1],
                             nsc.ap()[0:1, 3:4])
                v.tensor_scalar(csrc.ap()[0:1, 0:1],
                                       nsc.ap()[0:1, 4:5],
                                       1.02 * 0.5 * INV_SQRT_DH, None,
                                       AL.mult)

            @blk.tensor
            def _(t, h=h):
                vP.wait(t)
                _tm(t, pc.ap()[:, 0:2], ones128.ap(), csrc.ap(),
                    start=True, stop=True)
                tmark(t, tP)

            @blk.vector
            def _(v, h=h):
                v = vg.bind(v)
                tP.wait(v)
                v.tensor_scalar(cbias.ap(), pc.ap()[:, 0:1], -1.0,
                                       None, AL.mult)
            cb_ready = vP.n

            exp_done = {}
            pt_used = {}
            for kc in range(NKC):
                sb = pa if kc % 2 == 0 else pb
                ptbuf = pt0 if kc % 2 == 0 else pt1

                @blk.tensor
                def _(t, h=h, kc=kc, sb=sb, khb=khb, par=par):
                    if kc < 2:
                        vP.wait(t, at=cb_ready)
                        dKK[par].wait(t, at=khdma[h])
                    else:
                        sP.wait(t, at=exp_done[kc - 2])
                    for half in range(2):
                        _tm(t, sb.ap()[:, 512 * half:512 * (half + 1)],
                                 khb.ap()[:, 128 * kc:128 * (kc + 1)],
                                 q_bf.ap()[:, h, 512 * half:512 * (half + 1)],
                                 start=True, stop=True,
                                 skip_group_check=True)
                    tmark(t, tP)

                @blk.scalar
                def _(s, h=h, kc=kc, sb=sb, ptbuf=ptbuf):
                    tP.wait(s)
                    if kc >= 2:
                        tP.wait(s, at=pt_used[kc - 2])
                    exp_done[kc] = sP.inc(s.activation(
                        ptbuf.ap(), sb.ap()[:, :], AF.Exp,
                        bias=cbias.ap(), scale=INV_SQRT_DH))

                @blk.tensor
                def _(t, h=h, kc=kc, ptbuf=ptbuf, vhb=vhb, par=par):
                    sP.wait(t, at=exp_done[kc])
                    if kc == 0:
                        dKV[par].wait(t, at=vhdma[h])
                    _tm(t, pc.ap()[:, :], ones128.ap(),
                        ptbuf.ap()[:, 0:512],
                        start=(kc == 0), stop=(kc == NKC - 1),
                        skip_group_check=True)
                    _tm(t, pd.ap()[:, :], ones128.ap(),
                        ptbuf.ap()[:, 512:1024],
                        start=(kc == 0), stop=(kc == NKC - 1),
                        skip_group_check=True)
                    for half in range(2):
                        _tm(t, pv.ap()[:, 512 * half:512 * (half + 1)],
                                 vhb.ap()[:, kc, :],
                                 ptbuf.ap()[:, 512 * half:512 * (half + 1)],
                                 start=(kc == 0), stop=(kc == NKC - 1),
                                 skip_group_check=True)
                    pt_used[kc] = tmark(t, tP)

            @blk.vector
            def _(v, h=h):
                v = vg.bind(v)
                tP.wait(v)
                v.reciprocal(recipb.ap()[:, 0:512], pc.ap()[:, :])
                v.reciprocal(recipb.ap()[:, 512:1024], pd.ap()[:, :])
                v.tensor_mul(val_bf.ap()[:, h, :], pv.ap()[:, :],
                                    recipb.ap())
            pcpd_free[h] = vP.n
            headlast[h] = tP.n
        attn_done = vP.n

        for _i in range(4):
            slot_free[_i] = attn_done
        strips = [strip0, strip1, strip2, strip3]
        d_mm_done = {}
        d7_evac = {}
        half_spill_total = {}
        x1_spill = []
        df_consumed = {}
        df_norm = {}

        def emit_d_half(hf):
            dAh = dA if hf == 0 else dA2
            tiles = range(4 * hf, 4 * hf + 4)
            for ob in range(NOB):
                k = hf * NOB + ob
                par = k % 2
                wob = wo0 if par == 0 else wo1
                if k + 1 < 2 * NOB:
                    @blk.gpsimd
                    def _(g, k=k):
                        if k >= 1:
                            tP.wait(g, at=d_mm_done[k - 1])
                        dWO[(k + 1) % 2].inc(g.dma_start(
                            out=(wo1 if k % 2 == 0 else wo0).ap()[:, :, :],
                            in_=wo_d[:, 512 * ((k + 1) % NOB):
                                     512 * ((k + 1) % NOB + 1)].rearrange(
                                "(kc p) o -> p kc o", p=128)))

                for tt in tiles:
                    @blk.tensor
                    def _(t, ob=ob, tt=tt, wob=wob, par=par, k=k):
                        if tt == 4 * hf:
                            dWO[par].wait(t)
                            if k == 0:
                                vP.wait(t, at=attn_done)
                        si = next_slot()
                        if slot_free[si]:
                            vP.wait(t, at=slot_free[si])
                        for kc in range(HKV):
                            _tm(t, slots[si],
                                val_bf.ap()[:, kc, 128 * tt:128 * (tt + 1)],
                                wob.ap()[:, kc, :],
                                start=(kc == 0), stop=(kc == HKV - 1))
                        tmark(t, tP)

                    if ob < 7:
                        @blk.vector
                        def _(v, ob=ob, tt=tt, dAh=dAh):
                            v = vg.bind(v)
                            si = (slot_seq[0] - 1) % 4
                            sbi = (ob * 4 + (tt % 4)) % 4
                            tP.wait(v)
                            dAh.wait(v)
                            v.tensor_copy(strips[sbi].ap(), slots[si])
                            slot_free[si] = vP.n

                        @blk.scalar
                        def _(s, ob=ob, tt=tt, dAh=dAh):
                            sbi = (ob * 4 + (tt % 4)) % 4
                            vP.wait(s)
                            dAh.inc(s.dma_start(
                                out=attn_d[128 * tt:128 * (tt + 1),
                                           512 * ob:512 * (ob + 1)],
                                in_=strips[sbi].ap()))
                    else:
                        @blk.vector
                        def _(v, ob=ob, tt=tt):
                            v = vg.bind(v)
                            si = (slot_seq[0] - 1) % 4
                            tP.wait(v)
                            v.tensor_copy(att7.ap()[:, tt, :], slots[si])
                            slot_free[si] = vP.n
                            d7_evac[tt] = vP.n
                d_mm_done[k] = tP.n
            half_spill_total[hf] = dAh.n

        def emit_df_chain(tt):
            hf = tt // 4
            dAh = dA if hf == 0 else dA2
            lns = lnsum.ap()[:, 2 * (tt % 8):2 * (tt % 8) + 2]
            lnmt = lnm.ap()[:, 4 * (tt % 8):4 * (tt % 8) + 4]
            scr = nsc2.ap()[:, 0:1] if False else None
            x1bf = df_x1bfA if tt % 2 == 0 else df_x1bfB

            @blk.sync
            def _(sy, tt=tt, dAh=dAh):
                dAh.wait(sy, at=half_spill_total[hf])
                vP.wait(sy, at=d7_evac[tt])
                if tt >= 1:
                    sP.wait(sy, at=df_consumed[tt - 1])
                    dS.wait(sy, at=x1_spill[-1])
                dS.inc(sy.dma_start(
                    out=df_acc.ap()[:, 0:3584],
                    in_=attn_d[128 * tt:128 * (tt + 1), :]))
                dS.inc(sy.dma_start(out=df_xtm.ap(),
                                    in_=xtm_d[128 * tt:128 * (tt + 1), :]))
            df_ld = dS.n

            @blk.vector
            def _(v, tt=tt, df_ld=df_ld):
                v = vg.bind(v)
                dS.wait(v, at=df_ld)
                v.tensor_add(df_acc.ap()[:, 0:3584], df_acc.ap()[:, 0:3584],
                             df_xtm.ap()[:, 0:3584])
                v.tensor_add(df_acc.ap()[:, 3584:4096],
                             att7.ap()[:, tt, :], df_xtm.ap()[:, 3584:4096])

            @blk.scalar
            def _(s, tt=tt, lns=lns):
                vP.wait(s)
                if tt == 0:
                    s.preload_act_table(AF.Sqrt)
                s.activation(df_xtm.ap(), df_acc.ap(), AF.Copy,
                             accum_out=lns[:, 0:1])
                sP.inc(s.activation(df_xtm.ap(), df_acc.ap(), AF.Square,
                                    accum_out=lns[:, 1:2]))

            @blk.vector
            def _(v, tt=tt, lns=lns, lnmt=lnmt):
                v = vg.bind(v)
                sP.wait(v)
                v.tensor_scalar(lnmt[:, 0:1], lns[:, 0:1], 1.0 / D, None,
                                AL.mult)
                v.tensor_scalar(lnmt[:, 1:2], lns[:, 1:2], 1.0 / D, None,
                                AL.mult)
                v.tensor_mul(lnmt[:, 2:3], lnmt[:, 0:1], lnmt[:, 0:1])
                v.tensor_sub(lnmt[:, 1:2], lnmt[:, 1:2], lnmt[:, 2:3])
                v.tensor_scalar(lnmt[:, 1:2], lnmt[:, 1:2], 1.0, LN_EPS,
                                AL.mult, AL.add)

            @blk.scalar
            def _(s, tt=tt, lnmt=lnmt):
                vP.wait(s)
                sP.inc(s.activation(lnmt[:, 2:3], lnmt[:, 1:2], AF.Sqrt))

            @blk.vector
            def _(v, tt=tt, lnmt=lnmt):
                v = vg.bind(v)
                sP.wait(v)
                v.reciprocal(lnmt[:, 2:3], lnmt[:, 2:3])
                v.tensor_mul(lnmt[:, 3:4], lnmt[:, 0:1], lnmt[:, 2:3])
                v.tensor_scalar(lnmt[:, 3:4], lnmt[:, 3:4], -1.0, None,
                                AL.mult)

            @blk.vector
            def _(v, tt=tt, lnmt=lnmt):
                v = vg.bind(v)
                if tt >= 1:
                    dS.wait(v, at=x1_spill[-1])
                v.tensor_scalar(df_xtm.ap(), df_acc.ap(), lnmt[:, 2:3],
                                lnmt[:, 3:4], AL.mult, AL.add)
                df_norm[tt] = vP.n

            @blk.scalar
            def _(s, tt=tt, x1bf=x1bf):
                vP.wait(s, at=df_norm[tt])
                if tt >= 2:
                    tP.wait(s, at=x1t_done[tt - 2])
                df_consumed[tt] = sP.inc(s.copy(x1bf.ap(), df_xtm.ap()))

            @blk.sync
            def _(sy, tt=tt):
                sP.wait(sy, at=df_consumed[tt])
                x1_spill.append(dS.inc(sy.dma_start(
                    out=x1sp_d[128 * tt:128 * (tt + 1), :],
                    in_=df_xtm.ap())))

        x1t_done = {}

        def emit_x1fm_transpose(tt):
            x1bf = df_x1bfA if tt % 2 == 0 else df_x1bfB
            ptv = pc.ap().bitcast(BF16).rearrange("p (j c) -> p j c", c=128)
            for batch in range(4):
                @blk.tensor
                def _(t, tt=tt, batch=batch, x1bf=x1bf):
                    if batch == 0:
                        sP.wait(t, at=df_consumed[tt])
                    if (tt, batch) != (0, 0):
                        vP.wait(t, at=x1t_evac[-1])
                    for j in range(8):
                        dc = 8 * batch + j
                        _tt(t, ptb[0:128, 128 * j:128 * (j + 1)],
                            x1bf.ap()[:, 128 * dc:128 * (dc + 1)],
                            ident.ap())
                    tmark(t, tP)

                @blk.vector
                def _(v, tt=tt, batch=batch):
                    v = vg.bind(v)
                    tP.wait(v)
                    v.tensor_copy(
                        x1_fm.ap()[:, 8 * batch:8 * (batch + 1),
                                   128 * tt:128 * (tt + 1)], ptv)
                    x1t_evac.append(vP.n)
            x1t_done[tt] = tP.n

        x1t_evac = []
        emit_d_half(0)
        emit_df_chain(0)
        emit_df_chain(1)
        emit_x1fm_transpose(0)
        emit_x1fm_transpose(1)
        emit_df_chain(2)
        emit_df_chain(3)
        emit_d_half(1)
        emit_x1fm_transpose(2)
        emit_x1fm_transpose(3)
        for tt in range(4, NT):
            emit_df_chain(tt)
            emit_x1fm_transpose(tt)
        x1fm_done = vP.n
        d_t_last = tP.n
        d_spill_all = dA.n
        d2_spill_all = dA2.n

        @blk.vector
        def _(v):
            v = vg.bind(v)
            tP.wait(v, at=d_t_last)
            v.memset(b8_bf.ap(), 0.0)
            v.memset(bob0.ap(), 0.0)
            v.memset(bob1.ap(), 0.0)
            v.memset(lim_bf.ap(), 0.0)
            v.memset(wgt.ap().rearrange("p a r -> p (a r)"), 0.0)
        e_zero = vP.n

        @blk.gpsimd
        def _(g):
            tP.wait(g, at=d_t_last)
            vP.wait(g, at=e_zero)
            dG.inc(g.dma_start(out=a_bf.ap()[:, :, :],
                               in_=a_d.rearrange("(dc p) r -> p dc r", p=128)))
            dG.inc(g.dma_start(out=w8_bf.ap()[:, :, :],
                               in_=w8_d.rearrange("(dc p) e -> p dc e", p=128)))
            dG.inc(g.dma_start(out=limt_bf.ap(), in_=limt_d))
            dG.inc(g.dma_start(out=lim_bf.ap()[0:E, :], in_=lim_d))
            dG.inc(g.dma_start(out=b8_bf.ap()[0:1, :], in_=b_d[:, 0:E]))
            dWP[0].inc(g.dma_start(
                out=wping.ap()[:, :, :],
                in_=wt_d[0].rearrange("(dc p) o -> p dc o", p=128)))
            dWP[0].inc(g.dma_start(out=bmob0.ap(), in_=bm_d[:, 0:512]))
            dWP[0].inc(g.dma_start(out=bob0.ap()[0:1, :], in_=b_d[:, 0:512]))
        e_small = dG.n

        @blk.tensor
        def _(t):
            vP.wait(t, at=x1fm_done)
            dG.wait(t, at=e_small)
            for half in range(2):
                for dc in range(NDC):
                    _tm(t, pv.ap()[0:RK, 512 * half:512 * (half + 1)],
                             a_bf.ap()[:, dc, :],
                             x1_fm.ap()[:, dc, 512 * half:512 * (half + 1)],
                             start=(dc == 0), stop=(dc == NDC - 1))
            tmark(t, tP)

        @blk.vector
        def _(v):
            v = vg.bind(v)
            tP.wait(v)
            v.tensor_copy(z_s.ap(), pv.ap()[0:RK, :])
            v.tensor_copy(z_bf.ap(), z_s.ap())
        z_ready = vP.n

        @blk.tensor
        def _(t):
            vP.wait(t, at=z_ready)
            x1r = x1_fm.ap().rearrange("p dc (at w) -> p dc at w", w=256)
            zr = z_bf.ap().rearrange("r (at w) -> r at w", w=256)
            for at_ in range(4):
                for dc in range(NDC):
                    _tm(t, pc.ap()[:, 8 * at_:8 * (at_ + 1)],
                        x1r[:, dc, at_, 1::2], w8_bf.ap()[:, dc, :],
                        start=(dc == 0), stop=False, skip_group_check=True)
                _tm(t, pc.ap()[:, 8 * at_:8 * (at_ + 1)],
                    ones128.ap(), b8_bf.ap(), start=False, stop=True,
                    skip_group_check=True)
                _tm(t, pd.ap()[:, 8 * at_:8 * (at_ + 1)],
                         zr[:, at_, 1::2], limt_bf.ap(),
                         start=True, stop=True)
            tmark(t, tP)

        @blk.vector
        def _(v):
            v = vg.bind(v)
            tP.wait(v)
            v.tensor_copy(h8_s.ap().rearrange("p a e -> p (a e)"),
                          pc.ap()[:, 0:32])
            v.tensor_copy(ds_s.ap().rearrange("p a e -> p (a e)"),
                          pd.ap()[:, 0:32])
            v.tensor_reduce(nsc2.ap()[:, 4:5],
                            h8_s.ap().rearrange("p a e -> p (a e)"),
                            AX.X, AL.max, apply_absolute_value=True)
            v.tensor_reduce(nsc2.ap()[:, 5:6],
                                   ds_s.ap().rearrange("p a e -> p (a e)"),
                                   AX.X, AL.max, apply_absolute_value=True)

        @blk.sync
        def _(sy):
            vP.wait(sy)
            dS.inc(sy.dma_start(out=scpr_d[:, :], in_=nsc2.ap()[:, 4:6]))
            dS.wait(sy)
            dS.inc(sy.dma_start(out=mx2.ap(),
                                in_=scpr_d.ap().transpose([1, 0]),
                                allow_slow_non_contiguous=True))

        @blk.vector
        def _(v):
            v = vg.bind(v)
            dS.wait(v)
            v.tensor_reduce(nsc2.ap()[0:2, 6:7], mx2.ap(), AX.X,
                                   AL.max)

        @blk.sync
        def _(sy):
            vP.wait(sy)
            dS.inc(sy.dma_start(out=scsrc_d.ap(), in_=nsc2.ap()[0:8, 6:7]))

        @blk.gpsimd
        def _(g):
            dS.wait(g)
            cc.inc(g.collective_compute(
                "AllReduce", AL.max, replica_groups=ALLG,
                ins=[scsrc_d.ap().opt()], outs=[scdst_d.ap().opt()]))

        @blk.gpsimd
        def _(g):
            cc.wait(g)
            dG.inc(g.dma_start(
                out=scsrc.ap()[0:1, :],
                in_=scdst_d.ap()[0:2].rearrange("(a b) -> a b", a=1)))

        @blk.tensor
        def _(t):
            dG.wait(t)
            _tm(t, pc.ap()[:, 0:2], ones128.ap(), scsrc.ap(),
                start=True, stop=True)
            tmark(t, tP)

        @blk.vector
        def _(v):
            v = vg.bind(v)
            tP.wait(v)
            v.tensor_scalar_max(recips.ap(), pc.ap()[:, 0:2], SC_EPS)
            v.reciprocal(recips.ap(), recips.ap())

        wt_done = []
        for at_ in range(4):
            @blk.vector
            def _(v, at_=at_):
                v = vg.bind(v)
                v.tensor_scalar(gt1.ap(), h8_s.ap()[:, at_, :],
                                recips.ap()[:, 0:1], 0.5, AL.mult, AL.mult)
                v.tensor_scalar(gt2.ap(), ds_s.ap()[:, at_, :],
                                recips.ap()[:, 1:2], 0.5, AL.mult, AL.mult)
                v.tensor_add(gt1.ap(), gt1.ap(), gt2.ap())
                v.tensor_reduce(nsc2.ap()[:, 7:8], gt1.ap(), AX.X, AL.max)
                v.tensor_scalar(nsc2.ap()[:, 8:9], nsc2.ap()[:, 7:8],
                                       -1.0, None, AL.mult)

            @blk.scalar
            def _(s, at_=at_):
                vP.wait(s)
                if at_ == 0:
                    s.preload_act_table(AF.Exp)
                sP.inc(s.activation(gt2.ap(), gt1.ap(), AF.Exp,
                                    bias=nsc2.ap()[:, 8:9]))

            @blk.vector
            def _(v, at_=at_):
                v = vg.bind(v)
                sP.wait(v)
                v.tensor_reduce(nsc2.ap()[:, 9:10], gt2.ap(), AX.X, AL.add)
                v.reciprocal(nsc2.ap()[:, 10:11], nsc2.ap()[:, 9:10])
                v.tensor_scalar(gt1.ap(), gt2.ap(), nsc2.ap()[:, 10:11], None,
                                AL.mult)
                v.max(out=gt3.ap(), in_=gt1.ap())
                v.tensor_scalar(nsc2.ap()[:, 11:12], gt3.ap()[:, 1:2],
                                       -1.0, None, AL.mult)

            @blk.scalar
            def _(s, at_=at_):
                vP.wait(s)
                sP.inc(s.activation(gt2.ap(), gt1.ap(), AF.Tanh,
                                    bias=nsc2.ap()[:, 11:12], scale=1.0))

            @blk.vector
            def _(v, at_=at_):
                v = vg.bind(v)
                sP.wait(v)
                v.tensor_scalar(gt2.ap(), gt2.ap(), 0.5, 0.5, AL.mult, AL.add)
                v.tensor_mul(gt1.ap(), gt1.ap(), gt2.ap())
                v.tensor_reduce(nsc2.ap()[:, 12:13], gt1.ap(), AX.X, AL.add)
                v.tensor_scalar(nsc2.ap()[:, 12:13], nsc2.ap()[:, 12:13], 1.0,
                                1e-9, AL.mult, AL.add)
                v.reciprocal(nsc2.ap()[:, 13:14], nsc2.ap()[:, 12:13])
                v.tensor_scalar(
                    wgt.ap()[:, at_, 0:E], gt1.ap(), nsc2.ap()[:, 13:14], None,
                    AL.mult)
                wt_done.append(vP.n)

        @blk.tensor
        def _(t):
            vP.wait(t, at=wt_done[-1])
            for at_ in range(4):
                _tt(t, ptb[0:RK, 128 * at_:128 * (at_ + 1)],
                    wgt.ap()[:, at_, :], ident.ap())
            tmark(t, tP)

        @blk.vector
        def _(v):
            v = vg.bind(v)
            tP.wait(v)
            v.tensor_copy(wem.ap(), ptb[0:RK, 0:512])

        @blk.tensor
        def _(t):
            vP.wait(t)
            _tm(t, pd.ap()[0:RK, :], lim_bf.ap(), wem.ap(),
                     start=True, stop=True)
            tmark(t, tP)

        @blk.vector
        def _(v):
            v = vg.bind(v)
            tP.wait(v)
            v.tensor_copy(pmix_s.ap(), pd.ap()[0:RK, :])
            v.tensor_copy(pmix_tok.ap()[:, 0::2], pmix_s.ap())
            v.tensor_copy(pmix_tok.ap()[:, 1::2], pmix_s.ap())
            v.scalar_tensor_tensor(zp_bf.ap(), z_s.ap(), 0.5,
                                          pmix_tok.ap(), AL.mult, AL.mult)
        zp_ready = vP.n

        e_spill = []
        e_slot_free = [z_ready] * 6
        e_seq = [0]
        strip_rd2 = [d_spill_all] * 4
        e_mm_done = {}
        e7_evac = []
        for ob in range(NOB):
            par = ob % 2
            wbuf = wping if par == 0 else wpong
            bmob = bmob0 if par == 0 else bmob1
            bob = bob0 if par == 0 else bob1

            if ob + 1 < NOB:
                @blk.gpsimd
                def _(g, ob=ob):
                    if ob == 0:
                        sP.wait(g, at=df_consumed[NT - 1])
                        dS.wait(g, at=x1_spill[-1])
                    if ob >= 1:
                        tP.wait(g, at=e_mm_done[ob - 1])
                    np_ = (ob + 1) % 2
                    dWP[np_].inc(g.dma_start(
                        out=(wpong if ob % 2 == 0 else wping).ap()[:, :, :],
                        in_=wt_d[ob + 1].rearrange("(dc p) o -> p dc o",
                                                   p=128)))
                    dWP[np_].inc(g.dma_start(
                        out=(bmob1 if ob % 2 == 0 else bmob0).ap(),
                        in_=bm_d[:, 512 * (ob + 1):512 * (ob + 2)]))
                    dWP[np_].inc(g.dma_start(
                        out=(bob1 if ob % 2 == 0 else bob0).ap()[0:1, :],
                        in_=b_d[:, 512 * (ob + 1):512 * (ob + 2)]))

            for tt in range(NT):
                @blk.tensor
                def _(t, ob=ob, tt=tt, wbuf=wbuf, bmob=bmob, bob=bob,
                      par=par):
                    if tt == 0:
                        dWP[par].wait(t)
                        if ob == 0:
                            vP.wait(t, at=x1fm_done)
                    si = e_seq[0] % 6
                    e_seq[0] += 1
                    if e_slot_free[si]:
                        vP.wait(t, at=e_slot_free[si])
                    for dc in range(NDC):
                        _tm(t, eslots[si],
                                 x1_fm.ap()[:, dc, 128 * tt:128 * (tt + 1)],
                                 wbuf.ap()[:, dc, :],
                                 start=(dc == 0), stop=False)
                    _tm(t, eslots[si], ones128.ap(), bob.ap(),
                        start=False, stop=False, skip_group_check=True)
                    if ob == 0 and tt == 0:
                        vP.wait(t, at=zp_ready)
                    _tm(t, eslots[si],
                             zp_bf.ap()[:, 128 * tt:128 * (tt + 1)],
                             bmob.ap(), start=False, stop=True)
                    tmark(t, tP)

                if ob < 7:
                    @blk.vector
                    def _(v, ob=ob, tt=tt):
                        v = vg.bind(v)
                        si = (e_seq[0] - 1) % 6
                        sbi = (ob * NT + tt) % 4
                        tP.wait(v)
                        dA.wait(v)
                        dA2.wait(v, at=d2_spill_all)
                        v.tensor_copy(strips[sbi].ap(), eslots[si])
                        e_slot_free[si] = vP.n

                    @blk.scalar
                    def _(s, ob=ob, tt=tt):
                        sbi = (ob * NT + tt) % 4
                        vP.wait(s)
                        ins = s.dma_start(
                            out=hdb_d[128 * tt:128 * (tt + 1),
                                      512 * ob:512 * (ob + 1)],
                            in_=strips[sbi].ap())
                        strip_rd2[sbi] = dA.inc(ins)
                        e_spill.append(dA.n)
                else:
                    @blk.vector
                    def _(v, ob=ob, tt=tt):
                        v = vg.bind(v)
                        si = (e_seq[0] - 1) % 6
                        tP.wait(v)
                        v.tensor_copy(hd7.ap()[:, tt, :], eslots[si])
                        e_slot_free[si] = vP.n
                        e7_evac.append(vP.n)
            e_mm_done[ob] = tP.n
        e_spill_all = dA.n

        y_out = []
        f_consumed = {}
        for tt in range(NT):
            @blk.sync
            def _(sy, tt=tt):
                dA.wait(sy, at=e_spill_all)
                vP.wait(sy, at=e7_evac[tt])
                if tt >= 1:
                    vP.wait(sy, at=f_consumed[tt - 1])
                    dA.wait(sy, at=y_out[tt - 1])
                dS.inc(sy.dma_start(out=f_x1.ap(),
                                    in_=x1sp_d[128 * tt:128 * (tt + 1), :]))
                dS.inc(sy.dma_start(out=f_hdb.ap()[:, 0:3584],
                                    in_=hdb_d[128 * tt:128 * (tt + 1), :]))
            f_ld = dS.n

            @blk.vector
            def _(v, tt=tt):
                v = vg.bind(v)
                dS.wait(v, at=f_ld)
                v.tensor_add(f_hdb.ap()[:, 0:3584], f_hdb.ap()[:, 0:3584],
                             f_x1.ap()[:, 0:3584])
                v.tensor_add(f_hdb.ap()[:, 3584:4096],
                                    hd7.ap()[:, tt, :],
                                    f_x1.ap()[:, 3584:4096])

            @blk.scalar
            def _(s, tt=tt):
                vP.wait(s)
                if tt == 0:
                    s.preload_act_table(AF.Sqrt)
                s.activation(f_x1.ap(), f_hdb.ap(), AF.Copy,
                             accum_out=lnsum.ap()[:, 0:1])
                sP.inc(s.activation(f_x1.ap(), f_hdb.ap(), AF.Square,
                                    accum_out=lnsum.ap()[:, 1:2]))

            @blk.vector
            def _(v, tt=tt):
                v = vg.bind(v)
                sP.wait(v)
                v.tensor_scalar(lnm.ap()[:, 0:1], lnsum.ap()[:, 0:1], 1.0 / D,
                                None, AL.mult)
                v.tensor_scalar(lnm.ap()[:, 1:2], lnsum.ap()[:, 1:2], 1.0 / D,
                                None, AL.mult)
                v.tensor_mul(nsc2.ap()[:, 0:1], lnm.ap()[:, 0:1],
                             lnm.ap()[:, 0:1])
                v.tensor_sub(lnm.ap()[:, 1:2], lnm.ap()[:, 1:2],
                             nsc2.ap()[:, 0:1])
                v.tensor_scalar(lnm.ap()[:, 1:2], lnm.ap()[:, 1:2],
                                       1.0, LN_EPS, AL.mult, AL.add)

            @blk.scalar
            def _(s, tt=tt):
                vP.wait(s)
                sP.inc(s.activation(nsc2.ap()[:, 1:2], lnm.ap()[:, 1:2],
                                    AF.Sqrt))

            @blk.vector
            def _(v, tt=tt):
                v = vg.bind(v)
                sP.wait(v)
                v.reciprocal(lnm.ap()[:, 2:3], nsc2.ap()[:, 1:2])
                v.tensor_mul(nsc2.ap()[:, 2:3], lnm.ap()[:, 1:2],
                             lnm.ap()[:, 2:3])
                v.tensor_mul(nsc2.ap()[:, 2:3], nsc2.ap()[:, 2:3],
                             lnm.ap()[:, 2:3])
                v.tensor_scalar(nsc2.ap()[:, 2:3], nsc2.ap()[:, 2:3], -0.5,
                                1.5, AL.mult, AL.add)
                v.tensor_mul(lnm.ap()[:, 2:3], lnm.ap()[:, 2:3],
                             nsc2.ap()[:, 2:3])
                v.tensor_mul(lnm.ap()[:, 3:4], lnm.ap()[:, 0:1],
                             lnm.ap()[:, 2:3])
                v.tensor_scalar(lnm.ap()[:, 3:4], lnm.ap()[:, 3:4],
                                       -1.0, None, AL.mult)

            @blk.vector
            def _(v, tt=tt):
                v = vg.bind(v)
                v.tensor_scalar(
                    f_x1.ap(), f_hdb.ap(), lnm.ap()[:, 2:3], lnm.ap()[:, 3:4],
                    AL.mult, AL.add)
                f_consumed[tt] = vP.n

            @blk.scalar
            def _(s, tt=tt):
                vP.wait(s, at=f_consumed[tt])
                y_out.append(dA.inc(s.dma_start(
                    out=y_d[128 * tt:128 * (tt + 1), :], in_=f_x1.ap())))

        @blk.sync
        def _(sy):
            dA.wait(sy, at=y_out[-1])

        if taps:
            tp_kall = nc.dram_tensor("tp_kall", [4 * HKV, DH, TOK], BF16,
                                     kind="ExternalOutput").ap()
            tp_vall = nc.dram_tensor("tp_vall", [4 * HKV, TOK, DH], BF16,
                                     kind="ExternalOutput").ap()
            tp_att = nc.dram_tensor("tp_att", [TOK, 7 * 512], F32,
                                    kind="ExternalOutput").ap()
            tp_x1 = nc.dram_tensor("tp_x1", [TOK, D], F32,
                                   kind="ExternalOutput").ap()
            tp_hdb = nc.dram_tensor("tp_hdb", [TOK, 7 * 512], F32,
                                    kind="ExternalOutput").ap()
            tp_sc = nc.dram_tensor("tp_sc", [128, 2], F32,
                                   kind="ExternalOutput").ap()
            tp_zp = nc.dram_tensor("tp_zp", [RK, TOK], BF16,
                                   kind="ExternalOutput").ap()

            @blk.sync
            def _(sy):
                dA.wait(sy, at=y_out[-1])
                dS.inc(sy.dma_start(out=tp_kall, in_=kall_d.ap()))
                dS.inc(sy.dma_start(out=tp_vall, in_=vall_d.ap()))
                dS.inc(sy.dma_start(out=tp_att, in_=attn_d.ap()))
                dS.inc(sy.dma_start(out=tp_x1, in_=x1sp_d.ap()))
                dS.inc(sy.dma_start(out=tp_hdb, in_=hdb_d.ap()))
                dS.inc(sy.dma_start(out=tp_sc, in_=recips.ap()))
                dS.inc(sy.dma_start(out=tp_zp, in_=zp_bf.ap()))
                dS.wait(sy)

    return nc


_CACHE = {}


def kernel(L0_attn_x, L0_attn_wq, L0_attn_wk, L0_attn_wv, L0_attn_wo,
           L0_ffn_W, L0_ffn_b, L0_ffn_A, L0_ffn_Bmat, L0_ffn_LiMEs):
    x = np.ascontiguousarray(np.asarray(L0_attn_x, dtype=np.float32))
    wq = np.ascontiguousarray(np.asarray(L0_attn_wq, dtype=np.float32))
    wk = np.ascontiguousarray(np.asarray(L0_attn_wk, dtype=np.float32))
    wv = np.ascontiguousarray(np.asarray(L0_attn_wv, dtype=np.float32))
    wo = np.ascontiguousarray(np.asarray(L0_attn_wo, dtype=np.float32))
    W = np.ascontiguousarray(np.asarray(L0_ffn_W, dtype=np.float32))
    bvec = np.ascontiguousarray(np.asarray(L0_ffn_b, dtype=np.float32))
    A = np.ascontiguousarray(np.asarray(L0_ffn_A, dtype=np.float32))
    Bm = np.ascontiguousarray(np.asarray(L0_ffn_Bmat, dtype=np.float32))
    Lm = np.ascontiguousarray(np.asarray(L0_ffn_LiMEs, dtype=np.float32))

    wqg = np.ascontiguousarray(wq.reshape(D, HKV, G, DH).transpose(1, 2, 0, 3))
    wkf = np.ascontiguousarray(wk.reshape(D, HKV, DH).transpose(1, 0, 2))
    wvf = np.ascontiguousarray(wv.reshape(D, HKV, DH).transpose(1, 0, 2))
    wt = np.ascontiguousarray(W.T.reshape(D, NOB, 512).transpose(1, 0, 2))
    w8 = np.ascontiguousarray(W[0:E, :].T)
    bvec2 = np.ascontiguousarray(bvec.reshape(1, D))
    identbf = np.eye(128, dtype=ml_dtypes.bfloat16)
    limT = np.ascontiguousarray(Lm.T)

    in_maps = []
    for c in range(8):
        b = c // 4
        r0 = (c % 4) * TOK
        xs = x[b, r0:r0 + TOK, :]
        in_maps.append({
            "xfm": np.ascontiguousarray(xs.T),
            "xtm": np.ascontiguousarray(xs),
            "wqg": wqg, "wkf": wkf, "wvf": wvf, "wo": wo,
            "wt": wt, "w8": w8, "bvec": bvec2, "amat": A, "bmat": Bm,
            "limt": limT, "lime": Lm, "identbf": identbf,
        })

    if "nc" not in _CACHE:
        _CACHE["nc"] = build_program()
    nc = _CACHE["nc"]
    res = run_bass_kernel_spmd(nc, in_maps, core_ids=list(range(8)))

    out = np.empty((B, S, D), dtype=np.float32)
    for c in range(8):
        b = c // 4
        r0 = (c % 4) * TOK
        out[b, r0:r0 + TOK, :] = res.results[c]["y"]
    return out


# version 07
# speedup vs baseline: 1.0849x; 1.0583x over previous
import numpy as np
import ml_dtypes

import concourse.bass as bass
import concourse.mybir as mybir
from concourse.bass_utils import run_bass_kernel_spmd

F32 = mybir.dt.float32
BF16 = mybir.dt.bfloat16
AF = mybir.ActivationFunctionType
AL = mybir.AluOpType
AX = mybir.AxisListType

B, S, D = 2, 4096, 4096
HKV, DH, KV, G = 8, 128, 1024, 4
TOK = 1024
NT = TOK // 128
NDC = D // 128
NKC = S // 128
NOB = 8
RK, E = 32, 8
INV_SQRT_DH = 1.0 / float(np.sqrt(DH))
LN_EPS = 1e-5
SC_EPS = 1e-6
NGROUPS = [[0, 1, 2, 3], [4, 5, 6, 7]]
ALLG = [[0, 1, 2, 3, 4, 5, 6, 7]]


class Led:

    def __init__(self, nc, name, dma=False):
        self.h = nc.alloc_semaphore(name)
        self.n = 0
        self.step = 16 if dma else 1

    def inc(self, instr, k=None):
        k = self.step if k is None else k
        instr.then_inc(self.h, k)
        self.n += k
        return self.n

    def wait(self, eng, at=None):
        eng.wait_ge(self.h, self.n if at is None else at)



class VGuard:

    def __init__(self, vP):
        self.vP = vP
        self.w = {}
        self.guarded = 0

    def bind(self, v):
        self.v = v
        return self

    def _name(self, ap):
        try:
            return ap.tensor.name
        except AttributeError:
            return None

    def _pre(self, ins_aps, out_aps):
        need = 0
        for ap in list(ins_aps) + list(out_aps):
            n = self._name(ap)
            if n is not None and n in self.w:
                need = max(need, self.w[n])
        if need > self.guarded:
            self.v.wait_ge(self.vP.h, need)
            self.guarded = need

    def _post(self, ins, out_aps):
        self.vP.inc(ins)
        for ap in out_aps:
            n = self._name(ap)
            if n is not None:
                self.w[n] = self.vP.n
        return ins

    def wait_ge(self, sem, val):
        return self.v.wait_ge(sem, val)

    def memset(self, out, c):
        self._pre([], [out])
        return self._post(self.v.memset(out, c), [out])

    def tensor_copy(self, out, in_):
        self._pre([in_], [out])
        return self._post(self.v.tensor_copy(out, in_), [out])

    def tensor_add(self, out, a, b):
        self._pre([a, b], [out])
        return self._post(self.v.tensor_add(out, a, b), [out])

    def tensor_sub(self, out, a, b):
        self._pre([a, b], [out])
        return self._post(self.v.tensor_sub(out, a, b), [out])

    def tensor_mul(self, out, a, b):
        self._pre([a, b], [out])
        return self._post(self.v.tensor_mul(out, a, b), [out])

    def tensor_max(self, out, a, b):
        self._pre([a, b], [out])
        return self._post(self.v.tensor_max(out, a, b), [out])

    def tensor_scalar(self, out, in_, s1, s2, op0, op1=None, **kw):
        aps = [in_] + [s for s in (s1, s2) if hasattr(s, "tensor")]
        self._pre(aps, [out])
        if op1 is None:
            ins = self.v.tensor_scalar(out, in_, s1, s2, op0, **kw)
        else:
            ins = self.v.tensor_scalar(out, in_, s1, s2, op0, op1, **kw)
        return self._post(ins, [out])

    def tensor_scalar_max(self, out, in_, s1):
        aps = [in_] + ([s1] if hasattr(s1, "tensor") else [])
        self._pre(aps, [out])
        return self._post(self.v.tensor_scalar_max(out, in_, s1), [out])

    def tensor_reduce(self, out, in_, ax, op, **kw):
        self._pre([in_], [out])
        return self._post(self.v.tensor_reduce(out, in_, ax, op, **kw), [out])

    def reciprocal(self, out, in_):
        self._pre([in_], [out])
        return self._post(self.v.reciprocal(out, in_), [out])

    def max(self, out, in_):
        self._pre([in_], [out])
        return self._post(self.v.max(out=out, in_=in_), [out])

    def scalar_tensor_tensor(self, out, in0, sc, in1, op0, op1, **kw):
        aps = [in0, in1] + ([sc] if hasattr(sc, "tensor") else [])
        self._pre(aps, [out])
        return self._post(
            self.v.scalar_tensor_tensor(out, in0, sc, in1, op0, op1, **kw),
            [out])


_LAST_T = [None]


def _tm(t, *a, **k):
    ins = t.matmul(*a, **k)
    _LAST_T[0] = ins
    return ins


def _tt(t, *a, **k):
    ins = t.transpose(*a, **k)
    _LAST_T[0] = ins
    return ins


def tmark(t, tP):
    return tP.inc(_LAST_T[0])

def build_program(taps=False):
    nc = bass.Bass("TRN2", target_bir_lowering=False)

    xfm_d = nc.dram_tensor("xfm", [D, TOK], F32, kind="ExternalInput").ap()
    xtm_d = nc.dram_tensor("xtm", [TOK, D], F32, kind="ExternalInput").ap()
    wqg_d = nc.dram_tensor("wqg", [HKV, G, D, DH], F32, kind="ExternalInput").ap()
    wkf_d = nc.dram_tensor("wkf", [HKV, D, DH], F32, kind="ExternalInput").ap()
    wvf_d = nc.dram_tensor("wvf", [HKV, D, DH], F32, kind="ExternalInput").ap()
    wo_d = nc.dram_tensor("wo", [KV, D], F32, kind="ExternalInput").ap()
    wt_d = nc.dram_tensor("wt", [NOB, D, 512], F32, kind="ExternalInput").ap()
    w8_d = nc.dram_tensor("w8", [D, E], F32, kind="ExternalInput").ap()
    b_d = nc.dram_tensor("bvec", [1, D], F32, kind="ExternalInput").ap()
    a_d = nc.dram_tensor("amat", [D, RK], F32, kind="ExternalInput").ap()
    bm_d = nc.dram_tensor("bmat", [RK, D], F32, kind="ExternalInput").ap()
    limt_d = nc.dram_tensor("limt", [RK, E], F32, kind="ExternalInput").ap()
    lim_d = nc.dram_tensor("lime", [E, RK], F32, kind="ExternalInput").ap()
    id_d = nc.dram_tensor("identbf", [128, 128], BF16, kind="ExternalInput").ap()
    y_d = nc.dram_tensor("y", [TOK, D], F32, kind="ExternalOutput").ap()

    kown_d = nc.dram_tensor("kown", [HKV, DH, TOK], BF16)
    kall_d = nc.dram_tensor("kall", [4 * HKV, DH, TOK], BF16)
    vown_d = nc.dram_tensor("vown", [HKV, TOK, DH], BF16)
    vall_d = nc.dram_tensor("vall", [4 * HKV, TOK, DH], BF16)
    scpr_d = nc.dram_tensor("scpr", [128, 2], F32)
    scsrc_d = nc.dram_tensor("scsrc", [8], F32)
    scdst_d = nc.dram_tensor("scdst", [8], F32)
    attn_d = nc.dram_tensor("attnout", [TOK, 7 * 512], F32)
    x1sp_d = nc.dram_tensor("x1sp", [TOK, D], F32)
    hdb_d = nc.dram_tensor("hdb", [TOK, 7 * 512], F32)

    SB_END = 229376
    base = SB_END - nc.sbuf_bytes_remaining
    cur = [base]

    def alloc_at(name, shape, dtype, offset):
        o = (offset + 63) // 64 * 64
        h = nc.alloc_sbuf_tensor_at(name, shape, dtype, offset=o)
        per = int(np.prod(shape[1:])) * mybir.dt.size(dtype)
        return h, o + per

    def perm(name, shape, dtype):
        h, end = alloc_at(name, shape, dtype, cur[0])
        cur[0] = end
        return h

    class Region:
        def __init__(self, size):
            self.start = (cur[0] + 63) // 64 * 64
            self.size = size
            cur[0] = self.start + size

        def seq(self, *specs):
            out = []
            pos = self.start
            for name, shape, dtype in specs:
                h, pos = alloc_at(name, shape, dtype, pos)
                out.append(h)
            assert pos <= self.start + self.size, \
                f"region overflow {pos - self.start} > {self.size}"
            return out

    ident = perm("ident", [128, 128], BF16)
    ones128 = perm("ones128", [128, 128], BF16)
    csrc = perm("csrc", [128, 2], BF16)
    scsrc = perm("scsrc", [128, 2], BF16)
    cbias = perm("cbias", [128, 1], F32)
    nsc = perm("nsc", [128, 16], F32)
    nsc2 = perm("nsc2", [128, 16], F32)
    lnm = perm("lnm", [128, 32], F32)
    lnsum = perm("lnsum", [128, 16], F32)
    recips = perm("recips", [128, 2], F32)
    mx2 = perm("mx2", [2, 128], F32)
    sc2 = perm("sc2", [1, 2], F32)

    r1 = Region(16384)
    (q_bf,) = r1.seq(("q_bf", [128, HKV, TOK], BF16))
    (att7,) = r1.seq(("att7", [128, NT, 512], F32))
    z_s, z_bf, pmix_tok, pmix_s, h8_s, ds_s, gt1, gt2, gt3, wgt, wem = r1.seq(
        ("z_s", [RK, TOK], F32), ("z_bf", [RK, TOK], BF16),
        ("pmix_tok", [RK, TOK], F32), ("pmix_s", [RK, 512], F32),
        ("h8_s", [128, 4, E], F32), ("ds_s", [128, 4, E], F32),
        ("gt1", [128, E], F32), ("gt2", [128, E], F32), ("gt3", [128, E], F32),
        ("wgt", [128, 4, RK], BF16), ("wem", [RK, 512], BF16))
    (f_x1,) = r1.seq(("f_x1", [128, D], F32))

    r2 = Region(16384)
    (val_bf,) = r2.seq(("val_bf", [128, HKV, TOK], BF16))
    (f_hdb,) = r2.seq(("f_hdb", [128, D], F32))

    r3 = Region(16384)
    (k_bf,) = r3.seq(("k_bf", [128, HKV, TOK], BF16))
    ksq, qsq, recipb = r3.seq(
        ("ksq", [128, S], BF16), ("qsq", [128, TOK], BF16),
        ("recipb", [128, TOK], F32))
    df_x1bfA, df_x1bfB = r3.seq(("df_x1bfA", [128, D], BF16),
                                ("df_x1bfB", [128, D], BF16))
    (hd7,) = r3.seq(("hd7", [128, NT, 512], F32))

    r4 = Region(16384)
    (v_fm,) = r4.seq(("v_fm", [128, HKV, TOK], BF16))
    (df_acc,) = r4.seq(("df_acc", [128, D], F32))
    r5 = Region(16384)
    wkv0, wkv1 = r5.seq(("wkv0", [128, NDC, DH], BF16),
                        ("wkv1", [128, NDC, DH], BF16))
    (df_xtm,) = r5.seq(("df_xtm", [128, D], F32))
    pt0, pt1 = r5.seq(("pt0", [128, TOK], BF16), ("pt1", [128, TOK], BF16))
    wpong, _ = alloc_at("wpong", [128, NDC, 512], BF16, r4.start)

    r6 = Region(65536)
    (xfm_bf,) = r6.seq(("xfm_bf", [128, NDC, TOK], BF16))
    kh0, kh1, vh0, vh1, qn1 = r6.seq(
        ("kh0", [128, S], BF16), ("kh1", [128, S], BF16),
        ("vh0", [128, NKC, DH], BF16), ("vh1", [128, NKC, DH], BF16),
        ("qn1", [1, TOK], F32))
    (x1_fm,) = r6.seq(("x1_fm", [128, NDC, TOK], BF16))

    r7 = Region(60416)
    wqs0, wqs1, wqs2, eff_bf, vtt = r7.seq(
        ("wqs0", [128, D], F32), ("wqs1", [128, D], F32),
        ("wqs2", [128, D], F32), ("eff_bf", [128, NDC, DH], BF16),
        ("vtt", [128, HKV, DH], BF16))
    wo0, wo1, strip0, strip1, strip2, strip3, wping = r7.seq(
        ("wo0", [128, HKV, 512], BF16), ("wo1", [128, HKV, 512], BF16),
        ("strip0", [128, 512], F32), ("strip1", [128, 512], F32),
        ("strip2", [128, 512], F32), ("strip3", [128, 512], F32),
        ("wping", [128, NDC, 512], BF16))
    a_bf, zp_bf, w8_bf, limt_bf, lim_bf, b8_bf, bmob0, bmob1, bob0, bob1 = \
        r7.seq(("a_bf", [128, NDC, RK], BF16), ("zp_bf", [RK, TOK], BF16),
               ("w8_bf", [128, NDC, E], BF16), ("limt_bf", [RK, E], BF16),
               ("lim_bf", [RK, RK], BF16), ("b8_bf", [128, E], BF16),
               ("bmob0", [RK, 512], BF16), ("bmob1", [RK, 512], BF16),
               ("bob0", [128, 512], BF16), ("bob1", [128, 512], BF16))

    assert cur[0] <= SB_END, f"SBUF over budget: {cur[0] - base} used"
    globals()["_SBUF_TOP"] = cur[0]

    pa = nc.alloc_psum_tensor("pa", [128, 1024], F32)
    pb = nc.alloc_psum_tensor("pb", [128, 1024], F32)
    pv = nc.alloc_psum_tensor("pv", [128, 1024], F32)
    pc = nc.alloc_psum_tensor("pc", [128, 512], F32)
    pd = nc.alloc_psum_tensor("pd", [128, 512], F32)
    ptb = pc.ap().bitcast(BF16)

    slots = [pa.ap()[:, 0:512], pa.ap()[:, 512:1024],
             pb.ap()[:, 0:512], pb.ap()[:, 512:1024]]
    eslots = slots + [pv.ap()[:, 0:512], pv.ap()[:, 512:1024]]

    dG = Led(nc, "dG", dma=True)
    dS = Led(nc, "dS", dma=True)
    dA = Led(nc, "dA", dma=True)
    dQ = Led(nc, "dQ", dma=True)
    dQ1 = Led(nc, "dQ1", dma=True)
    dQ2 = Led(nc, "dQ2", dma=True)
    dW = [Led(nc, "dW0", dma=True), Led(nc, "dW1", dma=True)]
    dKK = [Led(nc, "dKK0", dma=True), Led(nc, "dKK1", dma=True)]
    dKV = [Led(nc, "dKV0", dma=True), Led(nc, "dKV1", dma=True)]
    dWO = [Led(nc, "dWO0", dma=True), Led(nc, "dWO1", dma=True)]
    dWP = [Led(nc, "dWP0", dma=True), Led(nc, "dWP1", dma=True)]
    dA2 = Led(nc, "dA2", dma=True)
    cc = Led(nc, "cc")
    tP = Led(nc, "tP")
    vP = Led(nc, "vP")
    sP = Led(nc, "sP")
    vg = VGuard(vP)

    with nc.Block() as blk:
        @blk.vector
        def _(v):
            v = vg.bind(v)
            v.memset(ones128.ap(), 1.0)
            v.memset(csrc.ap(), 0.0)
            v.memset(scsrc.ap(), 0.0)
            v.memset(nsc2.ap()[0:8, 6:7], 0.0)

        @blk.sync
        def _(sy):
            dS.inc(sy.dma_start(out=ident.ap(), in_=id_d))
        ident_dma = dS.n

        @blk.scalar
        def _(s):
            s.preload_act_table(AF.Exp)

        @blk.gpsimd
        def _(g):
            src = xfm_d.rearrange("(c p) t -> p c t", p=128)
            for i in range(4):
                dG.inc(g.dma_start(out=xfm_bf.ap()[:, 8 * i:8 * (i + 1), :],
                                   in_=src[:, 8 * i:8 * (i + 1), :]))
        xfm_done = dG.n

        slot_free = [0, 0, 0, 0]
        slot_seq = [0]

        def next_slot():
            i = slot_seq[0] % 4
            slot_seq[0] += 1
            return i

        stage_consumed = [0, 0]
        kevac = []
        vevac = []

        def proj_pass(which, f, src_d, out_sb):
            par = f % 2
            stg = wkv0 if par == 0 else wkv1

            @blk.gpsimd
            def _(g):
                if stage_consumed[par]:
                    tP.wait(g, at=stage_consumed[par])
                dW[par].inc(g.dma_start(
                    out=stg.ap()[:, :, :],
                    in_=src_d[f].rearrange("(dc p) c -> p dc c", p=128)))

            @blk.tensor
            def _(t):
                dW[par].wait(t)
                dG.wait(t, at=xfm_done)
                for tb in range(2):
                    si = next_slot()
                    if slot_free[si]:
                        vP.wait(t, at=slot_free[si])
                    for dc in range(NDC):
                        _tm(t, slots[si], stg.ap()[:, dc, :],
                                 xfm_bf.ap()[:, dc, 512 * tb:512 * (tb + 1)],
                                 start=(dc == 0), stop=(dc == NDC - 1))
                    tmark(t, tP)
            stage_consumed[par] = tP.n

            @blk.vector
            def _(v):
                v = vg.bind(v)
                for tb in range(2):
                    si = (slot_seq[0] - 2 + tb) % 4
                    tP.wait(v, at=tP.n - (1 - tb))
                    ins = v.tensor_copy(
                        out_sb.ap()[:, f, 512 * tb:512 * (tb + 1)], slots[si])
                    slot_free[si] = vP.n

        for f in range(HKV):
            proj_pass("k", f, wkf_d, k_bf)
            kevac.append(vP.n)

            @blk.sync
            def _(sy, f=f):
                vP.wait(sy, at=kevac[f])
                dS.inc(sy.dma_start(out=kown_d[f], in_=k_bf.ap()[:, f, :]))
        kown_done = dS.n

        @blk.gpsimd
        def _(g):
            dS.wait(g, at=kown_done)
            cc.inc(g.collective_compute(
                "AllGather", AL.bypass, replica_groups=NGROUPS,
                ins=[kown_d.ap().opt()], outs=[kall_d.ap().opt()]))
        cc_k = cc.n

        for f in range(HKV):
            proj_pass("v", f, wvf_d, v_fm)
            vevac.append(vP.n)

        vtt_ev = []
        vtt_store = []
        for tt in range(NT):
            @blk.tensor
            def _(t, tt=tt):
                if tt == 0:
                    vP.wait(t, at=vevac[-1])
                    dS.wait(t, at=kown_done)
                else:
                    vP.wait(t, at=vtt_ev[tt - 1])
                for h in range(HKV):
                    _tt(t, ptb[:, 128 * h:128 * (h + 1)],
                                v_fm.ap()[:, h, 128 * tt:128 * (tt + 1)],
                                ident.ap())
                tmark(t, tP)

            @blk.vector
            def _(v, tt=tt):
                v = vg.bind(v)
                tP.wait(v)
                if tt >= 1:
                    dS.wait(v, at=vtt_store[tt - 1])
                v.tensor_copy(
                    vtt.ap().rearrange("p h d -> p (h d)"), ptb[:, :])
                vtt_ev.append(vP.n)

            @blk.sync
            def _(sy, tt=tt):
                vP.wait(sy, at=vtt_ev[tt])
                vtt_store.append(dS.inc(sy.dma_start(
                    out=vown_d.ap()[:, 128 * tt:128 * (tt + 1), :]
                    .transpose([1, 0, 2]),
                    in_=vtt.ap()[:, :, :])))

        @blk.gpsimd
        def _(g):
            dS.wait(g, at=vtt_store[-1])
            cc.inc(g.collective_compute(
                "AllGather", AL.bypass, replica_groups=NGROUPS,
                ins=[vown_d.ap().opt()], outs=[vall_d.ap().opt()]))
        cc_v = cc.n

        qevac = []
        cast_done = [0]
        wqb = [wqs0, wqs1, wqs2]
        dQb = [dQ, dQ1, dQ2]
        buf_free_v = [0, 0, 0]
        buf_free_s = [0, 0, 0]
        add_led = {}

        def wq_ld(s, k, gi, bi):
            dQb[bi].inc(s.dma_start(
                out=wqb[bi].ap().rearrange("p (dc c) -> p dc c", c=128),
                in_=wqg_d[k, gi].rearrange("(dc p) c -> p dc c", p=128)))

        for kvh in range(HKV):
            ai = kvh % 3
            l1 = (kvh + 1) % 3
            l2 = (kvh + 2) % 3

            @blk.scalar
            def _(s, kvh=kvh, ai=ai, l1=l1, l2=l2):
                if buf_free_v[ai]:
                    vP.wait(s, at=buf_free_v[ai])
                wq_ld(s, kvh, 0, ai)
                if buf_free_v[l1]:
                    vP.wait(s, at=buf_free_v[l1])
                wq_ld(s, kvh, 1, l1)
                if buf_free_s[l2]:
                    sP.wait(s, at=buf_free_s[l2])
                wq_ld(s, kvh, 2, l2)
            g01 = (dQb[ai].n, dQb[l1].n)
            g2 = dQb[l2].n

            @blk.vector
            def _(v, kvh=kvh, ai=ai, l1=l1, g01=g01):
                v = vg.bind(v)
                dQb[ai].wait(v, at=g01[0])
                dQb[l1].wait(v, at=g01[1])
                v.tensor_add(wqb[ai].ap(), wqb[ai].ap(), wqb[l1].ap())
            add1 = vP.n

            @blk.scalar
            def _(s, kvh=kvh, l1=l1, add1=add1):
                vP.wait(s, at=add1)
                wq_ld(s, kvh, 3, l1)
            g3 = dQb[l1].n

            @blk.vector
            def _(v, kvh=kvh, ai=ai, l1=l1, l2=l2, g2=g2, g3=g3):
                v = vg.bind(v)
                dQb[l2].wait(v, at=g2)
                v.tensor_add(wqb[ai].ap(), wqb[ai].ap(), wqb[l2].ap())
                buf_free_v[l2] = vP.n
                dQb[l1].wait(v, at=g3)
                v.tensor_add(wqb[ai].ap(), wqb[ai].ap(), wqb[l1].ap())
                buf_free_v[l1] = vP.n
            add3 = vP.n

            @blk.scalar
            def _(s, kvh=kvh, ai=ai, add3=add3):
                vP.wait(s, at=add3)
                if kvh > 0:
                    tP.wait(s)
                cast_done[0] = sP.inc(s.copy(
                    eff_bf.ap().rearrange("p dc c -> p (dc c)"),
                    wqb[ai].ap()))
                buf_free_s[ai] = cast_done[0]

            @blk.tensor
            def _(t, kvh=kvh):
                sP.wait(t, at=cast_done[0])
                for tb in range(2):
                    si = next_slot()
                    if slot_free[si]:
                        vP.wait(t, at=slot_free[si])
                    for dc in range(NDC):
                        _tm(t, slots[si], eff_bf.ap()[:, dc, :],
                            xfm_bf.ap()[:, dc, 512 * tb:512 * (tb + 1)],
                            start=(dc == 0), stop=(dc == NDC - 1))
                    tmark(t, tP)

            @blk.vector
            def _(v, kvh=kvh):
                v = vg.bind(v)
                for tb in range(2):
                    si = (slot_seq[0] - 2 + tb) % 4
                    tP.wait(v, at=tP.n - (1 - tb))
                    ins = v.tensor_copy(
                        q_bf.ap()[:, kvh, 512 * tb:512 * (tb + 1)], slots[si])
                    slot_free[si] = vP.n
            qevac.append(vP.n)
        q_mm_last = tP.n
        wq_cast_last = cast_done[0]

        @blk.gpsimd
        def _(g):
            sP.wait(g, at=wq_cast_last)
            tP.wait(g, at=q_mm_last)
            dWO[0].inc(g.dma_start(
                out=wo0.ap()[:, :, :],
                in_=wo_d[:, 0:512].rearrange("(kc p) o -> p kc o", p=128)))

        headlast = {}
        khdma = []
        vhdma = []
        for h in range(HKV):
            par = h % 2
            khb = kh0 if par == 0 else kh1
            vhb = vh0 if par == 0 else vh1

            @blk.sync
            def _(sy, h=h, par=par, khb=khb, vhb=vhb):
                cc.wait(sy, at=cc_v)
                if h < 2:
                    tP.wait(sy, at=q_mm_last)
                else:
                    tP.wait(sy, at=headlast[h - 2])
                for sh in range(4):
                    dKK[par].inc(sy.dma_start(
                        out=khb.ap()[:, TOK * sh:TOK * (sh + 1)],
                        in_=kall_d[sh * HKV + h]))
                khdma.append(dKK[par].n)
                for sh in range(4):
                    dKV[par].inc(sy.dma_start(
                        out=vhb.ap()[:, 8 * sh:8 * (sh + 1), :],
                        in_=vall_d[sh * HKV + h].rearrange(
                            "(kc p) dh -> p kc dh", p=128)))
                vhdma.append(dKV[par].n)

            @blk.scalar
            def _(s, h=h):
                vP.wait(s, at=qevac[h])
                sP.inc(s.activation(qsq.ap(), q_bf.ap()[:, h, :], AF.Square))
            qsq_done = sP.n

            @blk.tensor
            def _(t, h=h):
                sP.wait(t, at=qsq_done)
                if h > 0:
                    vP.wait(t, at=pcpd_free[h - 1])
                _tm(t, pc.ap()[:, :], ones128.ap(), qsq.ap()[:, 0:512],
                    start=True, stop=True)
                _tm(t, pd.ap()[:, :], ones128.ap(), qsq.ap()[:, 512:1024],
                    start=True, stop=True)
                tmark(t, tP)

            if h == 0:
                pcpd_free = {}

            @blk.vector
            def _(v, h=h):
                v = vg.bind(v)
                tP.wait(v)
                v.tensor_reduce(nsc.ap()[0:1, 0:1], pc.ap()[0:1, :], AX.X,
                                AL.max)
                v.tensor_reduce(nsc.ap()[0:1, 1:2], pd.ap()[0:1, :], AX.X,
                                AL.max)
                v.tensor_max(nsc.ap()[0:1, 0:1], nsc.ap()[0:1, 0:1],
                                    nsc.ap()[0:1, 1:2])
            qn_red = vP.n

            @blk.scalar
            def _(s, h=h, par=par, khb=khb):
                dKK[par].wait(s, at=khdma[h])
                sP.inc(s.activation(ksq.ap(), khb.ap()[:, :], AF.Square))
            ksq_done = sP.n

            kn_red = []
            for piece in range(8):
                @blk.tensor
                def _(t, h=h, piece=piece):
                    if piece == 0:
                        sP.wait(t, at=ksq_done)
                        vP.wait(t, at=qn_red)
                    if piece >= 2:
                        vP.wait(t, at=kn_red[piece - 2])
                    tgt = pc if piece % 2 == 0 else pd
                    _tm(t, tgt.ap()[:, :], ones128.ap(),
                        ksq.ap()[:, 512 * piece:512 * (piece + 1)],
                        start=True, stop=True)
                    tmark(t, tP)

                @blk.vector
                def _(v, h=h, piece=piece):
                    v = vg.bind(v)
                    tP.wait(v)
                    tgt = pc if piece % 2 == 0 else pd
                    v.tensor_reduce(nsc.ap()[0:1, 2:3], tgt.ap()[0:1, :],
                                    AX.X, AL.max)
                    if piece == 0:
                        ins = v.tensor_copy(nsc.ap()[0:1, 3:4],
                                            nsc.ap()[0:1, 2:3])
                    else:
                        ins = v.tensor_max(nsc.ap()[0:1, 3:4],
                                           nsc.ap()[0:1, 3:4],
                                           nsc.ap()[0:1, 2:3])
                    kn_red.append(vP.n)

            @blk.vector
            def _(v, h=h):
                v = vg.bind(v)
                v.tensor_add(nsc.ap()[0:1, 4:5], nsc.ap()[0:1, 0:1],
                             nsc.ap()[0:1, 3:4])
                v.tensor_scalar(csrc.ap()[0:1, 0:1],
                                       nsc.ap()[0:1, 4:5],
                                       1.02 * 0.5 * INV_SQRT_DH, None,
                                       AL.mult)

            @blk.tensor
            def _(t, h=h):
                vP.wait(t)
                _tm(t, pc.ap()[:, 0:2], ones128.ap(), csrc.ap(),
                    start=True, stop=True)
                tmark(t, tP)

            @blk.vector
            def _(v, h=h):
                v = vg.bind(v)
                tP.wait(v)
                v.tensor_scalar(cbias.ap(), pc.ap()[:, 0:1], -1.0,
                                       None, AL.mult)
            cb_ready = vP.n

            exp_done = {}
            pt_used = {}
            for kc in range(NKC):
                sb = pa if kc % 2 == 0 else pb
                ptbuf = pt0 if kc % 2 == 0 else pt1

                @blk.tensor
                def _(t, h=h, kc=kc, sb=sb, khb=khb, par=par):
                    if kc < 2:
                        vP.wait(t, at=cb_ready)
                        dKK[par].wait(t, at=khdma[h])
                    else:
                        sP.wait(t, at=exp_done[kc - 2])
                    for half in range(2):
                        _tm(t, sb.ap()[:, 512 * half:512 * (half + 1)],
                                 khb.ap()[:, 128 * kc:128 * (kc + 1)],
                                 q_bf.ap()[:, h, 512 * half:512 * (half + 1)],
                                 start=True, stop=True,
                                 skip_group_check=True)
                    tmark(t, tP)

                @blk.scalar
                def _(s, h=h, kc=kc, sb=sb, ptbuf=ptbuf):
                    tP.wait(s)
                    if kc >= 2:
                        tP.wait(s, at=pt_used[kc - 2])
                    exp_done[kc] = sP.inc(s.activation(
                        ptbuf.ap(), sb.ap()[:, :], AF.Exp,
                        bias=cbias.ap(), scale=INV_SQRT_DH))

                @blk.tensor
                def _(t, h=h, kc=kc, ptbuf=ptbuf, vhb=vhb, par=par):
                    sP.wait(t, at=exp_done[kc])
                    if kc == 0:
                        dKV[par].wait(t, at=vhdma[h])
                    _tm(t, pc.ap()[:, :], ones128.ap(),
                        ptbuf.ap()[:, 0:512],
                        start=(kc == 0), stop=(kc == NKC - 1),
                        skip_group_check=True)
                    _tm(t, pd.ap()[:, :], ones128.ap(),
                        ptbuf.ap()[:, 512:1024],
                        start=(kc == 0), stop=(kc == NKC - 1),
                        skip_group_check=True)
                    for half in range(2):
                        _tm(t, pv.ap()[:, 512 * half:512 * (half + 1)],
                                 vhb.ap()[:, kc, :],
                                 ptbuf.ap()[:, 512 * half:512 * (half + 1)],
                                 start=(kc == 0), stop=(kc == NKC - 1),
                                 skip_group_check=True)
                    pt_used[kc] = tmark(t, tP)

            @blk.vector
            def _(v, h=h):
                v = vg.bind(v)
                tP.wait(v)
                v.reciprocal(recipb.ap()[:, 0:512], pc.ap()[:, :])
                v.reciprocal(recipb.ap()[:, 512:1024], pd.ap()[:, :])
                v.tensor_mul(val_bf.ap()[:, h, :], pv.ap()[:, :],
                                    recipb.ap())
            pcpd_free[h] = vP.n
            headlast[h] = tP.n
        attn_done = vP.n

        for _i in range(4):
            slot_free[_i] = attn_done
        strips = [strip0, strip1, strip2, strip3]
        d_mm_done = {}
        d7_evac = {}
        half_spill_total = {}
        x1_spill = []
        df_consumed = {}
        df_norm = {}

        def emit_d_half(hf):
            dAh = dA if hf == 0 else dA2
            tiles = range(4 * hf, 4 * hf + 4)
            for ob in range(NOB):
                k = hf * NOB + ob
                par = k % 2
                wob = wo0 if par == 0 else wo1
                if k + 1 < 2 * NOB:
                    @blk.gpsimd
                    def _(g, k=k):
                        if k >= 1:
                            tP.wait(g, at=d_mm_done[k - 1])
                        dWO[(k + 1) % 2].inc(g.dma_start(
                            out=(wo1 if k % 2 == 0 else wo0).ap()[:, :, :],
                            in_=wo_d[:, 512 * ((k + 1) % NOB):
                                     512 * ((k + 1) % NOB + 1)].rearrange(
                                "(kc p) o -> p kc o", p=128)))

                for tt in tiles:
                    @blk.tensor
                    def _(t, ob=ob, tt=tt, wob=wob, par=par, k=k):
                        if tt == 4 * hf:
                            dWO[par].wait(t)
                            if k == 0:
                                vP.wait(t, at=attn_done)
                        si = next_slot()
                        if slot_free[si]:
                            vP.wait(t, at=slot_free[si])
                        for kc in range(HKV):
                            _tm(t, slots[si],
                                val_bf.ap()[:, kc, 128 * tt:128 * (tt + 1)],
                                wob.ap()[:, kc, :],
                                start=(kc == 0), stop=(kc == HKV - 1))
                        tmark(t, tP)

                    if ob < 7:
                        @blk.vector
                        def _(v, ob=ob, tt=tt, dAh=dAh):
                            v = vg.bind(v)
                            si = (slot_seq[0] - 1) % 4
                            sbi = (ob * 4 + (tt % 4)) % 4
                            tP.wait(v)
                            dAh.wait(v)
                            v.tensor_copy(strips[sbi].ap(), slots[si])
                            slot_free[si] = vP.n

                        @blk.scalar
                        def _(s, ob=ob, tt=tt, dAh=dAh):
                            sbi = (ob * 4 + (tt % 4)) % 4
                            vP.wait(s)
                            dAh.inc(s.dma_start(
                                out=attn_d[128 * tt:128 * (tt + 1),
                                           512 * ob:512 * (ob + 1)],
                                in_=strips[sbi].ap()))
                    else:
                        @blk.vector
                        def _(v, ob=ob, tt=tt):
                            v = vg.bind(v)
                            si = (slot_seq[0] - 1) % 4
                            tP.wait(v)
                            v.tensor_copy(att7.ap()[:, tt, :], slots[si])
                            slot_free[si] = vP.n
                            d7_evac[tt] = vP.n
                d_mm_done[k] = tP.n
            half_spill_total[hf] = dAh.n

        def emit_df_chain(tt):
            hf = tt // 4
            dAh = dA if hf == 0 else dA2
            lns = lnsum.ap()[:, 2 * (tt % 8):2 * (tt % 8) + 2]
            lnmt = lnm.ap()[:, 4 * (tt % 8):4 * (tt % 8) + 4]
            scr = nsc2.ap()[:, 0:1] if False else None
            x1bf = df_x1bfA if tt % 2 == 0 else df_x1bfB

            @blk.sync
            def _(sy, tt=tt, dAh=dAh):
                dAh.wait(sy, at=half_spill_total[hf])
                vP.wait(sy, at=d7_evac[tt])
                if tt >= 1:
                    sP.wait(sy, at=df_consumed[tt - 1])
                    dS.wait(sy, at=x1_spill[-1])
                dS.inc(sy.dma_start(
                    out=df_acc.ap()[:, 0:3584],
                    in_=attn_d[128 * tt:128 * (tt + 1), :]))
                dS.inc(sy.dma_start(out=df_xtm.ap(),
                                    in_=xtm_d[128 * tt:128 * (tt + 1), :]))
            df_ld = dS.n

            @blk.vector
            def _(v, tt=tt, df_ld=df_ld):
                v = vg.bind(v)
                dS.wait(v, at=df_ld)
                v.tensor_add(df_acc.ap()[:, 0:3584], df_acc.ap()[:, 0:3584],
                             df_xtm.ap()[:, 0:3584])
                v.tensor_add(df_acc.ap()[:, 3584:4096],
                             att7.ap()[:, tt, :], df_xtm.ap()[:, 3584:4096])

            @blk.scalar
            def _(s, tt=tt, lns=lns):
                vP.wait(s)
                if tt == 0:
                    s.preload_act_table(AF.Sqrt)
                s.activation(df_xtm.ap(), df_acc.ap(), AF.Copy,
                             accum_out=lns[:, 0:1])
                sP.inc(s.activation(df_xtm.ap(), df_acc.ap(), AF.Square,
                                    accum_out=lns[:, 1:2]))

            @blk.vector
            def _(v, tt=tt, lns=lns, lnmt=lnmt):
                v = vg.bind(v)
                sP.wait(v)
                v.tensor_scalar(lnmt[:, 0:1], lns[:, 0:1], 1.0 / D, None,
                                AL.mult)
                v.tensor_scalar(lnmt[:, 1:2], lns[:, 1:2], 1.0 / D, None,
                                AL.mult)
                v.tensor_mul(lnmt[:, 2:3], lnmt[:, 0:1], lnmt[:, 0:1])
                v.tensor_sub(lnmt[:, 1:2], lnmt[:, 1:2], lnmt[:, 2:3])
                v.tensor_scalar(lnmt[:, 1:2], lnmt[:, 1:2], 1.0, LN_EPS,
                                AL.mult, AL.add)

            @blk.scalar
            def _(s, tt=tt, lnmt=lnmt):
                vP.wait(s)
                sP.inc(s.activation(lnmt[:, 2:3], lnmt[:, 1:2], AF.Sqrt))

            @blk.vector
            def _(v, tt=tt, lnmt=lnmt):
                v = vg.bind(v)
                sP.wait(v)
                v.reciprocal(lnmt[:, 2:3], lnmt[:, 2:3])
                v.tensor_mul(lnmt[:, 3:4], lnmt[:, 0:1], lnmt[:, 2:3])
                v.tensor_scalar(lnmt[:, 3:4], lnmt[:, 3:4], -1.0, None,
                                AL.mult)

            @blk.vector
            def _(v, tt=tt, lnmt=lnmt):
                v = vg.bind(v)
                if tt >= 1:
                    dS.wait(v, at=x1_spill[-1])
                v.tensor_scalar(df_xtm.ap(), df_acc.ap(), lnmt[:, 2:3],
                                lnmt[:, 3:4], AL.mult, AL.add)
                df_norm[tt] = vP.n

            @blk.scalar
            def _(s, tt=tt, x1bf=x1bf):
                vP.wait(s, at=df_norm[tt])
                if tt >= 2:
                    tP.wait(s, at=x1t_done[tt - 2])
                df_consumed[tt] = sP.inc(s.copy(x1bf.ap(), df_xtm.ap()))

            @blk.sync
            def _(sy, tt=tt):
                sP.wait(sy, at=df_consumed[tt])
                x1_spill.append(dS.inc(sy.dma_start(
                    out=x1sp_d[128 * tt:128 * (tt + 1), :],
                    in_=df_xtm.ap())))

        x1t_done = {}

        def emit_x1fm_transpose(tt):
            x1bf = df_x1bfA if tt % 2 == 0 else df_x1bfB
            ptv = pc.ap().bitcast(BF16).rearrange("p (j c) -> p j c", c=128)
            for batch in range(4):
                @blk.tensor
                def _(t, tt=tt, batch=batch, x1bf=x1bf):
                    if batch == 0:
                        sP.wait(t, at=df_consumed[tt])
                    if (tt, batch) != (0, 0):
                        vP.wait(t, at=x1t_evac[-1])
                    for j in range(8):
                        dc = 8 * batch + j
                        _tt(t, ptb[0:128, 128 * j:128 * (j + 1)],
                            x1bf.ap()[:, 128 * dc:128 * (dc + 1)],
                            ident.ap())
                    tmark(t, tP)

                @blk.vector
                def _(v, tt=tt, batch=batch):
                    v = vg.bind(v)
                    tP.wait(v)
                    v.tensor_copy(
                        x1_fm.ap()[:, 8 * batch:8 * (batch + 1),
                                   128 * tt:128 * (tt + 1)], ptv)
                    x1t_evac.append(vP.n)
            x1t_done[tt] = tP.n

        x1t_evac = []
        emit_d_half(0)
        emit_df_chain(0)
        emit_df_chain(1)
        emit_x1fm_transpose(0)
        emit_x1fm_transpose(1)
        emit_df_chain(2)
        emit_df_chain(3)
        emit_d_half(1)
        d_t_last = tP.n
        d_spill_all = dA.n
        d2_spill_all = dA2.n
        emit_x1fm_transpose(2)
        emit_x1fm_transpose(3)
        x1fm_h1 = vP.n

        @blk.vector
        def _(v):
            v = vg.bind(v)
            tP.wait(v, at=d_t_last)
            v.memset(b8_bf.ap(), 0.0)
            v.memset(bob0.ap(), 0.0)
            v.memset(bob1.ap(), 0.0)
            v.memset(lim_bf.ap(), 0.0)
        e_zero = vP.n

        @blk.gpsimd
        def _(g):
            tP.wait(g, at=q_mm_last)
            dWP[0].inc(g.dma_start(
                out=wping.ap()[:, :, :],
                in_=wt_d[0].rearrange("(dc p) o -> p dc o", p=128)))

        @blk.gpsimd
        def _(g):
            tP.wait(g, at=d_t_last)
            vP.wait(g, at=e_zero)
            dG.inc(g.dma_start(out=a_bf.ap()[:, :, :],
                               in_=a_d.rearrange("(dc p) r -> p dc r", p=128)))
            dG.inc(g.dma_start(out=w8_bf.ap()[:, :, :],
                               in_=w8_d.rearrange("(dc p) e -> p dc e", p=128)))
            dG.inc(g.dma_start(out=limt_bf.ap(), in_=limt_d))
            dG.inc(g.dma_start(out=lim_bf.ap()[0:E, :], in_=lim_d))
            dG.inc(g.dma_start(out=b8_bf.ap()[0:1, :], in_=b_d[:, 0:E]))
            dWP[0].inc(g.dma_start(out=bmob0.ap(), in_=bm_d[:, 0:512]))
            dWP[0].inc(g.dma_start(out=bob0.ap()[0:1, :], in_=b_d[:, 0:512]))
        e_small = dG.n
        e_wp0 = dWP[0].n

        z_ev = {}

        def emit_z(hf):
            @blk.tensor
            def _(t, hf=hf):
                if hf == 0:
                    vP.wait(t, at=x1fm_h1)
                    dG.wait(t, at=e_small)
                else:
                    vP.wait(t, at=x1fm_done)
                for dc in range(NDC):
                    _tm(t, pv.ap()[0:RK, 512 * hf:512 * (hf + 1)],
                        a_bf.ap()[:, dc, :],
                        x1_fm.ap()[:, dc, 512 * hf:512 * (hf + 1)],
                        start=(dc == 0), stop=(dc == NDC - 1))
                tmark(t, tP)

            @blk.vector
            def _(v, hf=hf):
                v = vg.bind(v)
                tP.wait(v)
                v.tensor_copy(z_s.ap()[:, 512 * hf:512 * (hf + 1)],
                              pv.ap()[0:RK, 512 * hf:512 * (hf + 1)])
                v.tensor_copy(z_bf.ap()[:, 512 * hf:512 * (hf + 1)],
                              z_s.ap()[:, 512 * hf:512 * (hf + 1)])
                z_ev[hf] = vP.n

        e_spill = []
        e_slot_free = [0] * 6
        strip_rd2 = [0, 0, 0, 0]
        e_mm_done = {}
        e7_evac = {}
        e_half_total = {}
        held_open = {}

        def e2_slot(tt):
            return tt % 6

        def emit_e2_hmm(tt, wbuf, hold):
            si = e2_slot(tt)

            @blk.tensor
            def _(t, tt=tt, wbuf=wbuf, si=si, hold=hold):
                if e_slot_free[si]:
                    vP.wait(t, at=e_slot_free[si])
                for dc in range(NDC):
                    _tm(t, eslots[si],
                        x1_fm.ap()[:, dc, 128 * tt:128 * (tt + 1)],
                        wbuf.ap()[:, dc, :],
                        start=(dc == 0), stop=False, skip_group_check=True)
                tmark(t, tP)
            held_open[tt] = True

        def emit_e2_close(tt, ob, bmob, bob, dAh):
            si = e2_slot(tt)

            @blk.tensor
            def _(t, tt=tt, si=si, bmob=bmob, bob=bob):
                _tm(t, eslots[si], ones128.ap(), bob.ap(),
                    start=False, stop=False, skip_group_check=True)
                _tm(t, eslots[si],
                    zp_bf.ap()[:, 128 * tt:128 * (tt + 1)],
                    bmob.ap(), start=False, stop=True, skip_group_check=True)
                tmark(t, tP)

            if ob < 7:
                @blk.vector
                def _(v, tt=tt, ob=ob, si=si, dAh=dAh):
                    v = vg.bind(v)
                    sbi = (ob * 4 + (tt % 4)) % 4
                    tP.wait(v)
                    dAh.wait(v)
                    if dAh is dA:
                        dA2.wait(v, at=d2_spill_all)
                    else:
                        dA.wait(v, at=e_half_total[0])
                    v.tensor_copy(strips[sbi].ap(), eslots[si])
                    e_slot_free[si] = vP.n

                @blk.scalar
                def _(s, tt=tt, ob=ob, dAh=dAh):
                    sbi = (ob * 4 + (tt % 4)) % 4
                    vP.wait(s)
                    dAh.inc(s.dma_start(
                        out=hdb_d[128 * tt:128 * (tt + 1),
                                  512 * ob:512 * (ob + 1)],
                        in_=strips[sbi].ap()))
            else:
                @blk.vector
                def _(v, tt=tt, si=si):
                    v = vg.bind(v)
                    tP.wait(v)
                    v.tensor_copy(hd7.ap()[:, tt, :], eslots[si])
                    e_slot_free[si] = vP.n
                    e7_evac[tt] = vP.n

        @blk.tensor
        def _(t):
            dWP[0].wait(t, at=e_wp0)
            vP.wait(t, at=x1fm_h1)
        for tt in range(4):
            emit_e2_hmm(tt, wping, hold=True)
        emit_z(0)

        for tt in range(4, NT):
            emit_df_chain(tt)
            emit_x1fm_transpose(tt)
        x1fm_done = vP.n
        emit_z(1)

        @blk.tensor
        def _(t):
            vP.wait(t, at=max(z_ev[0], z_ev[1]))
            x1r = x1_fm.ap().rearrange("p dc (at w) -> p dc at w", w=256)
            zr = z_bf.ap().rearrange("r (at w) -> r at w", w=256)
            for at_ in range(4):
                for dc in range(NDC):
                    _tm(t, pc.ap()[:, 8 * at_:8 * (at_ + 1)],
                        x1r[:, dc, at_, 1::2], w8_bf.ap()[:, dc, :],
                        start=(dc == 0), stop=False, skip_group_check=True)
                _tm(t, pc.ap()[:, 8 * at_:8 * (at_ + 1)],
                    ones128.ap(), b8_bf.ap(), start=False, stop=True,
                    skip_group_check=True)
                _tm(t, pd.ap()[:, 8 * at_:8 * (at_ + 1)],
                    zr[:, at_, 1::2], limt_bf.ap(),
                    start=True, stop=True, skip_group_check=True)
            tmark(t, tP)

        @blk.vector
        def _(v):
            v = vg.bind(v)
            tP.wait(v)
            v.memset(wgt.ap().rearrange("p a r -> p (a r)"), 0.0)
            v.tensor_copy(h8_s.ap().rearrange("p a e -> p (a e)"),
                          pc.ap()[:, 0:32])
            v.tensor_copy(ds_s.ap().rearrange("p a e -> p (a e)"),
                          pd.ap()[:, 0:32])
            v.tensor_reduce(nsc2.ap()[:, 4:5],
                            h8_s.ap().rearrange("p a e -> p (a e)"),
                            AX.X, AL.max, apply_absolute_value=True)
            v.tensor_reduce(nsc2.ap()[:, 5:6],
                            ds_s.ap().rearrange("p a e -> p (a e)"),
                            AX.X, AL.max, apply_absolute_value=True)

        @blk.sync
        def _(sy):
            vP.wait(sy)
            dS.inc(sy.dma_start(out=scpr_d[:, :], in_=nsc2.ap()[:, 4:6]))
            dS.wait(sy)
            dS.inc(sy.dma_start(out=mx2.ap(),
                                in_=scpr_d.ap().transpose([1, 0]),
                                allow_slow_non_contiguous=True))

        @blk.vector
        def _(v):
            v = vg.bind(v)
            dS.wait(v)
            v.tensor_reduce(nsc2.ap()[0:2, 6:7], mx2.ap(), AX.X, AL.max)

        @blk.sync
        def _(sy):
            vP.wait(sy)
            dS.inc(sy.dma_start(out=scsrc_d.ap(), in_=nsc2.ap()[0:8, 6:7]))

        @blk.gpsimd
        def _(g):
            dS.wait(g)
            cc.inc(g.collective_compute(
                "AllReduce", AL.max, replica_groups=ALLG,
                ins=[scsrc_d.ap().opt()], outs=[scdst_d.ap().opt()]))

        @blk.gpsimd
        def _(g):
            cc.wait(g)
            dG.inc(g.dma_start(
                out=scsrc.ap()[0:1, :],
                in_=scdst_d.ap()[0:2].rearrange("(a b) -> a b", a=1)))

        @blk.tensor
        def _(t):
            dG.wait(t)
            _tm(t, pc.ap()[:, 0:2], ones128.ap(), scsrc.ap(),
                start=True, stop=True, skip_group_check=True)
            tmark(t, tP)

        @blk.vector
        def _(v):
            v = vg.bind(v)
            tP.wait(v)
            v.tensor_scalar_max(recips.ap(), pc.ap()[:, 0:2], SC_EPS)
            v.reciprocal(recips.ap(), recips.ap())

        wt_done = []
        for at_ in range(4):
            @blk.vector
            def _(v, at_=at_):
                v = vg.bind(v)
                v.tensor_scalar(gt1.ap(), h8_s.ap()[:, at_, :],
                                recips.ap()[:, 0:1], 0.5, AL.mult, AL.mult)
                v.tensor_scalar(gt2.ap(), ds_s.ap()[:, at_, :],
                                recips.ap()[:, 1:2], 0.5, AL.mult, AL.mult)
                v.tensor_add(gt1.ap(), gt1.ap(), gt2.ap())
                v.tensor_reduce(nsc2.ap()[:, 7:8], gt1.ap(), AX.X, AL.max)
                v.tensor_scalar(nsc2.ap()[:, 8:9], nsc2.ap()[:, 7:8],
                                -1.0, None, AL.mult)

            @blk.scalar
            def _(s, at_=at_):
                vP.wait(s)
                if at_ == 0:
                    s.preload_act_table(AF.Exp)
                sP.inc(s.activation(gt2.ap(), gt1.ap(), AF.Exp,
                                    bias=nsc2.ap()[:, 8:9]))

            @blk.vector
            def _(v, at_=at_):
                v = vg.bind(v)
                sP.wait(v)
                v.tensor_reduce(nsc2.ap()[:, 9:10], gt2.ap(), AX.X, AL.add)
                v.reciprocal(nsc2.ap()[:, 10:11], nsc2.ap()[:, 9:10])
                v.tensor_scalar(gt1.ap(), gt2.ap(), nsc2.ap()[:, 10:11], None,
                                AL.mult)
                v.max(out=gt3.ap(), in_=gt1.ap())
                v.tensor_scalar(nsc2.ap()[:, 11:12], gt3.ap()[:, 1:2],
                                -1.0, None, AL.mult)

            @blk.scalar
            def _(s, at_=at_):
                vP.wait(s)
                sP.inc(s.activation(gt2.ap(), gt1.ap(), AF.Tanh,
                                    bias=nsc2.ap()[:, 11:12], scale=1.0))

            @blk.vector
            def _(v, at_=at_):
                v = vg.bind(v)
                sP.wait(v)
                v.tensor_scalar(gt2.ap(), gt2.ap(), 0.5, 0.5, AL.mult, AL.add)
                v.tensor_mul(gt1.ap(), gt1.ap(), gt2.ap())
                v.tensor_reduce(nsc2.ap()[:, 12:13], gt1.ap(), AX.X, AL.add)
                v.tensor_scalar(nsc2.ap()[:, 12:13], nsc2.ap()[:, 12:13], 1.0,
                                1e-9, AL.mult, AL.add)
                v.reciprocal(nsc2.ap()[:, 13:14], nsc2.ap()[:, 12:13])
                v.tensor_scalar(
                    wgt.ap()[:, at_, 0:E], gt1.ap(), nsc2.ap()[:, 13:14],
                    None, AL.mult)
                wt_done.append(vP.n)

        @blk.tensor
        def _(t):
            vP.wait(t, at=wt_done[-1])
            for at_ in range(4):
                _tt(t, ptb[0:RK, 128 * at_:128 * (at_ + 1)],
                    wgt.ap()[:, at_, :], ident.ap())
            tmark(t, tP)

        @blk.vector
        def _(v):
            v = vg.bind(v)
            tP.wait(v)
            v.tensor_copy(wem.ap(), ptb[0:RK, 0:512])

        @blk.tensor
        def _(t):
            vP.wait(t)
            _tm(t, pd.ap()[0:RK, :], lim_bf.ap(), wem.ap(),
                start=True, stop=True, skip_group_check=True)
            tmark(t, tP)

        @blk.vector
        def _(v):
            v = vg.bind(v)
            tP.wait(v)
            v.tensor_copy(pmix_s.ap(), pd.ap()[0:RK, :])
            v.tensor_copy(pmix_tok.ap()[:, 0::2], pmix_s.ap())
            v.tensor_copy(pmix_tok.ap()[:, 1::2], pmix_s.ap())
            v.scalar_tensor_tensor(zp_bf.ap(), z_s.ap(), 0.5,
                                   pmix_tok.ap(), AL.mult, AL.mult)
        zp_ready = vP.n

        @blk.tensor
        def _(t):
            vP.wait(t, at=zp_ready)

        def emit_e2_half(hf, f_after=None):
            dAh = dA if hf == 0 else dA2
            tiles = list(range(4 * hf, 4 * hf + 4))
            for ob in range(NOB):
                k = hf * NOB + ob
                par = k % 2
                wbuf = wping if par == 0 else wpong
                bmob = bmob0 if par == 0 else bmob1
                bob = bob0 if par == 0 else bob1
                if k + 1 < 2 * NOB:
                    @blk.gpsimd
                    def _(g, k=k):
                        if k >= 1:
                            tP.wait(g, at=e_mm_done[k - 1])
                        np_ = (k + 1) % 2
                        nob = (k + 1) % NOB
                        dWP[np_].inc(g.dma_start(
                            out=(wpong if np_ else wping).ap()[:, :, :],
                            in_=wt_d[nob].rearrange("(dc p) o -> p dc o",
                                                    p=128)))
                        dWP[np_].inc(g.dma_start(
                            out=(bmob1 if np_ else bmob0).ap(),
                            in_=bm_d[:, 512 * nob:512 * (nob + 1)]))
                        dWP[np_].inc(g.dma_start(
                            out=(bob1 if np_ else bob0).ap()[0:1, :],
                            in_=b_d[:, 512 * nob:512 * (nob + 1)]))

                for tt in tiles:
                    @blk.tensor
                    def _(t, tt=tt, par=par, k=k):
                        if tt == tiles[0]:
                            dWP[par].wait(t)
                    if not (k == 0 and held_open.get(tt)):
                        emit_e2_hmm(tt, wbuf, hold=False)
                    emit_e2_close(tt, ob, bmob, bob, dAh)
                    held_open[tt] = False
                e_mm_done[k] = tP.n
                if f_after and ob in f_after:
                    emit_f(f_after[ob])
            e_half_total[hf] = dAh.n

        y_out = []
        f_consumed = {}

        def emit_f(tt):
            hf = tt // 4
            dAh = dA if hf == 0 else dA2
            lns = lnsum.ap()[:, 2 * tt:2 * tt + 2]
            lnmt = lnm.ap()[:, 4 * tt:4 * tt + 4]

            @blk.sync
            def _(sy, tt=tt, dAh=dAh):
                dAh.wait(sy, at=e_half_total[hf])
                vP.wait(sy, at=e7_evac[tt])
                if y_out:
                    vP.wait(sy, at=f_consumed[max(f_consumed)])
                    dA.wait(sy, at=y_out[-1])
                dS.inc(sy.dma_start(out=f_x1.ap(),
                                    in_=x1sp_d[128 * tt:128 * (tt + 1), :]))
                dS.inc(sy.dma_start(out=f_hdb.ap()[:, 0:3584],
                                    in_=hdb_d[128 * tt:128 * (tt + 1), :]))
            f_ld = dS.n

            @blk.vector
            def _(v, tt=tt, f_ld=f_ld):
                v = vg.bind(v)
                dS.wait(v, at=f_ld)
                v.tensor_add(f_hdb.ap()[:, 0:3584], f_hdb.ap()[:, 0:3584],
                             f_x1.ap()[:, 0:3584])
                v.tensor_add(f_hdb.ap()[:, 3584:4096], hd7.ap()[:, tt, :],
                             f_x1.ap()[:, 3584:4096])

            @blk.scalar
            def _(s, tt=tt, lns=lns):
                vP.wait(s)
                if tt == 0:
                    s.preload_act_table(AF.Sqrt)
                s.activation(f_x1.ap(), f_hdb.ap(), AF.Copy,
                             accum_out=lns[:, 0:1])
                sP.inc(s.activation(f_x1.ap(), f_hdb.ap(), AF.Square,
                                    accum_out=lns[:, 1:2]))

            @blk.vector
            def _(v, tt=tt, lns=lns, lnmt=lnmt):
                v = vg.bind(v)
                sP.wait(v)
                v.tensor_scalar(lnmt[:, 0:1], lns[:, 0:1], 1.0 / D, None,
                                AL.mult)
                v.tensor_scalar(lnmt[:, 1:2], lns[:, 1:2], 1.0 / D, None,
                                AL.mult)
                v.tensor_mul(lnmt[:, 2:3], lnmt[:, 0:1], lnmt[:, 0:1])
                v.tensor_sub(lnmt[:, 1:2], lnmt[:, 1:2], lnmt[:, 2:3])
                v.tensor_scalar(lnmt[:, 1:2], lnmt[:, 1:2], 1.0, LN_EPS,
                                AL.mult, AL.add)

            @blk.scalar
            def _(s, tt=tt, lnmt=lnmt):
                vP.wait(s)
                sP.inc(s.activation(lnmt[:, 2:3], lnmt[:, 1:2], AF.Sqrt))

            @blk.vector
            def _(v, tt=tt, lnmt=lnmt):
                v = vg.bind(v)
                sP.wait(v)
                v.reciprocal(lnmt[:, 2:3], lnmt[:, 2:3])
                v.tensor_mul(lnmt[:, 3:4], lnmt[:, 0:1], lnmt[:, 2:3])
                v.tensor_scalar(lnmt[:, 3:4], lnmt[:, 3:4], -1.0, None,
                                AL.mult)
                v.tensor_scalar(f_x1.ap(), f_hdb.ap(), lnmt[:, 2:3],
                                lnmt[:, 3:4], AL.mult, AL.add)
                f_consumed[tt] = vP.n

            @blk.scalar
            def _(s, tt=tt):
                vP.wait(s, at=f_consumed[tt])
                y_out.append(dA.inc(s.dma_start(
                    out=y_d[128 * tt:128 * (tt + 1), :], in_=f_x1.ap())))

        emit_e2_half(0)
        emit_e2_half(1, f_after={1: 0, 3: 1, 5: 2, 7: 3})
        for tt in range(4, NT):
            emit_f(tt)

        @blk.sync
        def _(sy):
            dA.wait(sy, at=y_out[-1])

        if taps:
            tp_kall = nc.dram_tensor("tp_kall", [4 * HKV, DH, TOK], BF16,
                                     kind="ExternalOutput").ap()
            tp_vall = nc.dram_tensor("tp_vall", [4 * HKV, TOK, DH], BF16,
                                     kind="ExternalOutput").ap()
            tp_att = nc.dram_tensor("tp_att", [TOK, 7 * 512], F32,
                                    kind="ExternalOutput").ap()
            tp_x1 = nc.dram_tensor("tp_x1", [TOK, D], F32,
                                   kind="ExternalOutput").ap()
            tp_hdb = nc.dram_tensor("tp_hdb", [TOK, 7 * 512], F32,
                                    kind="ExternalOutput").ap()
            tp_sc = nc.dram_tensor("tp_sc", [128, 2], F32,
                                   kind="ExternalOutput").ap()
            tp_zp = nc.dram_tensor("tp_zp", [RK, TOK], BF16,
                                   kind="ExternalOutput").ap()

            @blk.sync
            def _(sy):
                dA.wait(sy, at=y_out[-1])
                dS.inc(sy.dma_start(out=tp_kall, in_=kall_d.ap()))
                dS.inc(sy.dma_start(out=tp_vall, in_=vall_d.ap()))
                dS.inc(sy.dma_start(out=tp_att, in_=attn_d.ap()))
                dS.inc(sy.dma_start(out=tp_x1, in_=x1sp_d.ap()))
                dS.inc(sy.dma_start(out=tp_hdb, in_=hdb_d.ap()))
                dS.inc(sy.dma_start(out=tp_sc, in_=recips.ap()))
                dS.inc(sy.dma_start(out=tp_zp, in_=zp_bf.ap()))
                dS.wait(sy)

    return nc


_CACHE = {}


def kernel(L0_attn_x, L0_attn_wq, L0_attn_wk, L0_attn_wv, L0_attn_wo,
           L0_ffn_W, L0_ffn_b, L0_ffn_A, L0_ffn_Bmat, L0_ffn_LiMEs):
    x = np.ascontiguousarray(np.asarray(L0_attn_x, dtype=np.float32))
    wq = np.ascontiguousarray(np.asarray(L0_attn_wq, dtype=np.float32))
    wk = np.ascontiguousarray(np.asarray(L0_attn_wk, dtype=np.float32))
    wv = np.ascontiguousarray(np.asarray(L0_attn_wv, dtype=np.float32))
    wo = np.ascontiguousarray(np.asarray(L0_attn_wo, dtype=np.float32))
    W = np.ascontiguousarray(np.asarray(L0_ffn_W, dtype=np.float32))
    bvec = np.ascontiguousarray(np.asarray(L0_ffn_b, dtype=np.float32))
    A = np.ascontiguousarray(np.asarray(L0_ffn_A, dtype=np.float32))
    Bm = np.ascontiguousarray(np.asarray(L0_ffn_Bmat, dtype=np.float32))
    Lm = np.ascontiguousarray(np.asarray(L0_ffn_LiMEs, dtype=np.float32))

    wqg = np.ascontiguousarray(wq.reshape(D, HKV, G, DH).transpose(1, 2, 0, 3))
    wkf = np.ascontiguousarray(wk.reshape(D, HKV, DH).transpose(1, 0, 2))
    wvf = np.ascontiguousarray(wv.reshape(D, HKV, DH).transpose(1, 0, 2))
    wt = np.ascontiguousarray(W.T.reshape(D, NOB, 512).transpose(1, 0, 2))
    w8 = np.ascontiguousarray(W[0:E, :].T)
    bvec2 = np.ascontiguousarray(bvec.reshape(1, D))
    identbf = np.eye(128, dtype=ml_dtypes.bfloat16)
    limT = np.ascontiguousarray(Lm.T)

    in_maps = []
    for c in range(8):
        b = c // 4
        r0 = (c % 4) * TOK
        xs = x[b, r0:r0 + TOK, :]
        in_maps.append({
            "xfm": np.ascontiguousarray(xs.T),
            "xtm": np.ascontiguousarray(xs),
            "wqg": wqg, "wkf": wkf, "wvf": wvf, "wo": wo,
            "wt": wt, "w8": w8, "bvec": bvec2, "amat": A, "bmat": Bm,
            "limt": limT, "lime": Lm, "identbf": identbf,
        })

    if "nc" not in _CACHE:
        _CACHE["nc"] = build_program()
    nc = _CACHE["nc"]
    res = run_bass_kernel_spmd(nc, in_maps, core_ids=list(range(8)))

    out = np.empty((B, S, D), dtype=np.float32)
    for c in range(8):
        b = c // 4
        r0 = (c % 4) * TOK
        out[b, r0:r0 + TOK, :] = res.results[c]["y"]
    return out


# version 08 (final)
# speedup vs baseline: 1.1118x; 1.0247x over previous
import numpy as np
import ml_dtypes

import concourse.bass as bass
import concourse.mybir as mybir
from concourse.bass_utils import run_bass_kernel_spmd

F32 = mybir.dt.float32
BF16 = mybir.dt.bfloat16
AF = mybir.ActivationFunctionType
AL = mybir.AluOpType
AX = mybir.AxisListType

B, S, D = 2, 4096, 4096
HKV, DH, KV, G = 8, 128, 1024, 4
TOK = 1024
NT = TOK // 128
NDC = D // 128
NKC = S // 128
NOB = 8
RK, E = 32, 8
INV_SQRT_DH = 1.0 / float(np.sqrt(DH))
LN_EPS = 1e-5
SC_EPS = 1e-6
NGROUPS = [[0, 1, 2, 3], [4, 5, 6, 7]]
ALLG = [[0, 1, 2, 3, 4, 5, 6, 7]]


class Led:

    def __init__(self, nc, name, dma=False):
        self.h = nc.alloc_semaphore(name)
        self.n = 0
        self.step = 16 if dma else 1

    def inc(self, instr, k=None):
        k = self.step if k is None else k
        instr.then_inc(self.h, k)
        self.n += k
        return self.n

    def wait(self, eng, at=None):
        eng.wait_ge(self.h, self.n if at is None else at)



class VGuard:

    def __init__(self, vP):
        self.vP = vP
        self.w = {}
        self.guarded = 0

    def bind(self, v):
        self.v = v
        return self

    def _name(self, ap):
        try:
            return ap.tensor.name
        except AttributeError:
            return None

    def _pre(self, ins_aps, out_aps):
        need = 0
        for ap in list(ins_aps) + list(out_aps):
            n = self._name(ap)
            if n is not None and n in self.w:
                need = max(need, self.w[n])
        if need > self.guarded:
            self.v.wait_ge(self.vP.h, need)
            self.guarded = need

    def _post(self, ins, out_aps):
        self.vP.inc(ins)
        for ap in out_aps:
            n = self._name(ap)
            if n is not None:
                self.w[n] = self.vP.n
        return ins

    def wait_ge(self, sem, val):
        return self.v.wait_ge(sem, val)

    def memset(self, out, c):
        self._pre([], [out])
        return self._post(self.v.memset(out, c), [out])

    def tensor_copy(self, out, in_):
        self._pre([in_], [out])
        return self._post(self.v.tensor_copy(out, in_), [out])

    def tensor_add(self, out, a, b):
        self._pre([a, b], [out])
        return self._post(self.v.tensor_add(out, a, b), [out])

    def tensor_sub(self, out, a, b):
        self._pre([a, b], [out])
        return self._post(self.v.tensor_sub(out, a, b), [out])

    def tensor_mul(self, out, a, b):
        self._pre([a, b], [out])
        return self._post(self.v.tensor_mul(out, a, b), [out])

    def tensor_max(self, out, a, b):
        self._pre([a, b], [out])
        return self._post(self.v.tensor_max(out, a, b), [out])

    def tensor_scalar(self, out, in_, s1, s2, op0, op1=None, **kw):
        aps = [in_] + [s for s in (s1, s2) if hasattr(s, "tensor")]
        self._pre(aps, [out])
        if op1 is None:
            ins = self.v.tensor_scalar(out, in_, s1, s2, op0, **kw)
        else:
            ins = self.v.tensor_scalar(out, in_, s1, s2, op0, op1, **kw)
        return self._post(ins, [out])

    def tensor_scalar_max(self, out, in_, s1):
        aps = [in_] + ([s1] if hasattr(s1, "tensor") else [])
        self._pre(aps, [out])
        return self._post(self.v.tensor_scalar_max(out, in_, s1), [out])

    def tensor_reduce(self, out, in_, ax, op, **kw):
        self._pre([in_], [out])
        return self._post(self.v.tensor_reduce(out, in_, ax, op, **kw), [out])

    def reciprocal(self, out, in_):
        self._pre([in_], [out])
        return self._post(self.v.reciprocal(out, in_), [out])

    def max(self, out, in_):
        self._pre([in_], [out])
        return self._post(self.v.max(out=out, in_=in_), [out])

    def scalar_tensor_tensor(self, out, in0, sc, in1, op0, op1, **kw):
        aps = [in0, in1] + ([sc] if hasattr(sc, "tensor") else [])
        self._pre(aps, [out])
        return self._post(
            self.v.scalar_tensor_tensor(out, in0, sc, in1, op0, op1, **kw),
            [out])


_LAST_T = [None]


def _tm(t, *a, **k):
    ins = t.matmul(*a, **k)
    _LAST_T[0] = ins
    return ins


def _tt(t, *a, **k):
    ins = t.transpose(*a, **k)
    _LAST_T[0] = ins
    return ins


def tmark(t, tP):
    return tP.inc(_LAST_T[0])

def build_program(taps=False):
    nc = bass.Bass("TRN2", target_bir_lowering=False)

    xfm_d = nc.dram_tensor("xfm", [128, NDC, TOK], F32, kind="ExternalInput").ap()
    xtm_d = nc.dram_tensor("xtm", [TOK, D], F32, kind="ExternalInput").ap()
    wqg_d = nc.dram_tensor("wqg", [HKV, G, 128, NDC, DH], F32, kind="ExternalInput").ap()
    wkf_d = nc.dram_tensor("wkf", [HKV, 128, NDC, DH], F32, kind="ExternalInput").ap()
    wvf_d = nc.dram_tensor("wvf", [HKV, 128, NDC, DH], F32, kind="ExternalInput").ap()
    wo_d = nc.dram_tensor("wo", [NOB, 128, HKV, 512], F32, kind="ExternalInput").ap()
    wt_d = nc.dram_tensor("wt", [NOB, 128, NDC, 512], F32, kind="ExternalInput").ap()
    w8_d = nc.dram_tensor("w8", [128, NDC, E], F32, kind="ExternalInput").ap()
    b_d = nc.dram_tensor("bvec", [1, D], F32, kind="ExternalInput").ap()
    a_d = nc.dram_tensor("amat", [128, NDC, RK], F32, kind="ExternalInput").ap()
    bm_d = nc.dram_tensor("bmat", [RK, D], F32, kind="ExternalInput").ap()
    limt_d = nc.dram_tensor("limt", [RK, E], F32, kind="ExternalInput").ap()
    lim_d = nc.dram_tensor("lime", [E, RK], F32, kind="ExternalInput").ap()
    id_d = nc.dram_tensor("identbf", [128, 128], BF16, kind="ExternalInput").ap()
    y_d = nc.dram_tensor("y", [TOK, D], F32, kind="ExternalOutput").ap()

    kown_d = nc.dram_tensor("kown", [HKV, DH, TOK], BF16)
    kall_d = nc.dram_tensor("kall", [4 * HKV, DH, TOK], BF16)
    vown_d = nc.dram_tensor("vown", [HKV, TOK, DH], BF16)
    vall_d = nc.dram_tensor("vall", [4 * HKV, TOK, DH], BF16)
    scpr_d = nc.dram_tensor("scpr", [128, 2], F32)
    scsrc_d = nc.dram_tensor("scsrc", [8], F32)
    scdst_d = nc.dram_tensor("scdst", [8], F32)
    attn_d = nc.dram_tensor("attnout", [TOK, 7 * 512], F32)
    x1sp_d = nc.dram_tensor("x1sp", [TOK, D], F32)
    hdb_d = nc.dram_tensor("hdb", [TOK, 7 * 512], F32)

    SB_END = 229376
    base = SB_END - nc.sbuf_bytes_remaining
    cur = [base]

    def alloc_at(name, shape, dtype, offset):
        o = (offset + 63) // 64 * 64
        h = nc.alloc_sbuf_tensor_at(name, shape, dtype, offset=o)
        per = int(np.prod(shape[1:])) * mybir.dt.size(dtype)
        return h, o + per

    def perm(name, shape, dtype):
        h, end = alloc_at(name, shape, dtype, cur[0])
        cur[0] = end
        return h

    class Region:
        def __init__(self, size):
            self.start = (cur[0] + 63) // 64 * 64
            self.size = size
            cur[0] = self.start + size

        def seq(self, *specs):
            out = []
            pos = self.start
            for name, shape, dtype in specs:
                h, pos = alloc_at(name, shape, dtype, pos)
                out.append(h)
            assert pos <= self.start + self.size, \
                f"region overflow {pos - self.start} > {self.size}"
            return out

    ident = perm("ident", [128, 128], BF16)
    ones128 = perm("ones128", [128, 128], BF16)
    csrc = perm("csrc", [128, 2], BF16)
    scsrc = perm("scsrc", [128, 2], BF16)
    cbias = perm("cbias", [128, 1], F32)
    nsc = perm("nsc", [128, 16], F32)
    nsc2 = perm("nsc2", [128, 16], F32)
    lnm = perm("lnm", [128, 32], F32)
    lnsum = perm("lnsum", [128, 16], F32)
    recips = perm("recips", [128, 2], F32)
    mx2 = perm("mx2", [2, 128], F32)
    sc2 = perm("sc2", [1, 2], F32)

    r1 = Region(16384)
    (q_bf,) = r1.seq(("q_bf", [128, HKV, TOK], BF16))
    (att7,) = r1.seq(("att7", [128, NT, 512], F32))
    z_s, z_bf, pmix_tok, pmix_s, h8_s, ds_s, gt1, gt2, gt3, wgt, wem = r1.seq(
        ("z_s", [RK, TOK], F32), ("z_bf", [RK, TOK], BF16),
        ("pmix_tok", [RK, TOK], F32), ("pmix_s", [RK, 512], F32),
        ("h8_s", [128, 4, E], F32), ("ds_s", [128, 4, E], F32),
        ("gt1", [128, E], F32), ("gt2", [128, E], F32), ("gt3", [128, E], F32),
        ("wgt", [128, 4, RK], BF16), ("wem", [RK, 512], BF16))
    (f_x1,) = r1.seq(("f_x1", [128, D], F32))

    r2 = Region(16384)
    (val_bf,) = r2.seq(("val_bf", [128, HKV, TOK], BF16))
    (f_hdb,) = r2.seq(("f_hdb", [128, D], F32))

    r3 = Region(16384)
    (k_bf,) = r3.seq(("k_bf", [128, HKV, TOK], BF16))
    ksq, qsq, recipb = r3.seq(
        ("ksq", [128, S], BF16), ("qsq", [128, TOK], BF16),
        ("recipb", [128, TOK], F32))
    df_x1bfA, df_x1bfB = r3.seq(("df_x1bfA", [128, D], BF16),
                                ("df_x1bfB", [128, D], BF16))
    (hd7,) = r3.seq(("hd7", [128, NT, 512], F32))

    r4 = Region(16384)
    (v_fm,) = r4.seq(("v_fm", [128, HKV, TOK], BF16))
    (df_acc,) = r4.seq(("df_acc", [128, D], F32))
    r5 = Region(16384)
    wkv0, wkv1 = r5.seq(("wkv0", [128, NDC, DH], BF16),
                        ("wkv1", [128, NDC, DH], BF16))
    (df_xtm,) = r5.seq(("df_xtm", [128, D], F32))
    pt0, pt1 = r5.seq(("pt0", [128, TOK], BF16), ("pt1", [128, TOK], BF16))
    wpong, _ = alloc_at("wpong", [128, NDC, 512], BF16, r4.start)

    r6 = Region(65536)
    (xfm_bf,) = r6.seq(("xfm_bf", [128, NDC, TOK], BF16))
    kh0, kh1, vh0, vh1, qn1 = r6.seq(
        ("kh0", [128, S], BF16), ("kh1", [128, S], BF16),
        ("vh0", [128, NKC, DH], BF16), ("vh1", [128, NKC, DH], BF16),
        ("qn1", [1, TOK], F32))
    (x1_fm,) = r6.seq(("x1_fm", [128, NDC, TOK], BF16))

    r7 = Region(60416)
    wqs0, wqs1, wqs2, eff_bf, vtt = r7.seq(
        ("wqs0", [128, D], F32), ("wqs1", [128, D], F32),
        ("wqs2", [128, D], F32), ("eff_bf", [128, NDC, DH], BF16),
        ("vtt", [128, HKV, DH], BF16))
    wo0, wo1, strip0, strip1, strip2, strip3, wping = r7.seq(
        ("wo0", [128, HKV, 512], BF16), ("wo1", [128, HKV, 512], BF16),
        ("strip0", [128, 512], F32), ("strip1", [128, 512], F32),
        ("strip2", [128, 512], F32), ("strip3", [128, 512], F32),
        ("wping", [128, NDC, 512], BF16))
    a_bf, zp_bf, w8_bf, limt_bf, lim_bf, b8_bf, bmob0, bmob1, bob0, bob1 = \
        r7.seq(("a_bf", [128, NDC, RK], BF16), ("zp_bf", [RK, TOK], BF16),
               ("w8_bf", [128, NDC, E], BF16), ("limt_bf", [RK, E], BF16),
               ("lim_bf", [RK, RK], BF16), ("b8_bf", [128, E], BF16),
               ("bmob0", [RK, 512], BF16), ("bmob1", [RK, 512], BF16),
               ("bob0", [128, 512], BF16), ("bob1", [128, 512], BF16))

    assert cur[0] <= SB_END, f"SBUF over budget: {cur[0] - base} used"
    globals()["_SBUF_TOP"] = cur[0]

    pa = nc.alloc_psum_tensor("pa", [128, 1024], F32)
    pb = nc.alloc_psum_tensor("pb", [128, 1024], F32)
    pv = nc.alloc_psum_tensor("pv", [128, 1024], F32)
    pc = nc.alloc_psum_tensor("pc", [128, 512], F32)
    pd = nc.alloc_psum_tensor("pd", [128, 512], F32)
    ptb = pc.ap().bitcast(BF16)

    slots = [pa.ap()[:, 0:512], pa.ap()[:, 512:1024],
             pb.ap()[:, 0:512], pb.ap()[:, 512:1024]]
    eslots = slots + [pv.ap()[:, 0:512], pv.ap()[:, 512:1024]]

    dG = Led(nc, "dG", dma=True)
    dS = Led(nc, "dS", dma=True)
    dA = Led(nc, "dA", dma=True)
    dQ = Led(nc, "dQ", dma=True)
    dQ1 = Led(nc, "dQ1", dma=True)
    dQ2 = Led(nc, "dQ2", dma=True)
    dW = [Led(nc, "dW0", dma=True), Led(nc, "dW1", dma=True)]
    dKK = [Led(nc, "dKK0", dma=True), Led(nc, "dKK1", dma=True)]
    dKV = [Led(nc, "dKV0", dma=True), Led(nc, "dKV1", dma=True)]
    dWO = [Led(nc, "dWO0", dma=True), Led(nc, "dWO1", dma=True)]
    dWP = [Led(nc, "dWP0", dma=True), Led(nc, "dWP1", dma=True)]
    dA2 = Led(nc, "dA2", dma=True)
    cc = Led(nc, "cc")
    tP = Led(nc, "tP")
    vP = Led(nc, "vP")
    sP = Led(nc, "sP")
    vg = VGuard(vP)

    with nc.Block() as blk:
        @blk.vector
        def _(v):
            v = vg.bind(v)
            v.memset(ones128.ap(), 1.0)
            v.memset(csrc.ap(), 0.0)
            v.memset(scsrc.ap(), 0.0)
            v.memset(nsc2.ap()[0:8, 6:7], 0.0)

        @blk.sync
        def _(sy):
            dS.inc(sy.dma_start(out=ident.ap(), in_=id_d))
        ident_dma = dS.n

        @blk.scalar
        def _(s):
            s.preload_act_table(AF.Exp)

        @blk.gpsimd
        def _(g):
            for i in range(4):
                dG.inc(g.dma_start(out=xfm_bf.ap()[:, 8 * i:8 * (i + 1), :],
                                   in_=xfm_d[:, 8 * i:8 * (i + 1), :]))
        xfm_done = dG.n

        slot_free = [0, 0, 0, 0]
        slot_seq = [0]

        def next_slot():
            i = slot_seq[0] % 4
            slot_seq[0] += 1
            return i

        stage_consumed = [0, 0]
        kevac = []
        vevac = []

        def proj_pass(which, f, src_d, out_sb):
            par = f % 2
            stg = wkv0 if par == 0 else wkv1

            @blk.gpsimd
            def _(g):
                if stage_consumed[par]:
                    tP.wait(g, at=stage_consumed[par])
                dW[par].inc(g.dma_start(
                    out=stg.ap()[:, :, :], in_=src_d[f]))

            @blk.tensor
            def _(t):
                dW[par].wait(t)
                dG.wait(t, at=xfm_done)
                for tb in range(2):
                    si = next_slot()
                    if slot_free[si]:
                        vP.wait(t, at=slot_free[si])
                    for dc in range(NDC):
                        _tm(t, slots[si], stg.ap()[:, dc, :],
                                 xfm_bf.ap()[:, dc, 512 * tb:512 * (tb + 1)],
                                 start=(dc == 0), stop=(dc == NDC - 1))
                    tmark(t, tP)
            stage_consumed[par] = tP.n

            @blk.vector
            def _(v):
                v = vg.bind(v)
                for tb in range(2):
                    si = (slot_seq[0] - 2 + tb) % 4
                    tP.wait(v, at=tP.n - (1 - tb))
                    ins = v.tensor_copy(
                        out_sb.ap()[:, f, 512 * tb:512 * (tb + 1)], slots[si])
                    slot_free[si] = vP.n

        for f in range(HKV):
            proj_pass("k", f, wkf_d, k_bf)
            kevac.append(vP.n)

            @blk.sync
            def _(sy, f=f):
                vP.wait(sy, at=kevac[f])
                dS.inc(sy.dma_start(out=kown_d[f], in_=k_bf.ap()[:, f, :]))
        kown_done = dS.n

        @blk.gpsimd
        def _(g):
            dS.wait(g, at=kown_done)
            cc.inc(g.collective_compute(
                "AllGather", AL.bypass, replica_groups=NGROUPS,
                ins=[kown_d.ap().opt()], outs=[kall_d.ap().opt()]))
        cc_k = cc.n

        for f in range(HKV):
            proj_pass("v", f, wvf_d, v_fm)
            vevac.append(vP.n)

        vtt_ev = []
        vtt_store = []
        for tt in range(NT):
            @blk.tensor
            def _(t, tt=tt):
                if tt == 0:
                    vP.wait(t, at=vevac[-1])
                    dS.wait(t, at=kown_done)
                else:
                    vP.wait(t, at=vtt_ev[tt - 1])
                for h in range(HKV):
                    _tt(t, ptb[:, 128 * h:128 * (h + 1)],
                                v_fm.ap()[:, h, 128 * tt:128 * (tt + 1)],
                                ident.ap())
                tmark(t, tP)

            @blk.vector
            def _(v, tt=tt):
                v = vg.bind(v)
                tP.wait(v)
                if tt >= 1:
                    dS.wait(v, at=vtt_store[tt - 1])
                v.tensor_copy(
                    vtt.ap().rearrange("p h d -> p (h d)"), ptb[:, :])
                vtt_ev.append(vP.n)

            @blk.sync
            def _(sy, tt=tt):
                vP.wait(sy, at=vtt_ev[tt])
                vtt_store.append(dS.inc(sy.dma_start(
                    out=vown_d.ap()[:, 128 * tt:128 * (tt + 1), :]
                    .transpose([1, 0, 2]),
                    in_=vtt.ap()[:, :, :])))

        @blk.gpsimd
        def _(g):
            dS.wait(g, at=vtt_store[-1])
            cc.inc(g.collective_compute(
                "AllGather", AL.bypass, replica_groups=NGROUPS,
                ins=[vown_d.ap().opt()], outs=[vall_d.ap().opt()]))
        cc_v = cc.n

        qevac = []
        cast_done = [0]
        wqb = [wqs0, wqs1, wqs2]
        dQb = [dQ, dQ1, dQ2]
        buf_free_v = [0, 0, 0]
        buf_free_s = [0, 0, 0]
        add_led = {}

        def wq_ld(s, k, gi, bi):
            dQb[bi].inc(s.dma_start(
                out=wqb[bi].ap().rearrange("p (dc c) -> p dc c", c=128),
                in_=wqg_d[k, gi]))

        for kvh in range(HKV):
            ai = kvh % 3
            l1 = (kvh + 1) % 3
            l2 = (kvh + 2) % 3

            @blk.scalar
            def _(s, kvh=kvh, ai=ai, l1=l1, l2=l2):
                if buf_free_v[ai]:
                    vP.wait(s, at=buf_free_v[ai])
                wq_ld(s, kvh, 0, ai)
                if buf_free_v[l1]:
                    vP.wait(s, at=buf_free_v[l1])
                wq_ld(s, kvh, 1, l1)
                if buf_free_s[l2]:
                    sP.wait(s, at=buf_free_s[l2])
                wq_ld(s, kvh, 2, l2)
            g01 = (dQb[ai].n, dQb[l1].n)
            g2 = dQb[l2].n

            @blk.vector
            def _(v, kvh=kvh, ai=ai, l1=l1, g01=g01):
                v = vg.bind(v)
                dQb[ai].wait(v, at=g01[0])
                dQb[l1].wait(v, at=g01[1])
                v.tensor_add(wqb[ai].ap(), wqb[ai].ap(), wqb[l1].ap())
            add1 = vP.n

            @blk.scalar
            def _(s, kvh=kvh, l1=l1, add1=add1):
                vP.wait(s, at=add1)
                wq_ld(s, kvh, 3, l1)
            g3 = dQb[l1].n

            @blk.vector
            def _(v, kvh=kvh, ai=ai, l1=l1, l2=l2, g2=g2, g3=g3):
                v = vg.bind(v)
                dQb[l2].wait(v, at=g2)
                v.tensor_add(wqb[ai].ap(), wqb[ai].ap(), wqb[l2].ap())
                buf_free_v[l2] = vP.n
                dQb[l1].wait(v, at=g3)
                v.tensor_add(wqb[ai].ap(), wqb[ai].ap(), wqb[l1].ap())
                buf_free_v[l1] = vP.n
            add3 = vP.n

            @blk.scalar
            def _(s, kvh=kvh, ai=ai, add3=add3):
                vP.wait(s, at=add3)
                if kvh > 0:
                    tP.wait(s)
                cast_done[0] = sP.inc(s.copy(
                    eff_bf.ap().rearrange("p dc c -> p (dc c)"),
                    wqb[ai].ap()))
                buf_free_s[ai] = cast_done[0]

            @blk.tensor
            def _(t, kvh=kvh):
                sP.wait(t, at=cast_done[0])
                for tb in range(2):
                    si = next_slot()
                    if slot_free[si]:
                        vP.wait(t, at=slot_free[si])
                    for dc in range(NDC):
                        _tm(t, slots[si], eff_bf.ap()[:, dc, :],
                            xfm_bf.ap()[:, dc, 512 * tb:512 * (tb + 1)],
                            start=(dc == 0), stop=(dc == NDC - 1))
                    tmark(t, tP)

            @blk.vector
            def _(v, kvh=kvh):
                v = vg.bind(v)
                for tb in range(2):
                    si = (slot_seq[0] - 2 + tb) % 4
                    tP.wait(v, at=tP.n - (1 - tb))
                    ins = v.tensor_copy(
                        q_bf.ap()[:, kvh, 512 * tb:512 * (tb + 1)], slots[si])
                    slot_free[si] = vP.n
            qevac.append(vP.n)
        q_mm_last = tP.n
        wq_cast_last = cast_done[0]

        @blk.gpsimd
        def _(g):
            sP.wait(g, at=wq_cast_last)
            tP.wait(g, at=q_mm_last)
            dWO[0].inc(g.dma_start(out=wo0.ap()[:, :, :], in_=wo_d[0]))

        headlast = {}
        khdma = []
        vhdma = []
        for h in range(HKV):
            par = h % 2
            khb = kh0 if par == 0 else kh1
            vhb = vh0 if par == 0 else vh1

            @blk.sync
            def _(sy, h=h, par=par, khb=khb, vhb=vhb):
                cc.wait(sy, at=cc_v)
                if h < 2:
                    tP.wait(sy, at=q_mm_last)
                else:
                    tP.wait(sy, at=headlast[h - 2])
                for sh in range(4):
                    dKK[par].inc(sy.dma_start(
                        out=khb.ap()[:, TOK * sh:TOK * (sh + 1)],
                        in_=kall_d[sh * HKV + h]))
                khdma.append(dKK[par].n)
                for sh in range(4):
                    dKV[par].inc(sy.dma_start(
                        out=vhb.ap()[:, 8 * sh:8 * (sh + 1), :],
                        in_=vall_d[sh * HKV + h].rearrange(
                            "(kc p) dh -> p kc dh", p=128)))
                vhdma.append(dKV[par].n)

            @blk.scalar
            def _(s, h=h):
                vP.wait(s, at=qevac[h])
                sP.inc(s.activation(qsq.ap(), q_bf.ap()[:, h, :], AF.Square))
            qsq_done = sP.n

            @blk.tensor
            def _(t, h=h):
                sP.wait(t, at=qsq_done)
                if h > 0:
                    vP.wait(t, at=pcpd_free[h - 1])
                _tm(t, pc.ap()[:, :], ones128.ap(), qsq.ap()[:, 0:512],
                    start=True, stop=True)
                _tm(t, pd.ap()[:, :], ones128.ap(), qsq.ap()[:, 512:1024],
                    start=True, stop=True)
                tmark(t, tP)

            if h == 0:
                pcpd_free = {}

            @blk.vector
            def _(v, h=h):
                v = vg.bind(v)
                tP.wait(v)
                v.tensor_reduce(nsc.ap()[0:1, 0:1], pc.ap()[0:1, :], AX.X,
                                AL.max)
                v.tensor_reduce(nsc.ap()[0:1, 1:2], pd.ap()[0:1, :], AX.X,
                                AL.max)
                v.tensor_max(nsc.ap()[0:1, 0:1], nsc.ap()[0:1, 0:1],
                                    nsc.ap()[0:1, 1:2])
            qn_red = vP.n

            @blk.scalar
            def _(s, h=h, par=par, khb=khb):
                dKK[par].wait(s, at=khdma[h])
                sP.inc(s.activation(ksq.ap(), khb.ap()[:, :], AF.Square))
            ksq_done = sP.n

            kn_red = []
            for piece in range(8):
                @blk.tensor
                def _(t, h=h, piece=piece):
                    if piece == 0:
                        sP.wait(t, at=ksq_done)
                        vP.wait(t, at=qn_red)
                    if piece >= 2:
                        vP.wait(t, at=kn_red[piece - 2])
                    tgt = pc if piece % 2 == 0 else pd
                    _tm(t, tgt.ap()[:, :], ones128.ap(),
                        ksq.ap()[:, 512 * piece:512 * (piece + 1)],
                        start=True, stop=True)
                    tmark(t, tP)

                @blk.vector
                def _(v, h=h, piece=piece):
                    v = vg.bind(v)
                    tP.wait(v)
                    tgt = pc if piece % 2 == 0 else pd
                    v.tensor_reduce(nsc.ap()[0:1, 2:3], tgt.ap()[0:1, :],
                                    AX.X, AL.max)
                    if piece == 0:
                        ins = v.tensor_copy(nsc.ap()[0:1, 3:4],
                                            nsc.ap()[0:1, 2:3])
                    else:
                        ins = v.tensor_max(nsc.ap()[0:1, 3:4],
                                           nsc.ap()[0:1, 3:4],
                                           nsc.ap()[0:1, 2:3])
                    kn_red.append(vP.n)

            @blk.vector
            def _(v, h=h):
                v = vg.bind(v)
                v.tensor_add(nsc.ap()[0:1, 4:5], nsc.ap()[0:1, 0:1],
                             nsc.ap()[0:1, 3:4])
                v.tensor_scalar(csrc.ap()[0:1, 0:1],
                                       nsc.ap()[0:1, 4:5],
                                       1.02 * 0.5 * INV_SQRT_DH, None,
                                       AL.mult)

            @blk.tensor
            def _(t, h=h):
                vP.wait(t)
                _tm(t, pc.ap()[:, 0:2], ones128.ap(), csrc.ap(),
                    start=True, stop=True)
                tmark(t, tP)

            @blk.vector
            def _(v, h=h):
                v = vg.bind(v)
                tP.wait(v)
                v.tensor_scalar(cbias.ap(), pc.ap()[:, 0:1], -1.0,
                                       None, AL.mult)
            cb_ready = vP.n

            exp_done = {}
            pt_used = {}
            for kc in range(NKC):
                sb = pa if kc % 2 == 0 else pb
                ptbuf = pt0 if kc % 2 == 0 else pt1

                @blk.tensor
                def _(t, h=h, kc=kc, sb=sb, khb=khb, par=par):
                    if kc < 2:
                        vP.wait(t, at=cb_ready)
                        dKK[par].wait(t, at=khdma[h])
                    else:
                        sP.wait(t, at=exp_done[kc - 2])
                    for half in range(2):
                        _tm(t, sb.ap()[:, 512 * half:512 * (half + 1)],
                                 khb.ap()[:, 128 * kc:128 * (kc + 1)],
                                 q_bf.ap()[:, h, 512 * half:512 * (half + 1)],
                                 start=True, stop=True,
                                 skip_group_check=True)
                    tmark(t, tP)

                @blk.scalar
                def _(s, h=h, kc=kc, sb=sb, ptbuf=ptbuf):
                    tP.wait(s)
                    if kc >= 2:
                        tP.wait(s, at=pt_used[kc - 2])
                    exp_done[kc] = sP.inc(s.activation(
                        ptbuf.ap(), sb.ap()[:, :], AF.Exp,
                        bias=cbias.ap(), scale=INV_SQRT_DH))

                @blk.tensor
                def _(t, h=h, kc=kc, ptbuf=ptbuf, vhb=vhb, par=par):
                    sP.wait(t, at=exp_done[kc])
                    if kc == 0:
                        dKV[par].wait(t, at=vhdma[h])
                    _tm(t, pc.ap()[:, :], ones128.ap(),
                        ptbuf.ap()[:, 0:512],
                        start=(kc == 0), stop=(kc == NKC - 1),
                        skip_group_check=True)
                    _tm(t, pd.ap()[:, :], ones128.ap(),
                        ptbuf.ap()[:, 512:1024],
                        start=(kc == 0), stop=(kc == NKC - 1),
                        skip_group_check=True)
                    for half in range(2):
                        _tm(t, pv.ap()[:, 512 * half:512 * (half + 1)],
                                 vhb.ap()[:, kc, :],
                                 ptbuf.ap()[:, 512 * half:512 * (half + 1)],
                                 start=(kc == 0), stop=(kc == NKC - 1),
                                 skip_group_check=True)
                    pt_used[kc] = tmark(t, tP)

            @blk.vector
            def _(v, h=h):
                v = vg.bind(v)
                tP.wait(v)
                v.reciprocal(recipb.ap()[:, 0:512], pc.ap()[:, :])
                v.reciprocal(recipb.ap()[:, 512:1024], pd.ap()[:, :])
                v.tensor_mul(val_bf.ap()[:, h, :], pv.ap()[:, :],
                                    recipb.ap())
            pcpd_free[h] = vP.n
            headlast[h] = tP.n
        attn_done = vP.n

        for _i in range(4):
            slot_free[_i] = attn_done
        strips = [strip0, strip1, strip2, strip3]
        d_mm_done = {}
        d7_evac = {}
        half_spill_total = {}
        x1_spill = []
        df_consumed = {}
        df_norm = {}

        def emit_d_half(hf):
            dAh = dA if hf == 0 else dA2
            tiles = range(4 * hf, 4 * hf + 4)
            for ob in range(NOB):
                k = hf * NOB + ob
                par = k % 2
                wob = wo0 if par == 0 else wo1
                if k + 1 < 2 * NOB:
                    @blk.gpsimd
                    def _(g, k=k):
                        if k >= 1:
                            tP.wait(g, at=d_mm_done[k - 1])
                        dWO[(k + 1) % 2].inc(g.dma_start(
                            out=(wo1 if k % 2 == 0 else wo0).ap()[:, :, :],
                            in_=wo_d[(k + 1) % NOB]))

                for tt in tiles:
                    @blk.tensor
                    def _(t, ob=ob, tt=tt, wob=wob, par=par, k=k):
                        if tt == 4 * hf:
                            dWO[par].wait(t)
                            if k == 0:
                                vP.wait(t, at=attn_done)
                        si = next_slot()
                        if slot_free[si]:
                            vP.wait(t, at=slot_free[si])
                        for kc in range(HKV):
                            _tm(t, slots[si],
                                val_bf.ap()[:, kc, 128 * tt:128 * (tt + 1)],
                                wob.ap()[:, kc, :],
                                start=(kc == 0), stop=(kc == HKV - 1))
                        tmark(t, tP)

                    if ob < 7:
                        @blk.vector
                        def _(v, ob=ob, tt=tt, dAh=dAh):
                            v = vg.bind(v)
                            si = (slot_seq[0] - 1) % 4
                            sbi = (ob * 4 + (tt % 4)) % 4
                            tP.wait(v)
                            dAh.wait(v)
                            v.tensor_copy(strips[sbi].ap(), slots[si])
                            slot_free[si] = vP.n

                        @blk.scalar
                        def _(s, ob=ob, tt=tt, dAh=dAh):
                            sbi = (ob * 4 + (tt % 4)) % 4
                            vP.wait(s)
                            dAh.inc(s.dma_start(
                                out=attn_d[128 * tt:128 * (tt + 1),
                                           512 * ob:512 * (ob + 1)],
                                in_=strips[sbi].ap()))
                    else:
                        @blk.vector
                        def _(v, ob=ob, tt=tt):
                            v = vg.bind(v)
                            si = (slot_seq[0] - 1) % 4
                            tP.wait(v)
                            v.tensor_copy(att7.ap()[:, tt, :], slots[si])
                            slot_free[si] = vP.n
                            d7_evac[tt] = vP.n
                d_mm_done[k] = tP.n
            half_spill_total[hf] = dAh.n

        def emit_df_chain(tt):
            hf = tt // 4
            dAh = dA if hf == 0 else dA2
            lns = lnsum.ap()[:, 2 * (tt % 8):2 * (tt % 8) + 2]
            lnmt = lnm.ap()[:, 4 * (tt % 8):4 * (tt % 8) + 4]
            scr = nsc2.ap()[:, 0:1] if False else None
            x1bf = df_x1bfA if tt % 2 == 0 else df_x1bfB

            @blk.sync
            def _(sy, tt=tt, dAh=dAh):
                dAh.wait(sy, at=half_spill_total[hf])
                vP.wait(sy, at=d7_evac[tt])
                if tt >= 1:
                    sP.wait(sy, at=df_consumed[tt - 1])
                    dS.wait(sy, at=x1_spill[-1])
                dS.inc(sy.dma_start(
                    out=df_acc.ap()[:, 0:3584],
                    in_=attn_d[128 * tt:128 * (tt + 1), :]))
                dS.inc(sy.dma_start(out=df_xtm.ap(),
                                    in_=xtm_d[128 * tt:128 * (tt + 1), :]))
            df_ld = dS.n

            @blk.vector
            def _(v, tt=tt, df_ld=df_ld):
                v = vg.bind(v)
                dS.wait(v, at=df_ld)
                v.tensor_add(df_acc.ap()[:, 0:3584], df_acc.ap()[:, 0:3584],
                             df_xtm.ap()[:, 0:3584])
                v.tensor_add(df_acc.ap()[:, 3584:4096],
                             att7.ap()[:, tt, :], df_xtm.ap()[:, 3584:4096])

            @blk.scalar
            def _(s, tt=tt, lns=lns):
                vP.wait(s)
                if tt == 0:
                    s.preload_act_table(AF.Sqrt)
                s.activation(df_xtm.ap(), df_acc.ap(), AF.Copy,
                             accum_out=lns[:, 0:1])
                sP.inc(s.activation(df_xtm.ap(), df_acc.ap(), AF.Square,
                                    accum_out=lns[:, 1:2]))

            @blk.vector
            def _(v, tt=tt, lns=lns, lnmt=lnmt):
                v = vg.bind(v)
                sP.wait(v)
                v.tensor_scalar(lnmt[:, 0:1], lns[:, 0:1], 1.0 / D, None,
                                AL.mult)
                v.tensor_scalar(lnmt[:, 1:2], lns[:, 1:2], 1.0 / D, None,
                                AL.mult)
                v.tensor_mul(lnmt[:, 2:3], lnmt[:, 0:1], lnmt[:, 0:1])
                v.tensor_sub(lnmt[:, 1:2], lnmt[:, 1:2], lnmt[:, 2:3])
                v.tensor_scalar(lnmt[:, 1:2], lnmt[:, 1:2], 1.0, LN_EPS,
                                AL.mult, AL.add)

            @blk.scalar
            def _(s, tt=tt, lnmt=lnmt):
                vP.wait(s)
                sP.inc(s.activation(lnmt[:, 2:3], lnmt[:, 1:2], AF.Sqrt))

            @blk.vector
            def _(v, tt=tt, lnmt=lnmt):
                v = vg.bind(v)
                sP.wait(v)
                v.reciprocal(lnmt[:, 2:3], lnmt[:, 2:3])
                v.tensor_mul(lnmt[:, 3:4], lnmt[:, 0:1], lnmt[:, 2:3])
                v.tensor_scalar(lnmt[:, 3:4], lnmt[:, 3:4], -1.0, None,
                                AL.mult)

            @blk.vector
            def _(v, tt=tt, lnmt=lnmt):
                v = vg.bind(v)
                if tt >= 1:
                    dS.wait(v, at=x1_spill[-1])
                v.tensor_scalar(df_xtm.ap(), df_acc.ap(), lnmt[:, 2:3],
                                lnmt[:, 3:4], AL.mult, AL.add)
                df_norm[tt] = vP.n

            @blk.scalar
            def _(s, tt=tt, x1bf=x1bf):
                vP.wait(s, at=df_norm[tt])
                if tt >= 2:
                    tP.wait(s, at=x1t_done[tt - 2])
                df_consumed[tt] = sP.inc(s.copy(x1bf.ap(), df_xtm.ap()))

            @blk.sync
            def _(sy, tt=tt):
                sP.wait(sy, at=df_consumed[tt])
                x1_spill.append(dS.inc(sy.dma_start(
                    out=x1sp_d[128 * tt:128 * (tt + 1), :],
                    in_=df_xtm.ap())))

        x1t_done = {}

        def emit_x1fm_transpose(tt):
            x1bf = df_x1bfA if tt % 2 == 0 else df_x1bfB
            ptv = pc.ap().bitcast(BF16).rearrange("p (j c) -> p j c", c=128)
            for batch in range(4):
                @blk.tensor
                def _(t, tt=tt, batch=batch, x1bf=x1bf):
                    if batch == 0:
                        sP.wait(t, at=df_consumed[tt])
                    if (tt, batch) != (0, 0):
                        vP.wait(t, at=x1t_evac[-1])
                    for j in range(8):
                        dc = 8 * batch + j
                        _tt(t, ptb[0:128, 128 * j:128 * (j + 1)],
                            x1bf.ap()[:, 128 * dc:128 * (dc + 1)],
                            ident.ap())
                    tmark(t, tP)

                @blk.vector
                def _(v, tt=tt, batch=batch):
                    v = vg.bind(v)
                    tP.wait(v)
                    v.tensor_copy(
                        x1_fm.ap()[:, 8 * batch:8 * (batch + 1),
                                   128 * tt:128 * (tt + 1)], ptv)
                    x1t_evac.append(vP.n)
            x1t_done[tt] = tP.n

        x1t_evac = []
        emit_d_half(0)
        emit_df_chain(0)
        emit_df_chain(1)
        emit_x1fm_transpose(0)
        emit_x1fm_transpose(1)
        emit_df_chain(2)
        emit_df_chain(3)
        emit_d_half(1)
        d_t_last = tP.n
        d_spill_all = dA.n
        d2_spill_all = dA2.n
        emit_x1fm_transpose(2)
        emit_x1fm_transpose(3)
        x1fm_h1 = vP.n

        @blk.vector
        def _(v):
            v = vg.bind(v)
            tP.wait(v, at=d_t_last)
            v.memset(b8_bf.ap(), 0.0)
            v.memset(bob0.ap(), 0.0)
            v.memset(bob1.ap(), 0.0)
            v.memset(lim_bf.ap(), 0.0)
        e_zero = vP.n

        @blk.gpsimd
        def _(g):
            tP.wait(g, at=q_mm_last)
            dWP[0].inc(g.dma_start(out=wping.ap()[:, :, :], in_=wt_d[0]))

        @blk.gpsimd
        def _(g):
            tP.wait(g, at=d_t_last)
            vP.wait(g, at=e_zero)
            dG.inc(g.dma_start(out=a_bf.ap()[:, :, :], in_=a_d))
            dG.inc(g.dma_start(out=w8_bf.ap()[:, :, :], in_=w8_d))
            dG.inc(g.dma_start(out=limt_bf.ap(), in_=limt_d))
            dG.inc(g.dma_start(out=lim_bf.ap()[0:E, :], in_=lim_d))
            dG.inc(g.dma_start(out=b8_bf.ap()[0:1, :], in_=b_d[:, 0:E]))
            dWP[0].inc(g.dma_start(out=bmob0.ap(), in_=bm_d[:, 0:512]))
            dWP[0].inc(g.dma_start(out=bob0.ap()[0:1, :], in_=b_d[:, 0:512]))
        e_small = dG.n
        e_wp0 = dWP[0].n

        z_ev = {}

        def emit_z(hf):
            @blk.tensor
            def _(t, hf=hf):
                if hf == 0:
                    vP.wait(t, at=x1fm_h1)
                    dG.wait(t, at=e_small)
                else:
                    vP.wait(t, at=x1fm_done)
                for dc in range(NDC):
                    _tm(t, pv.ap()[0:RK, 512 * hf:512 * (hf + 1)],
                        a_bf.ap()[:, dc, :],
                        x1_fm.ap()[:, dc, 512 * hf:512 * (hf + 1)],
                        start=(dc == 0), stop=(dc == NDC - 1))
                tmark(t, tP)

            @blk.vector
            def _(v, hf=hf):
                v = vg.bind(v)
                tP.wait(v)
                v.tensor_copy(z_s.ap()[:, 512 * hf:512 * (hf + 1)],
                              pv.ap()[0:RK, 512 * hf:512 * (hf + 1)])
                v.tensor_copy(z_bf.ap()[:, 512 * hf:512 * (hf + 1)],
                              z_s.ap()[:, 512 * hf:512 * (hf + 1)])
                z_ev[hf] = vP.n

        e_spill_at = {}
        e_slot_free = [0] * 6
        strip_rd2 = [0, 0, 0, 0]
        e_mm_done = {}
        e7_evac = {}
        e_half_total = {}
        held_open = {}

        def e2_slot(tt):
            return tt % 6

        def emit_e2_hmm(tt, wbuf, hold):
            si = e2_slot(tt)

            @blk.tensor
            def _(t, tt=tt, wbuf=wbuf, si=si, hold=hold):
                if e_slot_free[si]:
                    vP.wait(t, at=e_slot_free[si])
                for dc in range(NDC):
                    _tm(t, eslots[si],
                        x1_fm.ap()[:, dc, 128 * tt:128 * (tt + 1)],
                        wbuf.ap()[:, dc, :],
                        start=(dc == 0), stop=False, skip_group_check=True)
                tmark(t, tP)
            held_open[tt] = True

        def emit_e2_close(tt, ob, bmob, bob, dAh):
            si = e2_slot(tt)

            @blk.tensor
            def _(t, tt=tt, si=si, bmob=bmob, bob=bob):
                _tm(t, eslots[si], ones128.ap(), bob.ap(),
                    start=False, stop=False, skip_group_check=True)
                _tm(t, eslots[si],
                    zp_bf.ap()[:, 128 * tt:128 * (tt + 1)],
                    bmob.ap(), start=False, stop=True, skip_group_check=True)
                tmark(t, tP)

            if ob < 7:
                @blk.vector
                def _(v, tt=tt, ob=ob, si=si, dAh=dAh):
                    v = vg.bind(v)
                    sbi = (ob * 4 + (tt % 4)) % 4
                    tP.wait(v)
                    dAh.wait(v)
                    if dAh is dA:
                        dA2.wait(v, at=d2_spill_all)
                    else:
                        dA.wait(v, at=e_half_total[0])
                    v.tensor_copy(strips[sbi].ap(), eslots[si])
                    e_slot_free[si] = vP.n

                @blk.scalar
                def _(s, tt=tt, ob=ob, dAh=dAh):
                    sbi = (ob * 4 + (tt % 4)) % 4
                    vP.wait(s)
                    dAh.inc(s.dma_start(
                        out=hdb_d[128 * tt:128 * (tt + 1),
                                  512 * ob:512 * (ob + 1)],
                        in_=strips[sbi].ap()))
                e_spill_at[(ob, tt)] = dAh.n
            else:
                @blk.vector
                def _(v, tt=tt, si=si):
                    v = vg.bind(v)
                    tP.wait(v)
                    v.tensor_copy(hd7.ap()[:, tt, :], eslots[si])
                    e_slot_free[si] = vP.n
                    e7_evac[tt] = vP.n

        @blk.tensor
        def _(t):
            dWP[0].wait(t, at=e_wp0)
            vP.wait(t, at=x1fm_h1)
        for tt in range(4):
            emit_e2_hmm(tt, wping, hold=True)
        emit_z(0)

        for tt in range(4, NT):
            emit_df_chain(tt)
            emit_x1fm_transpose(tt)
        x1fm_done = vP.n
        emit_z(1)

        @blk.tensor
        def _(t):
            vP.wait(t, at=max(z_ev[0], z_ev[1]))
            x1r = x1_fm.ap().rearrange("p dc (at w) -> p dc at w", w=256)
            zr = z_bf.ap().rearrange("r (at w) -> r at w", w=256)
            for at_ in range(4):
                for dc in range(NDC):
                    _tm(t, pc.ap()[:, 8 * at_:8 * (at_ + 1)],
                        x1r[:, dc, at_, 1::2], w8_bf.ap()[:, dc, :],
                        start=(dc == 0), stop=False, skip_group_check=True)
                _tm(t, pc.ap()[:, 8 * at_:8 * (at_ + 1)],
                    ones128.ap(), b8_bf.ap(), start=False, stop=True,
                    skip_group_check=True)
                _tm(t, pd.ap()[:, 8 * at_:8 * (at_ + 1)],
                    zr[:, at_, 1::2], limt_bf.ap(),
                    start=True, stop=True, skip_group_check=True)
            tmark(t, tP)

        @blk.vector
        def _(v):
            v = vg.bind(v)
            tP.wait(v)
            v.memset(wgt.ap().rearrange("p a r -> p (a r)"), 0.0)
            v.tensor_copy(h8_s.ap().rearrange("p a e -> p (a e)"),
                          pc.ap()[:, 0:32])
            v.tensor_copy(ds_s.ap().rearrange("p a e -> p (a e)"),
                          pd.ap()[:, 0:32])
            v.tensor_reduce(nsc2.ap()[:, 4:5],
                            h8_s.ap().rearrange("p a e -> p (a e)"),
                            AX.X, AL.max, apply_absolute_value=True)
            v.tensor_reduce(nsc2.ap()[:, 5:6],
                            ds_s.ap().rearrange("p a e -> p (a e)"),
                            AX.X, AL.max, apply_absolute_value=True)

        @blk.sync
        def _(sy):
            vP.wait(sy)
            dS.inc(sy.dma_start(out=scpr_d[:, :], in_=nsc2.ap()[:, 4:6]))
            dS.wait(sy)
            dS.inc(sy.dma_start(out=mx2.ap(),
                                in_=scpr_d.ap().transpose([1, 0]),
                                allow_slow_non_contiguous=True))

        @blk.vector
        def _(v):
            v = vg.bind(v)
            dS.wait(v)
            v.tensor_reduce(nsc2.ap()[0:2, 6:7], mx2.ap(), AX.X, AL.max)

        @blk.sync
        def _(sy):
            vP.wait(sy)
            dS.inc(sy.dma_start(out=scsrc_d.ap(), in_=nsc2.ap()[0:8, 6:7]))

        @blk.gpsimd
        def _(g):
            dS.wait(g)
            cc.inc(g.collective_compute(
                "AllReduce", AL.max, replica_groups=ALLG,
                ins=[scsrc_d.ap().opt()], outs=[scdst_d.ap().opt()]))

        @blk.gpsimd
        def _(g):
            cc.wait(g)
            dG.inc(g.dma_start(
                out=scsrc.ap()[0:1, :],
                in_=scdst_d.ap()[0:2].rearrange("(a b) -> a b", a=1)))

        @blk.tensor
        def _(t):
            dG.wait(t)
            _tm(t, pc.ap()[:, 0:2], ones128.ap(), scsrc.ap(),
                start=True, stop=True, skip_group_check=True)
            tmark(t, tP)

        @blk.vector
        def _(v):
            v = vg.bind(v)
            tP.wait(v)
            v.tensor_scalar_max(recips.ap(), pc.ap()[:, 0:2], SC_EPS)
            v.reciprocal(recips.ap(), recips.ap())

        wt_done = []
        for at_ in range(4):
            @blk.vector
            def _(v, at_=at_):
                v = vg.bind(v)
                v.tensor_scalar(gt1.ap(), h8_s.ap()[:, at_, :],
                                recips.ap()[:, 0:1], 0.5, AL.mult, AL.mult)
                v.tensor_scalar(gt2.ap(), ds_s.ap()[:, at_, :],
                                recips.ap()[:, 1:2], 0.5, AL.mult, AL.mult)
                v.tensor_add(gt1.ap(), gt1.ap(), gt2.ap())
                v.tensor_reduce(nsc2.ap()[:, 7:8], gt1.ap(), AX.X, AL.max)
                v.tensor_scalar(nsc2.ap()[:, 8:9], nsc2.ap()[:, 7:8],
                                -1.0, None, AL.mult)

            @blk.scalar
            def _(s, at_=at_):
                vP.wait(s)
                if at_ == 0:
                    s.preload_act_table(AF.Exp)
                sP.inc(s.activation(gt2.ap(), gt1.ap(), AF.Exp,
                                    bias=nsc2.ap()[:, 8:9]))

            @blk.vector
            def _(v, at_=at_):
                v = vg.bind(v)
                sP.wait(v)
                v.tensor_reduce(nsc2.ap()[:, 9:10], gt2.ap(), AX.X, AL.add)
                v.reciprocal(nsc2.ap()[:, 10:11], nsc2.ap()[:, 9:10])
                v.tensor_scalar(gt1.ap(), gt2.ap(), nsc2.ap()[:, 10:11], None,
                                AL.mult)
                v.max(out=gt3.ap(), in_=gt1.ap())
                v.tensor_scalar(nsc2.ap()[:, 11:12], gt3.ap()[:, 1:2],
                                -1.0, None, AL.mult)

            @blk.scalar
            def _(s, at_=at_):
                vP.wait(s)
                sP.inc(s.activation(gt2.ap(), gt1.ap(), AF.Tanh,
                                    bias=nsc2.ap()[:, 11:12], scale=1.0))

            @blk.vector
            def _(v, at_=at_):
                v = vg.bind(v)
                sP.wait(v)
                v.tensor_scalar(gt2.ap(), gt2.ap(), 0.5, 0.5, AL.mult, AL.add)
                v.tensor_mul(gt1.ap(), gt1.ap(), gt2.ap())
                v.tensor_reduce(nsc2.ap()[:, 12:13], gt1.ap(), AX.X, AL.add)
                v.tensor_scalar(nsc2.ap()[:, 12:13], nsc2.ap()[:, 12:13], 1.0,
                                1e-9, AL.mult, AL.add)
                v.reciprocal(nsc2.ap()[:, 13:14], nsc2.ap()[:, 12:13])
                v.tensor_scalar(
                    wgt.ap()[:, at_, 0:E], gt1.ap(), nsc2.ap()[:, 13:14],
                    None, AL.mult)
                wt_done.append(vP.n)

        @blk.tensor
        def _(t):
            vP.wait(t, at=wt_done[-1])
            for at_ in range(4):
                _tt(t, ptb[0:RK, 128 * at_:128 * (at_ + 1)],
                    wgt.ap()[:, at_, :], ident.ap())
            tmark(t, tP)

        @blk.vector
        def _(v):
            v = vg.bind(v)
            tP.wait(v)
            v.tensor_copy(wem.ap(), ptb[0:RK, 0:512])

        @blk.tensor
        def _(t):
            vP.wait(t)
            _tm(t, pd.ap()[0:RK, :], lim_bf.ap(), wem.ap(),
                start=True, stop=True, skip_group_check=True)
            tmark(t, tP)

        @blk.vector
        def _(v):
            v = vg.bind(v)
            tP.wait(v)
            v.tensor_copy(pmix_s.ap(), pd.ap()[0:RK, :])
            v.tensor_copy(pmix_tok.ap()[:, 0::2], pmix_s.ap())
            v.tensor_copy(pmix_tok.ap()[:, 1::2], pmix_s.ap())
            v.scalar_tensor_tensor(zp_bf.ap(), z_s.ap(), 0.5,
                                   pmix_tok.ap(), AL.mult, AL.mult)
        zp_ready = vP.n

        @blk.tensor
        def _(t):
            vP.wait(t, at=zp_ready)

        def emit_e2_half(hf, f_after=None, inline_f=False):
            dAh = dA if hf == 0 else dA2
            tiles = list(range(4 * hf, 4 * hf + 4))
            for ob in range(NOB):
                k = hf * NOB + ob
                par = k % 2
                wbuf = wping if par == 0 else wpong
                bmob = bmob0 if par == 0 else bmob1
                bob = bob0 if par == 0 else bob1
                if k + 1 < 2 * NOB:
                    @blk.gpsimd
                    def _(g, k=k):
                        if k >= 1:
                            tP.wait(g, at=e_mm_done[k - 1])
                        np_ = (k + 1) % 2
                        nob = (k + 1) % NOB
                        dWP[np_].inc(g.dma_start(
                            out=(wpong if np_ else wping).ap()[:, :, :],
                            in_=wt_d[nob]))
                        dWP[np_].inc(g.dma_start(
                            out=(bmob1 if np_ else bmob0).ap(),
                            in_=bm_d[:, 512 * nob:512 * (nob + 1)]))
                        dWP[np_].inc(g.dma_start(
                            out=(bob1 if np_ else bob0).ap()[0:1, :],
                            in_=b_d[:, 512 * nob:512 * (nob + 1)]))

                for tt in tiles:
                    @blk.tensor
                    def _(t, tt=tt, par=par, k=k):
                        if tt == tiles[0]:
                            dWP[par].wait(t)
                    if not (k == 0 and held_open.get(tt)):
                        emit_e2_hmm(tt, wbuf, hold=False)
                    emit_e2_close(tt, ob, bmob, bob, dAh)
                    held_open[tt] = False
                    if inline_f and ob == NOB - 1:
                        emit_f(tt)
                e_mm_done[k] = tP.n
                if f_after and ob in f_after:
                    emit_f(f_after[ob])
            e_half_total[hf] = dAh.n

        y_out = []
        f_consumed = {}

        def emit_f(tt):
            hf = tt // 4
            dAh = dA if hf == 0 else dA2
            lns = lnsum.ap()[:, 2 * tt:2 * tt + 2]
            lnmt = lnm.ap()[:, 4 * tt:4 * tt + 4]

            spill_wait = (e_half_total[hf] if hf in e_half_total
                          else e_spill_at[(NOB - 2, tt)])

            @blk.sync
            def _(sy, tt=tt, dAh=dAh, spill_wait=spill_wait):
                dAh.wait(sy, at=spill_wait)
                vP.wait(sy, at=e7_evac[tt])
                if y_out:
                    vP.wait(sy, at=f_consumed[max(f_consumed)])
                    dA.wait(sy, at=y_out[-1])
                dS.inc(sy.dma_start(out=f_x1.ap(),
                                    in_=x1sp_d[128 * tt:128 * (tt + 1), :]))
                dS.inc(sy.dma_start(out=f_hdb.ap()[:, 0:3584],
                                    in_=hdb_d[128 * tt:128 * (tt + 1), :]))
            f_ld = dS.n

            @blk.vector
            def _(v, tt=tt, f_ld=f_ld):
                v = vg.bind(v)
                dS.wait(v, at=f_ld)
                v.tensor_add(f_hdb.ap()[:, 0:3584], f_hdb.ap()[:, 0:3584],
                             f_x1.ap()[:, 0:3584])
                v.tensor_add(f_hdb.ap()[:, 3584:4096], hd7.ap()[:, tt, :],
                             f_x1.ap()[:, 3584:4096])

            @blk.scalar
            def _(s, tt=tt, lns=lns):
                vP.wait(s)
                if tt == 0:
                    s.preload_act_table(AF.Sqrt)
                s.activation(f_x1.ap(), f_hdb.ap(), AF.Copy,
                             accum_out=lns[:, 0:1])
                sP.inc(s.activation(f_x1.ap(), f_hdb.ap(), AF.Square,
                                    accum_out=lns[:, 1:2]))

            @blk.vector
            def _(v, tt=tt, lns=lns, lnmt=lnmt):
                v = vg.bind(v)
                sP.wait(v)
                v.tensor_scalar(lnmt[:, 0:1], lns[:, 0:1], 1.0 / D, None,
                                AL.mult)
                v.tensor_scalar(lnmt[:, 1:2], lns[:, 1:2], 1.0 / D, None,
                                AL.mult)
                v.tensor_mul(lnmt[:, 2:3], lnmt[:, 0:1], lnmt[:, 0:1])
                v.tensor_sub(lnmt[:, 1:2], lnmt[:, 1:2], lnmt[:, 2:3])
                v.tensor_scalar(lnmt[:, 1:2], lnmt[:, 1:2], 1.0, LN_EPS,
                                AL.mult, AL.add)

            @blk.scalar
            def _(s, tt=tt, lnmt=lnmt):
                vP.wait(s)
                sP.inc(s.activation(lnmt[:, 2:3], lnmt[:, 1:2], AF.Sqrt))

            @blk.vector
            def _(v, tt=tt, lnmt=lnmt):
                v = vg.bind(v)
                sP.wait(v)
                v.reciprocal(lnmt[:, 2:3], lnmt[:, 2:3])
                v.tensor_mul(lnmt[:, 3:4], lnmt[:, 0:1], lnmt[:, 2:3])
                v.tensor_scalar(lnmt[:, 3:4], lnmt[:, 3:4], -1.0, None,
                                AL.mult)
                v.tensor_scalar(f_x1.ap(), f_hdb.ap(), lnmt[:, 2:3],
                                lnmt[:, 3:4], AL.mult, AL.add)
                f_consumed[tt] = vP.n

            @blk.scalar
            def _(s, tt=tt):
                vP.wait(s, at=f_consumed[tt])
                y_out.append(dA.inc(s.dma_start(
                    out=y_d[128 * tt:128 * (tt + 1), :], in_=f_x1.ap())))

        emit_e2_half(0)
        emit_e2_half(1, f_after={0: 0, 2: 1, 4: 2, 6: 3}, inline_f=True)

        @blk.sync
        def _(sy):
            dA.wait(sy, at=y_out[-1])

        if taps:
            tp_kall = nc.dram_tensor("tp_kall", [4 * HKV, DH, TOK], BF16,
                                     kind="ExternalOutput").ap()
            tp_vall = nc.dram_tensor("tp_vall", [4 * HKV, TOK, DH], BF16,
                                     kind="ExternalOutput").ap()
            tp_att = nc.dram_tensor("tp_att", [TOK, 7 * 512], F32,
                                    kind="ExternalOutput").ap()
            tp_x1 = nc.dram_tensor("tp_x1", [TOK, D], F32,
                                   kind="ExternalOutput").ap()
            tp_hdb = nc.dram_tensor("tp_hdb", [TOK, 7 * 512], F32,
                                    kind="ExternalOutput").ap()
            tp_sc = nc.dram_tensor("tp_sc", [128, 2], F32,
                                   kind="ExternalOutput").ap()
            tp_zp = nc.dram_tensor("tp_zp", [RK, TOK], BF16,
                                   kind="ExternalOutput").ap()

            @blk.sync
            def _(sy):
                dA.wait(sy, at=y_out[-1])
                dS.inc(sy.dma_start(out=tp_kall, in_=kall_d.ap()))
                dS.inc(sy.dma_start(out=tp_vall, in_=vall_d.ap()))
                dS.inc(sy.dma_start(out=tp_att, in_=attn_d.ap()))
                dS.inc(sy.dma_start(out=tp_x1, in_=x1sp_d.ap()))
                dS.inc(sy.dma_start(out=tp_hdb, in_=hdb_d.ap()))
                dS.inc(sy.dma_start(out=tp_sc, in_=recips.ap()))
                dS.inc(sy.dma_start(out=tp_zp, in_=zp_bf.ap()))
                dS.wait(sy)

    return nc


_CACHE = {}


def kernel(L0_attn_x, L0_attn_wq, L0_attn_wk, L0_attn_wv, L0_attn_wo,
           L0_ffn_W, L0_ffn_b, L0_ffn_A, L0_ffn_Bmat, L0_ffn_LiMEs):
    x = np.ascontiguousarray(np.asarray(L0_attn_x, dtype=np.float32))
    wq = np.ascontiguousarray(np.asarray(L0_attn_wq, dtype=np.float32))
    wk = np.ascontiguousarray(np.asarray(L0_attn_wk, dtype=np.float32))
    wv = np.ascontiguousarray(np.asarray(L0_attn_wv, dtype=np.float32))
    wo = np.ascontiguousarray(np.asarray(L0_attn_wo, dtype=np.float32))
    W = np.ascontiguousarray(np.asarray(L0_ffn_W, dtype=np.float32))
    bvec = np.ascontiguousarray(np.asarray(L0_ffn_b, dtype=np.float32))
    A = np.ascontiguousarray(np.asarray(L0_ffn_A, dtype=np.float32))
    Bm = np.ascontiguousarray(np.asarray(L0_ffn_Bmat, dtype=np.float32))
    Lm = np.ascontiguousarray(np.asarray(L0_ffn_LiMEs, dtype=np.float32))

    wqg = np.ascontiguousarray(
        wq.reshape(NDC, 128, HKV, G, DH).transpose(2, 3, 1, 0, 4))
    wkf = np.ascontiguousarray(
        wk.reshape(NDC, 128, HKV, DH).transpose(2, 1, 0, 3))
    wvf = np.ascontiguousarray(
        wv.reshape(NDC, 128, HKV, DH).transpose(2, 1, 0, 3))
    wo2 = np.ascontiguousarray(
        wo.reshape(HKV, 128, NOB, 512).transpose(2, 1, 0, 3))
    wt = np.ascontiguousarray(
        np.ascontiguousarray(W.T).reshape(NDC, 128, NOB, 512)
        .transpose(2, 1, 0, 3))
    w8 = np.ascontiguousarray(
        np.ascontiguousarray(W[0:E, :].T).reshape(NDC, 128, E)
        .transpose(1, 0, 2))
    A2 = np.ascontiguousarray(A.reshape(NDC, 128, RK).transpose(1, 0, 2))
    bvec2 = np.ascontiguousarray(bvec.reshape(1, D))
    identbf = np.eye(128, dtype=ml_dtypes.bfloat16)
    limT = np.ascontiguousarray(Lm.T)

    in_maps = []
    for c in range(8):
        b = c // 4
        r0 = (c % 4) * TOK
        xs = x[b, r0:r0 + TOK, :]
        xfm = np.ascontiguousarray(
            np.ascontiguousarray(xs.T).reshape(NDC, 128, TOK)
            .transpose(1, 0, 2))
        in_maps.append({
            "xfm": xfm,
            "xtm": np.ascontiguousarray(xs),
            "wqg": wqg, "wkf": wkf, "wvf": wvf, "wo": wo2,
            "wt": wt, "w8": w8, "bvec": bvec2, "amat": A2, "bmat": Bm,
            "limt": limT, "lime": Lm, "identbf": identbf,
        })

    if "nc" not in _CACHE:
        _CACHE["nc"] = build_program()
    nc = _CACHE["nc"]
    res = run_bass_kernel_spmd(nc, in_maps, core_ids=list(range(8)))

    out = np.empty((B, S, D), dtype=np.float32)
    for c in range(8):
        b = c // 4
        r0 = (c % 4) * TOK
        out[b, r0:r0 + TOK, :] = res.results[c]["y"]
    return out
